# version 16
# baseline (speedup 1.0000x reference)
import numpy as np
from contextlib import ExitStack

import concourse.bass as bass
import concourse.mybir as mybir
from concourse.bass_utils import run_bass_kernel_spmd

F32 = mybir.dt.float32
BF16 = mybir.dt.bfloat16
AF = mybir.ActivationFunctionType
ALU = mybir.AluOpType
AX = mybir.AxisListType

D = 1024
H = 16
DH = 64
FF = 4096
E = 8
DC = D // 128
FC = FF // 128
FG = 4
SCALE = DH ** (-0.25)
LN_EPS = 1e-5
SPARSE = True

ENGS = ("pe", "act", "dve", "pool", "sync")
ENGMAP = {"pe": "tensor", "act": "scalar", "dve": "vector",
          "pool": "gpsimd", "sync": "sync"}


class Prog:
    def __init__(self, nc):
        self.nc = nc
        self.Q = {k: [] for k in ENGS}
        self.sems = {}
        self.cnt = {}
        self.lastwait = {k: {} for k in ENGS}

    def add_sem(self, name, handle):
        self.sems[name] = handle
        self.cnt[name] = 0

    def bump(self, s, n=1):
        self.cnt[s] += n
        return self.cnt[s]

    def op(self, eng, fn, waits=(), inc=None, incn=1):
        tok = None
        wl = []
        for s, v in waits:
            if not v:
                continue
            if v > self.lastwait[eng].get(s, 0):
                self.lastwait[eng][s] = v
                wl.append((self.sems[s], v))
        if inc is not None:
            tok = self.bump(inc, incn)
            ih = (self.sems[inc], incn)
        else:
            ih = None

        def f(fn=fn, wl=wl, ih=ih, eng=eng):
            e = getattr(self.nc, ENGMAP[eng])
            for sh, v in wl:
                e.wait_ge(sh, v)
            if fn is not None:
                ins = fn(e)
                if ih is not None:
                    ins.then_inc(ih[0], ih[1])

        self.Q[eng].append(f)
        return tok

    def wait(self, eng, waits):
        self.op(eng, None, waits=waits)

    def barrier(self):
        snap = [("PE", self.cnt["PE"]), ("AC", self.cnt["AC"]),
                ("VE", self.cnt["VE"])]
        for eng, own in (("pe", "PE"), ("act", "AC"), ("dve", "VE")):
            self.wait(eng, [w for w in snap if w[0] != own])
        self.wait("sync", snap)

    def run(self, block):
        @block.tensor
        def _(e):
            for f in self.Q["pe"]:
                f()

        @block.scalar
        def _(e):
            for f in self.Q["act"]:
                f()

        @block.vector
        def _(e):
            for f in self.Q["dve"]:
                f()

        @block.gpsimd
        def _(e):
            for f in self.Q["pool"]:
                f()

        @block.sync
        def _(e):
            for f in self.Q["sync"]:
                f()


def build(S=4096):
    NT = S // 512
    NC = S // 128
    JGRP = NC // 2
    nc = bass.Bass("TRN2", target_bir_lowering=False)

    x_h = nc.dram_tensor("x", [S, D], F32, kind="ExternalInput").ap()
    wq_h = nc.dram_tensor("wq", [D, D], F32, kind="ExternalInput").ap()
    wk_h = nc.dram_tensor("wk", [D, D], F32, kind="ExternalInput").ap()
    wv_h = nc.dram_tensor("wv", [D, D], F32, kind="ExternalInput").ap()
    wo_h = nc.dram_tensor("wo", [D, D], F32, kind="ExternalInput").ap()
    bo_h = nc.dram_tensor("bo", [D], F32, kind="ExternalInput").ap()
    w1_h = nc.dram_tensor("w1", [E, D, FF], F32, kind="ExternalInput").ap()
    b1_h = nc.dram_tensor("b1", [E, FF], F32, kind="ExternalInput").ap()
    w2_h = nc.dram_tensor("w2", [E, FF, D], F32, kind="ExternalInput").ap()
    b2_h = nc.dram_tensor("b2", [E, D], F32, kind="ExternalInput").ap()
    wr_h = nc.dram_tensor("wr", [D, E], F32, kind="ExternalInput").ap()
    br_h = nc.dram_tensor("br", [E], F32, kind="ExternalInput").ap()
    gam_h = nc.dram_tensor("gamma", [D], F32, kind="ExternalInput").ap()
    bet_h = nc.dram_tensor("beta", [D], F32, kind="ExternalInput").ap()
    y_h = nc.dram_tensor("y", [S, D], F32, kind="ExternalOutput").ap()
    qt_h = nc.dram_tensor("qt_s", [D, S], BF16).ap()
    kt_h = nc.dram_tensor("kt_s", [D, S], BF16).ap()
    x1f_h = nc.dram_tensor("x1f_s", [S, D], F32).ap()
    x1t_h = nc.dram_tensor("x1t_s", [DC, 128, S], BF16).ap()
    w1s_h = nc.dram_tensor("w1s_s", [E, FG, DC, 128, 1024], BF16).ap()
    w2s_h = nc.dram_tensor("w2s_s", [E, FC, 128, D], BF16).ap()
    w1sel_h = nc.dram_tensor("w1sel_s", [2, FG, DC, 128, 1024], BF16).ap()
    w2sel_h = nc.dram_tensor("w2sel_s", [2, FC, 128, D], BF16).ap()

    P = Prog(nc)
    with ExitStack() as CTX:
        _semnames = ["PE", "AC", "VE", "PO", "Lsm", "Lb", "Lx0", "Lx1",
                     "LW", "Lk", "Lq0", "Lq1", "Sqk0", "Sqk1", "Sx1f",
                     "Sx1t", "Lw1a", "Lw1b", "Lw2a", "Lw2b", "Lxt0", "Lxt1",
                     "Lxf0", "Lxf1", "GCl", "GCs0", "GCs1", "ST0", "ST1", "GS", "GS2", "GB", "Lk1", "Lw1c", "Lw1d"]
        for sname in _semnames:
            P.add_sem(sname, CTX.enter_context(nc.semaphore(sname)))

        def sb(stack, name, shape, dtype=F32):
            return stack.enter_context(nc.sbuf_tensor(name, shape, dtype))

        def ps(stack, name, shape, dtype=F32):
            return stack.enter_context(nc.psum_tensor(name, shape, dtype))

        def load(dst, src, sem="Lsm", waits=()):
            return P.op("sync", lambda e, d=dst, s=src: e.dma_start(out=d, in_=s),
                        waits=waits, inc=sem, incn=16)

        store = load

        ones64 = sb(CTX, "ones64", [1, 64], F32)
        ones128 = sb(CTX, "ones128", [1, 128], F32)
        ident_b = sb(CTX, "ident_b", [128, 128], BF16)
        ident_f = sb(CTX, "ident_f", [128, 128], F32)
        sel = sb(CTX, "sel", [16, 2048], BF16)
        bo_row = sb(CTX, "bo_row", [1, D], F32)
        gam_row = sb(CTX, "gam_row", [1, D], F32)
        bet_row = sb(CTX, "bet_row", [1, D], F32)
        br_row = sb(CTX, "br_row", [1, E], F32)
        wr_sb = sb(CTX, "wr_sb", [128, DC * E], F32)
        b1T = None if SPARSE else sb(CTX, "b1T", [128, E * FC], F32)
        b2cb = sb(CTX, "b2cb", [128, D], F32)
        cwb = sb(CTX, "cwb", [128, E], F32)
        cwT = sb(CTX, "cwT", [8, 1], F32)
        b2c_row = sb(CTX, "b2c_row", [1, D], F32)
        bias_all = sb(CTX, "bias_all", [128, H * NT], F32)
        accq = sb(CTX, "accq", [128, NT * 16], F32)
        acck = sb(CTX, "acck", [128, 16], F32)
        qtbl = sb(CTX, "qtbl", [16, NT], F32)
        kmax2 = sb(CTX, "kmax2", [16, 1], F32)
        mtbl = sb(CTX, "mtbl", [16, NT], BF16)
        xsacc = sb(CTX, "xsacc", [128, E], F32)
        epsv = sb(CTX, "epsv", [128, 1], F32)
        io8 = sb(CTX, "io8", [1, 8], F32)
        r_tmp8 = sb(CTX, "r_tmp8", [1, 8], F32)
        r_ef = sb(CTX, "r_ef", [1, 2], F32)
        idx_sb = sb(CTX, "idx_sb", [1, 2], mybir.dt.int32)
        r_cws = sb(CTX, "r_cws", [1, 2], F32)
        cwsel = sb(CTX, "cwsel", [128, 2], F32)
        b1Tsel = sb(CTX, "b1Tsel", [128, FC * 2], F32)
        r_z = sb(CTX, "r_z", [1, E], F32)
        r_rw = sb(CTX, "r_rw", [1, E], F32)
        r_rw2 = sb(CTX, "r_rw2", [1, E], F32)
        r_m1 = sb(CTX, "r_m1", [1, 1], F32)
        r_m2 = sb(CTX, "r_m2", [1, 1], F32)
        r_m2n = sb(CTX, "r_m2n", [1, 1], F32)
        r_w1 = sb(CTX, "r_w1", [1, 1], F32)
        r_s = sb(CTX, "r_s", [1, 1], F32)
        r_mask1 = sb(CTX, "r_mask1", [1, E], F32)
        r_mask2 = sb(CTX, "r_mask2", [1, E], F32)
        r_d12 = sb(CTX, "r_d12", [1, E], F32)
        cw_row = sb(CTX, "cw_row", [1, E], F32)

        tpo = 0
        for buf, val in ((ones64, 1.0), (ones128, 1.0), (ident_b, 0.0),
                         (ident_f, 0.0), (sel, 1.0), (accq, 0.0), (acck, 0.0),
                         (xsacc, 0.0), (epsv, LN_EPS)):
            tpo = P.op("pool", lambda e, b=buf, v=val: e.memset(b[:], v),
                       inc="PO")
        tmemset = tpo
        tpo = P.op("pool", lambda e: e.affine_select(
            out=ident_b[:], in_=ident_b[:], compare_op=ALU.not_equal,
            fill=1.0, base=0, pattern=[[-1, 128]], channel_multiplier=1),
            waits=[("PO", tmemset)], inc="PO")
        tpo = P.op("pool", lambda e: e.affine_select(
            out=ident_f[:], in_=ident_f[:], compare_op=ALU.not_equal,
            fill=1.0, base=0, pattern=[[-1, 128]], channel_multiplier=1),
            waits=[("PO", tmemset)], inc="PO")
        tpo = P.op("pool", lambda e: e.affine_select(
            out=sel[:], in_=sel[:], compare_op=ALU.is_ge,
            fill=0.0, base=0, pattern=[[1, 2048]], channel_multiplier=-128),
            waits=[("PO", tmemset)], inc="PO")
        tpo = P.op("pool", lambda e: e.affine_select(
            out=sel[:], in_=sel[:], compare_op=ALU.is_ge,
            fill=0.0, base=127, pattern=[[-1, 2048]], channel_multiplier=128),
            waits=[("PO", tpo)], inc="PO")
        tpo = P.op("pool", lambda e: e.iota(
            io8[:], pattern=[[1, 8]], channel_multiplier=0,
            allow_small_or_imprecise_dtypes=True), inc="PO")
        t_const = tpo

        load(bo_row[:], bo_h.rearrange("(o d) -> o d", o=1))
        load(gam_row[:], gam_h.rearrange("(o d) -> o d", o=1))
        load(bet_row[:], bet_h.rearrange("(o d) -> o d", o=1))
        load(br_row[:], br_h.rearrange("(o e) -> o e", o=1))
        t_small = load(wr_sb[:].rearrange("p (c e) -> p c e", e=E),
                       wr_h.rearrange("(c p) e -> p c e", p=128))

        cst_f = [sb(CTX, f"cstf{i}", [128, 512], F32) for i in range(2)]
        cst_b = [sb(CTX, f"cstb{i}", [128, 512], BF16) for i in range(2)]
        cast_jobs = []
        for e_ in range(E):
            for dc in range(DC):
                for fg in range(FG):
                    for hh in range(2):
                        cast_jobs.append((
                            w1_h[e_, dc * 128:(dc + 1) * 128,
                                 fg * 1024 + hh * 512:
                                 fg * 1024 + (hh + 1) * 512],
                            w1s_h[e_, fg, dc][:, hh * 512:(hh + 1) * 512]))
            for fc in range(FC):
                for hh in range(2):
                    cast_jobs.append((
                        w2_h[e_, fc * 128:(fc + 1) * 128,
                             hh * 512:(hh + 1) * 512],
                        w2s_h[e_, fc][:, hh * 512:(hh + 1) * 512]))
        cast_done = {}
        cfree = [0, 0]
        csfree = [0, 0]
        per_e = len(cast_jobs) // E
        for j, (csrc, cdst) in enumerate(cast_jobs):
            i = j % 2
            tl = P.op("pool", lambda e, s=csrc, i=i: e.dma_start(
                out=cst_f[i][:], in_=s), waits=[("PO", cfree[i])],
                inc="GCl", incn=16)
            tc = P.op("pool", lambda e, i=i: e.tensor_copy(
                cst_b[i][:], cst_f[i][:]),
                waits=[("GCl", tl), (f"GCs{i}", csfree[i])], inc="PO")
            cfree[i] = tc
            tst = P.op("pool", lambda e, d=cdst, i=i: e.dma_start(
                out=d, in_=cst_b[i][:]), waits=[("PO", tc)],
                inc=f"GCs{i}", incn=16)
            csfree[i] = tst
            if (j + 1) % per_e == 0:
                cast_done[(j + 1) // per_e - 1] = (
                    P.cnt["GCs0"], P.cnt["GCs1"])

        with ExitStack() as C_AT:
            with ExitStack() as C_V:
                v_sb = sb(C_V, "v_sb", [128, NC * H * 65], BF16)

                with ExitStack() as C1:
                    wq_bf = sb(C1, "wq_bf", [128, DC * 1024], BF16)
                    wk_bf = sb(C1, "wk_bf", [128, DC * 1024], BF16)
                    wv_bf = sb(C1, "wv_bf", [128, DC * 1024], BF16)
                    wstg = sb(C1, "wstg", [128, 2048], F32)
                    xstg = sb(C1, "xstg", [128, 4096], F32)
                    xb = sb(C1, "xb", [128, 4096], BF16)
                    xT = sb(C1, "xT", [128, DC * 512], BF16)
                    qkst = [sb(C1, f"qkst{i}", [128, 512], BF16)
                            for i in range(2)]
                    sq_sb = [sb(C1, f"sq{i}", [128, 512], F32)
                             for i in range(2)]
                    nrm8 = [sb(C1, f"nrm8{i}", [128, 8], F32)
                            for i in range(2)]

                    with ExitStack() as PS1:
                        psT = [ps(PS1, f"psT{i}", [128, 1024], BF16)
                               for i in range(2)]
                        ps_q = ps(PS1, "ps_q", [128, 512], F32)
                        ps_k = ps(PS1, "ps_k", [128, 512], F32)
                        ps_v = ps(PS1, "ps_v", [128, 1024], F32)
                        ps_n = ps(PS1, "ps_n", [128, 1024], F32)

                        t_psq_free = 0
                        t_psn_free = [0, 0]
                        t_wst_free = 0
                        wtoks = []
                        for wh, wbf in ((wq_h, wq_bf), (wk_h, wk_bf),
                                        (wv_h, wv_bf)):
                            tdone = 0
                            for half in range(4):
                                tl = load(wstg[:].rearrange(
                                              "p (c d) -> p c d", d=1024),
                                          wh.rearrange("(c p) d -> p c d",
                                                       p=128)
                                          [:, half * 2:(half + 1) * 2, :],
                                          sem="LW",
                                          waits=[("VE", t_wst_free)])
                                tdone = P.op("dve", lambda e, w=wbf,
                                             half=half: e.tensor_copy(
                                        w[:, half * 2048:(half + 1) * 2048],
                                        wstg[:]),
                                    waits=[("LW", tl)], inc="VE")
                                t_wst_free = tdone
                            wtoks.append(tdone)
                        t_wq, t_wk, t_wv = wtoks

                        t_vones = P.op("dve", lambda e: e.memset(
                            v_sb[:].rearrange("p (c u) -> p c u", u=65)
                            [:, :, 64:65], 1.0), inc="VE")

                        t_xstg_free = 0
                        t_xb_free = 0
                        t_psT_free = [0, 0]
                        t_psk_free = 0
                        t_psv_free = [0, 0]
                        t_qkst_free = [0, 0]
                        t_sq_free = [0, 0]
                        t_nrm_free = [0, 0]
                        qk_slot = 0
                        for nt in range(NT):
                            tl = load(xstg[:].rearrange(
                                          "p (s d) -> p s d", d=D),
                                      x_h.rearrange("(t s p) d -> t p s d",
                                                    p=128, s=4)[nt],
                                      sem="Lx0",
                                      waits=[("VE", t_xstg_free)])
                            t_cast = P.op("dve", lambda e: e.tensor_copy(
                                xb[:], xstg[:]),
                                waits=[("Lx0", tl), ("PE", t_xb_free)],
                                inc="VE")
                            t_xstg_free = t_cast
                            t_xT_evac = [0, 0, 0, 0]
                            tm = 0
                            for ns in range(4):
                                for dc in range(DC):
                                    tm = P.op("pe", lambda e, ns=ns, dc=dc:
                                        e.transpose(
                                            psT[ns % 2][:, dc * 128:
                                                        (dc + 1) * 128],
                                            xb[:, ns * 1024 + dc * 128:
                                               ns * 1024 + (dc + 1) * 128],
                                            ident_b[:]),
                                        waits=[("VE", t_cast),
                                               ("PO", t_const),
                                               ("AC", t_psT_free[ns % 2])],
                                        inc="PE")
                                t_xT_evac[ns] = P.op("act", lambda e, ns=ns:
                                    e.activation(
                                        xT[:].rearrange("p (dc n) -> p dc n",
                                                        n=512)
                                        [:, :, ns * 128:(ns + 1) * 128],
                                        psT[ns % 2][:], AF.Copy),
                                    waits=[("PE", tm)], inc="AC")
                                t_psT_free[ns % 2] = t_xT_evac[ns]
                            t_xb_free = tm
                            t_xT = t_xT_evac[3]

                            for dpc in range(DC):
                                for wbf, twc, psq, which in (
                                        (wq_bf, t_wq, ps_q, "q"),
                                        (wk_bf, t_wk, ps_k, "k")):
                                    pfree = (t_psq_free if which == "q"
                                             else t_psk_free)
                                    tm = 0
                                    for dc in range(DC):
                                        tm = P.op("pe", lambda e, w=wbf,
                                                  dc=dc, dpc=dpc, p_=psq:
                                            e.matmul(
                                                p_[:],
                                                w[:, dc * 1024 + dpc * 128:
                                                  dc * 1024 + (dpc + 1) * 128],
                                                xT[:, dc * 512:(dc + 1) * 512],
                                                start=(dc == 0),
                                                stop=(dc == DC - 1)),
                                            waits=[("AC", t_xT), ("VE", twc),
                                                   ("AC", pfree)], inc="PE")
                                    ta = P.op("act", lambda e, slot=qk_slot,
                                              p_=psq: e.activation(
                                            qkst[slot][:], p_[:], AF.Copy,
                                            scale=SCALE),
                                        waits=[("PE", tm),
                                               (f"Sqk{qk_slot}",
                                                t_qkst_free[qk_slot])],
                                        inc="AC")
                                    if which == "q":
                                        t_psq_free = ta
                                    else:
                                        t_psk_free = ta
                                    dsth = qt_h if which == "q" else kt_h
                                    tst = store(
                                        dsth[dpc * 128:(dpc + 1) * 128,
                                             nt * 512:(nt + 1) * 512],
                                        qkst[qk_slot][:],
                                        sem=f"Sqk{qk_slot}",
                                        waits=[("AC", ta)])
                                    t_qkst_free[qk_slot] = tst
                                    qk_slot = (qk_slot + 1) % 2

                            for ns in range(4):
                                for ds in range(2):
                                    half = ds
                                    tm = 0
                                    for dc in range(DC):
                                        tm = P.op("pe", lambda e, ns=ns,
                                                  ds=ds, dc=dc, half=half:
                                            e.matmul(
                                                ps_v[:, half * 512:
                                                     (half + 1) * 512],
                                                xT[:, dc * 512 + ns * 128:
                                                   dc * 512 + (ns + 1) * 128],
                                                wv_bf[:, dc * 1024 + ds * 512:
                                                      dc * 1024 +
                                                      (ds + 1) * 512],
                                                start=(dc == 0),
                                                stop=(dc == DC - 1)),
                                            waits=[("AC", t_xT), ("VE", t_wv),
                                                   ("AC", t_psv_free[half]),
                                                   ("VE", t_vones)],
                                            inc="PE")
                                    base = (nt * 4 + ns) * (H * 65) + ds * 520
                                    t_psv_free[half] = P.op("act",
                                        lambda e, half=half, base=base:
                                        e.activation(
                                            v_sb[:, base:base + 520]
                                            .rearrange("p (h u) -> p h u",
                                                       u=65)[:, :, 0:64],
                                            ps_v[:, half * 512:
                                                 (half + 1) * 512],
                                            AF.Copy),
                                        waits=[("PE", tm)], inc="AC")

                            for wbf, twc, which in ((wq_bf, t_wq, "q"),
                                                    (wk_bf, t_wk, "k")):
                                for ns in range(4):
                                    for ds in range(2):
                                        half = (ns * 2 + ds) % 2
                                        tm = 0
                                        for dc in range(DC):
                                            tm = P.op("pe", lambda e, w=wbf,
                                                      ns=ns, ds=ds, dc=dc,
                                                      half=half: e.matmul(
                                                    ps_n[:, half * 512:
                                                         (half + 1) * 512],
                                                    xT[:, dc * 512 + ns * 128:
                                                       dc * 512 +
                                                       (ns + 1) * 128],
                                                    w[:, dc * 1024 + ds * 512:
                                                      dc * 1024 +
                                                      (ds + 1) * 512],
                                                    start=(dc == 0),
                                                    stop=(dc == DC - 1)),
                                                waits=[("AC", t_xT),
                                                       ("VE", twc),
                                                       ("AC",
                                                        t_psn_free[half])],
                                                inc="PE")
                                        tsq = P.op("act", lambda e, half=half:
                                            e.activation(
                                                sq_sb[half][:],
                                                ps_n[:, half * 512:
                                                     (half + 1) * 512],
                                                AF.Square, scale=SCALE),
                                            waits=[("PE", tm),
                                                   ("VE", t_sq_free[half])],
                                            inc="AC")
                                        t_psn_free[half] = tsq
                                        tr = 0
                                        for hh in range(8):
                                            tr = P.op("dve", lambda e,
                                                      half=half, hh=hh:
                                                e.reduce_sum(
                                                    nrm8[half][:, hh:hh + 1],
                                                    sq_sb[half][:, hh * 64:
                                                                (hh + 1) * 64],
                                                    axis=AX.X),
                                                waits=[("AC", tsq),
                                                       ("VE",
                                                        t_nrm_free[half])],
                                                inc="VE")
                                        if which == "q":
                                            a_ = accq[:, nt * 16 + ds * 8:
                                                      nt * 16 + ds * 8 + 8]
                                        else:
                                            a_ = acck[:, ds * 8:ds * 8 + 8]
                                        tmx = P.op("dve", lambda e, a=a_,
                                                   half=half: e.tensor_max(
                                                a, a, nrm8[half][:]),
                                            waits=[("VE", tr),
                                                   ("PO", t_const)],
                                            inc="VE")
                                        t_nrm_free[half] = tmx
                                        t_sq_free[half] = tr

                        t_acc = P.cnt["VE"]
                        tred = 0
                        for qt in range(NT):
                            tm = P.op("pe", lambda e, qt=qt: e.transpose(
                                ps_n[0:16, 0:128],
                                accq[:, qt * 16:(qt + 1) * 16], ident_f[:]),
                                waits=[("VE", t_acc), ("VE", tred),
                                       ("AC", max(t_psn_free))], inc="PE")
                            tred = P.op("dve", lambda e, qt=qt: e.reduce_max(
                                qtbl[0:16, qt:qt + 1], ps_n[0:16, 0:128],
                                axis=AX.X),
                                waits=[("PE", tm)], inc="VE")
                        tm = P.op("pe", lambda e: e.transpose(
                            ps_n[0:16, 0:128], acck[:, 0:16], ident_f[:]),
                            waits=[("VE", t_acc), ("VE", tred)], inc="PE")
                        tkm = P.op("dve", lambda e: e.reduce_max(
                            kmax2[0:16, 0:1], ps_n[0:16, 0:128], axis=AX.X),
                            waits=[("PE", tm)], inc="VE")
                        tq = P.op("dve", lambda e: e.tensor_scalar_mul(
                            qtbl[0:16, :], qtbl[0:16, :], kmax2[0:16, 0:1]),
                            waits=[("VE", tkm)], inc="VE")
                        tsq2 = P.op("act", lambda e: e.activation(
                            mtbl[0:16, :], qtbl[0:16, :], AF.Sqrt),
                            waits=[("VE", tq)], inc="AC")
                        tneg = P.op("dve", lambda e: e.tensor_scalar_mul(
                            mtbl[0:16, :], mtbl[0:16, :], -1.0),
                            waits=[("AC", tsq2)], inc="VE")
                        tm = 0
                        for h in range(H):
                            tm = P.op("pe", lambda e, h=h: e.matmul(
                                ps_n[:, h * NT:(h + 1) * NT],
                                sel[:, h * 128:(h + 1) * 128],
                                mtbl[0:16, :], start=True, stop=True),
                                waits=[("VE", tneg), ("PO", t_const),
                                       ("VE", tkm)], inc="PE")
                        t_bias = P.op("act", lambda e: e.activation(
                            bias_all[:], ps_n[:, 0:H * NT], AF.Copy),
                            waits=[("PE", tm)], inc="AC")

                attnT = C_AT.enter_context(nc.sbuf_tensor("attnT", [128, DC * S], BF16, side="right"))

                P.barrier()
                with ExitStack() as C2:
                    ktsl = [sb(C2, f"ktsl{i}", [128, S], BF16)
                            for i in range(2)]
                    qtsl = [sb(C2, f"qtsl{i}", [128, 512], BF16)
                            for i in range(2)]
                    ptA = sb(C2, "ptA", [128, 3 * 1024], BF16)
                    rp_sb = sb(C2, "rp_sb", [1, 512], F32)
                    nb_sb = sb(C2, "nb_sb", [64, 512], F32)
                    avst = [sb(C2, f"avst{i}", [128, 512], F32)
                            for i in range(2)]
                    rp2 = [sb(C2, f"rp2_{i}", [1, 512], F32)
                           for i in range(2)]
                    with ExitStack() as PS2:
                        SAB = [ps(PS2, f"SAB{i}", [128, 1024], F32)
                               for i in range(3)]
                        AVH = ps(PS2, "AVH", [128, 512], F32)
                        BC = ps(PS2, "BC", [128, 512], F32)

                        t_ktsl_free = 0
                        t_qt_load = [0, 0]
                        t_qtsl_free = [0, 0]
                        t_S_free = [0, 0, 0]
                        t_pt_free = [0, 0, 0]
                        t_AV_read = (0, 0)
                        t_avst_free = [0, 0]
                        t_nb_free = 0
                        t_rp_free = [0, 0]
                        t_BC_free = 0
                        norm_pend = []
                        uu = 0

                        def emit_norm(item):
                            nonlocal t_nb_free, t_BC_free
                            slot, h, nqt, tevp, trp = item
                            tbc = P.op("pe", lambda e, slot=slot:
                                e.matmul(
                                    BC[0:64, :], ones64[:],
                                    rp2[slot][:], start=True, stop=True),
                                waits=[("VE", trp), ("VE", t_BC_free)],
                                inc="PE")
                            t_rp_free[slot] = tbc
                            tnb = P.op("dve", lambda e:
                                e.tensor_copy(nb_sb[:], BC[0:64, :]),
                                waits=[("PE", tbc), ("VE", t_nb_free)],
                                inc="VE")
                            t_BC_free = tnb
                            pb = (h % 2) * 64
                            dst = attnT[pb:pb + 64,
                                        (h // 2) * S + nqt * 512:
                                        (h // 2) * S + (nqt + 1) * 512]
                            tev = P.op("dve", lambda e, slot=slot,
                                       dst=dst: e.tensor_mul(
                                    dst, avst[slot][0:64, :], nb_sb[:]),
                                waits=[("VE", tnb), ("AC", tevp)], inc="VE")
                            t_avst_free[slot] = tev
                            t_nb_free = tev

                        t_qk_f0 = P.cnt["Sqk0"]
                        t_qk_f1 = P.cnt["Sqk1"]
                        t_ktsl_free2 = [0, 0]
                        for h in range(H):
                            ks = h % 2
                            pr = h // 2
                            hi = h % 2
                            t_kt_load = 0
                            for dup in range(2):
                                t_kt_load = load(
                                    ktsl[ks][dup * 64:dup * 64 + 64, :],
                                    kt_h[h * 64:(h + 1) * 64, :],
                                    sem=f"Lk{ks}" if ks else "Lk",
                                    waits=[("Sqk0", t_qk_f0),
                                           ("Sqk1", t_qk_f1),
                                           ("PE", t_ktsl_free2[ks])])
                            for qt in range(NT):
                                qs = qt % 2
                                t_qt_load[qs] = 0
                                for dup in range(2):
                                    t_qt_load[qs] = load(
                                        qtsl[qs][dup * 64:dup * 64 + 64, :],
                                        qt_h[h * 64:(h + 1) * 64,
                                             qt * 512:(qt + 1) * 512],
                                        sem=f"Lq{qs}",
                                        waits=[("Sqk0", t_qk_f0),
                                               ("Sqk1", t_qk_f1),
                                               ("PE", t_qtsl_free[qs])])
                                exp_tok = {}
                                tAV = 0

                                def emit_av(g):
                                    nonlocal tAV
                                    gs = g % 3
                                    eT = exp_tok[g]
                                    for jj in range(2):
                                        first = (g == 0 and jj == 0)
                                        last = (g == JGRP - 1 and jj == 1)
                                        jc = g * 2 + jj
                                        vh = v_sb[:,
                                            jc * (H * 65) + h * 65:
                                            jc * (H * 65) + h * 65 + 65]
                                        tAV = P.op("pe", lambda e, vh=vh,
                                            gs=gs, jj=jj, first=first,
                                            last=last: e.matmul(
                                                AVH[0:65, :], vh,
                                                ptA[:, gs * 1024 +
                                                    jj * 512:
                                                    gs * 1024 +
                                                    (jj + 1) * 512],
                                                start=first, stop=last),
                                            waits=[("AC", eT),
                                                   ("VE", t_vones),
                                                   ("AC", t_AV_read[0]),
                                                   ("VE", t_AV_read[1])],
                                            inc="PE")
                                        t_pt_free[gs] = tAV

                                for g in range(JGRP):
                                    gs = g % 3
                                    ps_ = g % 3
                                    tS = 0
                                    for jj in range(2):
                                        jc = g * 2 + jj
                                        pbase = (jc % 2) * 64
                                        tS = P.op("pe", lambda e, jc=jc,
                                            jj=jj, qs=qs, gs=gs, ks=ks,
                                            pbase=pbase: e.matmul(
                                                SAB[gs][:, jj * 512:
                                                        (jj + 1) * 512],
                                                ktsl[ks][pbase:pbase + 64,
                                                     jc * 128:
                                                     (jc + 1) * 128],
                                                qtsl[qs][pbase:
                                                         pbase + 64, :],
                                                start=True, stop=True,
                                                tile_position=(pbase, 0)),
                                            waits=[(("Lk1" if ks else "Lk"),
                                                    t_kt_load),
                                                   (f"Lq{qs}",
                                                    t_qt_load[qs]),
                                                   ("AC", t_S_free[gs])],
                                            inc="PE")
                                    texp = P.op("act", lambda e, gs=gs,
                                        ps_=ps_, h=h, qt=qt: e.activation(
                                            ptA[:, ps_ * 1024:
                                                (ps_ + 1) * 1024],
                                            SAB[gs][:], AF.Exp,
                                            bias=bias_all[:,
                                                h * NT + qt:
                                                h * NT + qt + 1]),
                                        waits=[("PE", tS), ("AC", t_bias),
                                               ("PE", t_pt_free[ps_])],
                                        inc="AC")
                                    t_S_free[gs] = texp
                                    exp_tok[g] = texp
                                    if g > 0:
                                        emit_av(g - 1)
                                emit_av(JGRP - 1)
                                slot = uu % 2
                                tevp = P.op("act", lambda e, slot=slot:
                                    e.activation(
                                        avst[slot][0:64, :],
                                        AVH[0:64, :], AF.Copy),
                                    waits=[("PE", tAV),
                                           ("VE", t_avst_free[slot])],
                                    inc="AC")
                                trp = P.op("dve", lambda e, slot=slot:
                                    e.reciprocal(rp2[slot][:],
                                                 AVH[64:65, :]),
                                    waits=[("PE", tAV),
                                           ("PE", t_rp_free[slot])],
                                    inc="VE")
                                t_AV_read = (tevp, trp)
                                norm_pend.append((slot, h, qt, tevp, trp))
                                if len(norm_pend) > 1:
                                    emit_norm(norm_pend.pop(0))
                                t_qtsl_free[qs] = tAV
                                uu += 1
                            t_ktsl_free2[ks] = tAV
                        for item in norm_pend:
                            emit_norm(item)
                        t_attnT_ve = P.cnt["VE"]

            P.barrier()
            with ExitStack() as C3:
                wo_bf = sb(C3, "wo_bf", [128, DC * 1024], BF16)
                b1_sb = sb(C3, "b1_sb", [E, FF], F32)
                b2_sb = sb(C3, "b2_sb", [E, D], F32)
                bo_b = sb(C3, "bo_b", [128, D], F32)
                b1pair = sb(C3, "b1pair", [2, FF], F32)
                wstg3 = sb(C3, "wstg3", [128, 2048], F32)
                xt3 = [sb(C3, f"xt3_{i}", [128, 1024], F32) for i in range(2)]
                y3_sb = sb(C3, "y3_sb", [128, 1024], F32)
                x1_sb = sb(C3, "x1_sb", [128, 1024], F32)
                x1b_sb = sb(C3, "x1b_sb", [128, 1024], BF16)
                x1ts = sb(C3, "x1ts", [128, DC * 512], BF16)
                st3 = sb(C3, "st3", [128, 16], F32)
                rs3 = sb(C3, "rs3", [128, 2], F32)
                rsum8 = sb(C3, "rsum8", [128, 8], F32)
                with ExitStack() as PS3:
                    PO3 = ps(PS3, "PO3", [128, 1024], F32)
                    psT3 = ps(PS3, "psT3", [128, 1024], BF16)
                    RT = ps(PS3, "RT", [128, 512], F32)

                    t_b12 = load(b1_sb[:], b1_h, sem="Lb")
                    t_b12 = load(b2_sb[:], b2_h, sem="Lb")
                    t_bc_ac = 0
                    for hf in range(2):
                        tmb = P.op("pe", lambda e, hf=hf: e.matmul(
                                RT[:], ones128[:],
                                bo_row[:, hf * 512:(hf + 1) * 512],
                                start=True, stop=True),
                            waits=[("Lsm", t_small), ("PO", t_const),
                                   ("AC", t_bc_ac)], inc="PE")
                        t_bc_ac = P.op("act", lambda e, hf=hf: e.activation(
                                bo_b[:, hf * 512:(hf + 1) * 512], RT[:],
                                AF.Copy),
                            waits=[("PE", tmb)], inc="AC")
                    t_b1T = t_bc_ac
                    for g8 in (range(FC // 8) if not SPARSE else []):
                        tmb = 0
                        for j in range(8):
                            fc = g8 * 8 + j
                            tmb = P.op("pe", lambda e, fc=fc, j=j:
                                e.transpose(
                                    RT[:, j * 8:(j + 1) * 8],
                                    b1_sb[:, fc * 128:(fc + 1) * 128],
                                    ident_f[0:8, 0:8]),
                                waits=[("Lb", t_b12), ("PO", t_const),
                                       ("AC", t_b1T)], inc="PE")
                        t_b1T = P.op("act", lambda e, g8=g8: e.activation(
                                b1T[:, g8 * 64:(g8 + 1) * 64],
                                RT[:, 0:64], AF.Copy),
                            waits=[("PE", tmb)], inc="AC")

                    t_wo = 0
                    t_wst3 = 0
                    for half in range(4):
                        tl = load(wstg3[:].rearrange("p (c d) -> p c d",
                                                     d=1024),
                                  wo_h.rearrange("(c p) d -> p c d", p=128)
                                  [:, half * 2:(half + 1) * 2, :],
                                  sem="LW",
                                  waits=[("VE", t_wst3)])
                        t_wo = P.op("dve", lambda e, half=half: e.tensor_copy(
                                wo_bf[:, half * 2048:(half + 1) * 2048],
                                wstg3[:]),
                            waits=[("LW", tl)], inc="VE")
                        t_wst3 = t_wo

                    t_xt3_free = [0, 0]
                    t_PO3_free = [0, 0]
                    t_psT3_free = 0
                    t_x1ts_free = 0
                    t_y3_free = 0
                    t_x1_free = 0
                    t_x1b_free = 0
                    t_st3_free = 0
                    t_rsum_free = 0
                    t_x1f_store = 0
                    t_x1t_store = 0
                    for nc_ in range(NC):
                        xs = nc_ % 2
                        tlx = load(xt3[xs][:],
                                   x_h[nc_ * 128:(nc_ + 1) * 128, :],
                                   sem=f"Lx{xs}",
                                   waits=[("VE", t_xt3_free[xs])])
                        tds = [0, 0]
                        for ds in range(2):
                            tm = 0
                            for dc in range(DC):
                                tm = P.op("pe", lambda e, dc=dc, ds=ds,
                                          nc_=nc_: e.matmul(
                                        PO3[:, ds * 512:(ds + 1) * 512],
                                        attnT[:, dc * S + nc_ * 128:
                                              dc * S + (nc_ + 1) * 128],
                                        wo_bf[:, dc * 1024 + ds * 512:
                                              dc * 1024 + (ds + 1) * 512],
                                        start=(dc == 0), stop=(dc == DC - 1)),
                                    waits=[("VE", t_attnT_ve), ("VE", t_wo),
                                           ("VE", t_PO3_free[ds])], inc="PE")
                            tds[ds] = P.op("dve", lambda e, ds=ds, xs=xs:
                                e.tensor_add(
                                    y3_sb[:, ds * 512:(ds + 1) * 512],
                                    PO3[:, ds * 512:(ds + 1) * 512],
                                    xt3[xs][:, ds * 512:(ds + 1) * 512]),
                                waits=[("PE", tm), (f"Lx{xs}", tlx),
                                       ("VE", t_y3_free)], inc="VE")
                            t_PO3_free[ds] = tds[ds]
                        t_xt3_free[xs] = tds[1]
                        tyb = P.op("dve", lambda e: e.tensor_add(
                                y3_sb[:], y3_sb[:], bo_b[:]),
                            waits=[("VE", tds[1]), ("AC", t_bc_ac)], inc="VE")
                        tst = P.op("dve", lambda e: e.bn_stats(
                                st3[:, 0:6], y3_sb[:, 0:512]),
                            waits=[("VE", tyb), ("VE", t_st3_free)], inc="VE")
                        tst = P.op("dve", lambda e: e.bn_stats(
                                st3[:, 6:12], y3_sb[:, 512:1024]),
                            waits=[("VE", tst)], inc="VE")
                        tst = P.op("dve", lambda e: e.bn_aggr(
                                st3[:, 12:14], st3[:, 0:12]),
                            waits=[("VE", tst)], inc="VE")
                        tsr = P.op("act", lambda e: e.activation(
                                st3[:, 14:15], st3[:, 13:14], AF.Sqrt,
                                bias=epsv[:]),
                            waits=[("VE", tst)], inc="AC")
                        trc = P.op("dve", lambda e: e.reciprocal(
                                rs3[:, 0:1], st3[:, 14:15]),
                            waits=[("AC", tsr)], inc="VE")
                        t_st3_free = trc
                        tx1 = P.op("dve", lambda e: e.tensor_scalar(
                                x1_sb[:], y3_sb[:], st3[:, 12:13],
                                rs3[:, 0:1], ALU.subtract, ALU.mult),
                            waits=[("VE", trc), ("Sx1f", t_x1_free)],
                            inc="VE")
                        t_y3_free = tx1
                        t_x1f_store = store(
                            x1f_h[nc_ * 128:(nc_ + 1) * 128, :], x1_sb[:],
                            sem="Sx1f", waits=[("VE", tx1)])
                        t_x1_free = t_x1f_store
                        txb = P.op("dve", lambda e: e.tensor_copy(
                                x1b_sb[:], x1_sb[:]),
                            waits=[("VE", tx1), ("PE", t_x1b_free)], inc="VE")
                        tm = 0
                        for dc in range(DC):
                            tm = P.op("pe", lambda e, dc=dc: e.transpose(
                                    psT3[:, dc * 128:(dc + 1) * 128],
                                    x1b_sb[:, dc * 128:(dc + 1) * 128],
                                    ident_b[:]),
                                waits=[("VE", txb), ("AC", t_psT3_free),
                                       ("PO", t_const)], inc="PE")
                        t_x1b_free = tm
                        tev = P.op("act", lambda e, q4=nc_ % 4: e.activation(
                                x1ts[:].rearrange("p (dc n) -> p dc n", n=512)
                                [:, :, q4 * 128:(q4 + 1) * 128],
                                psT3[:], AF.Copy),
                            waits=[("PE", tm), ("Sx1t", t_x1ts_free)],
                            inc="AC")
                        t_psT3_free = tev
                        if nc_ % 4 == 3:
                            tile = nc_ // 4
                            tss = 0
                            for dc in range(DC):
                                tss = store(
                                    x1t_h[dc][:, tile * 512:(tile + 1) * 512],
                                    x1ts[:, dc * 512:(dc + 1) * 512],
                                    sem="Sx1t", waits=[("AC", tev)])
                            t_x1t_store = tss
                            t_x1ts_free = tss
                            trr = 0
                            for dc in range(DC):
                                trr = P.op("dve", lambda e, dc=dc:
                                    e.reduce_sum(
                                        rsum8[:, dc:dc + 1],
                                        x1ts[:, dc * 512:(dc + 1) * 512],
                                        axis=AX.X),
                                    waits=[("AC", tev), ("VE", t_rsum_free)],
                                    inc="VE")
                            t_rsum_free = P.op("dve", lambda e: e.tensor_add(
                                    xsacc[:], xsacc[:], rsum8[:]),
                                waits=[("VE", trr), ("PO", t_const)],
                                inc="VE")

                    t_xs = t_rsum_free
                    tm = 0
                    for dc in range(DC):
                        tm = P.op("pe", lambda e, dc=dc: e.matmul(
                                RT[0:1, 0:8], xsacc[:, dc:dc + 1],
                                wr_sb[:, dc * 8:(dc + 1) * 8],
                                start=(dc == 0), stop=(dc == DC - 1)),
                            waits=[("VE", t_xs), ("Lsm", t_small),
                                   ("AC", t_bias)], inc="PE")
                    tz0 = P.op("act", lambda e: e.activation(
                            r_z[:], RT[0:1, 0:8], AF.Copy, scale=1.0 / S),
                        waits=[("PE", tm)], inc="AC")
                    tz = P.op("dve", lambda e: e.tensor_add(
                            r_z[:], r_z[:], br_row[:]),
                        waits=[("AC", tz0), ("Lsm", t_small)], inc="VE")
                    tmx = P.op("dve", lambda e: e.reduce_max(
                            r_m1[:], r_z[:], axis=AX.X),
                        waits=[("VE", tz)], inc="VE")
                    tmn = P.op("dve", lambda e: e.tensor_scalar_mul(
                            r_m2n[:], r_m1[:], -1.0),
                        waits=[("VE", tmx)], inc="VE")
                    tex = P.op("act", lambda e: e.activation(
                            r_rw[:], r_z[:], AF.Exp, bias=r_m2n[:]),
                        waits=[("VE", tmn)], inc="AC")
                    tsm = P.op("dve", lambda e: e.reduce_sum(
                            r_s[:], r_rw[:], axis=AX.X),
                        waits=[("AC", tex)], inc="VE")
                    tsm = P.op("dve", lambda e: e.reciprocal(
                            r_s[:], r_s[:]),
                        waits=[("VE", tsm)], inc="VE")
                    trw = P.op("dve", lambda e: e.tensor_scalar_mul(
                            r_rw[:], r_rw[:], r_s[:]),
                        waits=[("VE", tsm)], inc="VE")
                    tm1 = P.op("dve", lambda e: e.reduce_max(
                            r_m1[:], r_rw[:], axis=AX.X),
                        waits=[("VE", trw)], inc="VE")
                    tk1 = P.op("dve", lambda e: e.tensor_single_scalar(
                            r_mask1[:], r_rw[:], r_m1[:], ALU.is_equal),
                        waits=[("VE", tm1)], inc="VE")
                    tr2 = P.op("dve", lambda e: e.scalar_tensor_tensor(
                            r_rw2[:], r_mask1[:], -1e30, r_rw[:],
                            ALU.mult, ALU.add),
                        waits=[("VE", tk1)], inc="VE")
                    tm2 = P.op("dve", lambda e: e.reduce_max(
                            r_m2[:], r_rw2[:], axis=AX.X),
                        waits=[("VE", tr2)], inc="VE")
                    tk2 = P.op("dve", lambda e: e.tensor_single_scalar(
                            r_mask2[:], r_rw2[:], r_m2[:], ALU.is_equal),
                        waits=[("VE", tm2)], inc="VE")
                    tmn2 = P.op("dve", lambda e: e.tensor_scalar_mul(
                            r_m2n[:], r_m2[:], -1.0),
                        waits=[("VE", tm2)], inc="VE")
                    tw1 = P.op("act", lambda e: e.activation(
                            r_w1[:], r_m1[:], AF.Sigmoid, bias=r_m2n[:]),
                        waits=[("VE", tmn2), ("VE", tk1)], inc="AC")
                    td12 = P.op("dve", lambda e: e.tensor_sub(
                            r_d12[:], r_mask1[:], r_mask2[:]),
                        waits=[("VE", tk2)], inc="VE")
                    tcw = P.op("dve", lambda e: e.scalar_tensor_tensor(
                            cw_row[:], r_d12[:], r_w1[:], r_mask2[:],
                            ALU.mult, ALU.add),
                        waits=[("VE", td12), ("AC", tw1)], inc="VE")
                    tm = P.op("pe", lambda e: e.matmul(
                            RT[:, 0:8], ones128[:], cw_row[:],
                            start=True, stop=True),
                        waits=[("VE", tcw), ("AC", tz0)], inc="PE")
                    t_cwb = P.op("act", lambda e: e.activation(
                            cwb[:], RT[:, 0:8], AF.Copy),
                        waits=[("PE", tm)], inc="AC")
                    tm = P.op("pe", lambda e: e.matmul(
                            RT[0:8, 8:9], cw_row[:], ones128[0:1, 0:1],
                            start=True, stop=True),
                        waits=[("VE", tcw), ("AC", t_cwb)], inc="PE")
                    tct = P.op("act", lambda e: e.activation(
                            cwT[:], RT[0:8, 8:9], AF.Copy),
                        waits=[("PE", tm)], inc="AC")
                    tprev = tct
                    for hf in range(2):
                        tmb = P.op("pe", lambda e, hf=hf: e.matmul(
                                RT[0:1, 0:512], cwT[:],
                                b2_sb[:, hf * 512:(hf + 1) * 512],
                                start=True, stop=True),
                            waits=[("AC", tprev), ("Lb", t_b12)], inc="PE")
                        tprev = P.op("act", lambda e, hf=hf: e.activation(
                                b2c_row[:, hf * 512:(hf + 1) * 512],
                                RT[0:1, 0:512], AF.Copy),
                            waits=[("PE", tmb)], inc="AC")
                    for hf in range(2):
                        tmb = P.op("pe", lambda e, hf=hf: e.matmul(
                                RT[:, 0:512], ones128[:],
                                b2c_row[:, hf * 512:(hf + 1) * 512],
                                start=True, stop=True),
                            waits=[("AC", tprev)], inc="PE")
                        tprev = P.op("act", lambda e, hf=hf: e.activation(
                                b2cb[:, hf * 512:(hf + 1) * 512],
                                RT[:, 0:512], AF.Copy),
                            waits=[("PE", tmb)], inc="AC")
                    tm = tprev
                    t_b2cb = tm
                    if SPARSE:
                        tt1 = P.op("dve", lambda e: e.tensor_mul(
                                r_tmp8[:], io8[:], r_mask1[:]),
                            waits=[("PO", t_const), ("VE", tk1)], inc="VE")
                        tt1 = P.op("dve", lambda e: e.reduce_sum(
                                r_ef[:, 0:1], r_tmp8[:], axis=AX.X),
                            waits=[("VE", tt1)], inc="VE")
                        tt2 = P.op("dve", lambda e: e.tensor_mul(
                                r_tmp8[:], io8[:], r_mask2[:]),
                            waits=[("VE", tt1), ("VE", tk2)], inc="VE")
                        tt2 = P.op("dve", lambda e: e.reduce_sum(
                                r_ef[:, 1:2], r_tmp8[:], axis=AX.X),
                            waits=[("VE", tt2)], inc="VE")
                        t_idx = P.op("dve", lambda e: e.tensor_copy(
                                idx_sb[:], r_ef[:]),
                            waits=[("VE", tt2)], inc="VE")
                        tc1 = P.op("dve", lambda e: e.tensor_copy(
                                r_cws[:, 0:1], r_w1[:]),
                            waits=[("AC", tw1)], inc="VE")
                        tc1 = P.op("dve", lambda e: e.tensor_scalar(
                                r_cws[:, 1:2], r_w1[:], -1.0, 1.0,
                                ALU.mult, ALU.add),
                            waits=[("VE", tc1)], inc="VE")
                        tm = P.op("pe", lambda e: e.matmul(
                                RT[:, 0:2], ones128[:], r_cws[:],
                                start=True, stop=True),
                            waits=[("VE", tc1), ("AC", t_b2cb)], inc="PE")
                        t_cwsel = P.op("act", lambda e: e.activation(
                                cwsel[:], RT[:, 0:2], AF.Copy),
                            waits=[("PE", tm)], inc="AC")
                        t_b1g = 0
                        with nc.gpsimd.register("rg0") as rg0:
                            for k in range(2):
                                def gfn(e, k=k, rg0=rg0):
                                    e.reg_load(rg0, idx_sb[0:1, k:k + 1])
                                    e.reg_mul(rg0, rg0, FF)
                                    return e.dma_start(
                                        out=b1pair[k:k + 1, :],
                                        in_=bass.AP(b1_h.tensor, rg0,
                                                    [[FF, 1], [1, FF]]))
                                t_b1g = P.op("pool", gfn,
                                    waits=[("VE", t_idx), ("Lb", t_b12)],
                                    inc="GB", incn=16)
                            for k in range(2):
                                for half in range(4):
                                    def wfn(e, k=k, half=half, rg0=rg0):
                                        e.reg_load(rg0, idx_sb[0:1, k:k + 1])
                                        e.reg_mul(rg0, rg0, FG * DC * 128 * 1024)
                                        e.reg_add(rg0, rg0, half * 1048576)
                                        return e.dma_start(
                                            out=bass.AP(
                                                w1sel_h.tensor,
                                                k * FG * DC * 128 * 1024 +
                                                half * 1048576,
                                                [[16384, 64], [1, 16384]]),
                                            in_=bass.AP(w1s_h.tensor, rg0,
                                                [[16384, 64], [1, 16384]]))
                                    t_w1sel = P.op("pool", wfn,
                                         waits=[("VE", t_idx),
                                                ("GCs0", cast_done[E - 1][0]),
                                                ("GCs1", cast_done[E - 1][1])],
                                         inc="GS", incn=16)
                            for k in range(2):
                                for half in range(4):
                                    def wfn2(e, k=k, half=half, rg0=rg0):
                                        e.reg_load(rg0, idx_sb[0:1, k:k + 1])
                                        e.reg_mul(rg0, rg0, FC * 128 * D)
                                        e.reg_add(rg0, rg0, half * 1048576)
                                        return e.dma_start(
                                            out=bass.AP(
                                                w2sel_h.tensor,
                                                k * FC * 128 * D +
                                                half * 1048576,
                                                [[16384, 64], [1, 16384]]),
                                            in_=bass.AP(w2s_h.tensor, rg0,
                                                [[16384, 64], [1, 16384]]))
                                    t_w2sel = P.op("pool", wfn2,
                                         waits=[("VE", t_idx)],
                                         inc="GS2", incn=16)
                        t_b1Ts = t_cwsel
                        for g16 in range(2):
                            tmb = 0
                            for j in range(16):
                                fc = g16 * 16 + j
                                tmb = P.op("pe", lambda e, fc=fc, j=j:
                                    e.transpose(
                                        RT[:, j * 2:(j + 1) * 2],
                                        b1pair[:, fc * 128:(fc + 1) * 128],
                                        ident_f[0:2, 0:2]),
                                    waits=[("GB", t_b1g), ("PO", t_const),
                                           ("AC", t_b1Ts)], inc="PE")
                            t_b1Ts = P.op("act", lambda e, g16=g16:
                                e.activation(
                                    b1Tsel[:, g16 * 32:(g16 + 1) * 32],
                                    RT[:, 0:32], AF.Copy),
                                waits=[("PE", tmb)], inc="AC")

        P.barrier()
        with ExitStack() as C4:
            w1r = sb(C4, "w1r", [128, 18 * 1024], BF16)
            w2r = sb(C4, "w2r", [128, 8 * 1024], BF16)
            x1tt = [sb(C4, f"x1tt{i}", [128, DC * 512], BF16)
                    for i in range(2)]
            h_sb = sb(C4, "h_sb", [128, FC * 512], BF16)
            oacc = sb(C4, "oacc", [128, 4 * 1024], F32)
            x1ft = [sb(C4, f"x1ft{i}", [128, 4 * 1024], F32)
                    for i in range(2)]
            gam_b = sb(C4, "gam_b", [128, D], F32)
            bet_b = sb(C4, "bet_b", [128, D], F32)
            ep_t = sb(C4, "ep_t", [128, 1024], F32)
            ep_o = [sb(C4, f"ep_o{i}", [128, 1024], F32) for i in range(2)]
            st4 = sb(C4, "st4", [128, 16], F32)
            rs4 = sb(C4, "rs4", [128, 2], F32)
            with ExitStack() as PS4:
                PO4 = [ps(PS4, f"PO4_{i}", [128, 1024], F32) for i in range(4)]

                t_gb = 0
                for row, dstb in ((gam_row, gam_b), (bet_row, bet_b)):
                    for hf in range(2):
                        tmb = P.op("pe", lambda e, r=row, hf=hf: e.matmul(
                                PO4[3][:, 0:512], ones128[:],
                                r[:, hf * 512:(hf + 1) * 512],
                                start=True, stop=True),
                            waits=[("Lsm", t_small), ("PO", t_const),
                                   ("AC", t_gb)], inc="PE")
                        t_gb = P.op("act", lambda e, d=dstb, hf=hf:
                            e.activation(
                                d[:, hf * 512:(hf + 1) * 512],
                                PO4[3][:, 0:512], AF.Copy),
                            waits=[("PE", tmb)], inc="AC")

                w1_ready = [0] * 18
                w1_free = [0] * 18
                w2_ready = [0] * 8
                w2_free = [0] * 8
                w1_ctr = 0
                w2_ctr = 0
                t_x1tt = [0, 0]
                t_x1tt_free = [0, 0]
                t_x1ft = [0, 0]
                t_x1ft_free = [0, 0]
                t_h_free = 0
                t_PO4h_free = [0, 0]
                t_po40_evac = [0, 0]
                t_oacc_evac = 0
                t_oacc_read = 0
                t_out_free = [0, 0]
                t_st4_free = 0
                gelu_tok = {}

                for tile in range(NT):
                    ts_ = tile % 2
                    tld = 0
                    for dc in range(DC):
                        tld = load(x1tt[ts_][:, dc * 512:(dc + 1) * 512],
                                   x1t_h[dc][:, tile * 512:(tile + 1) * 512],
                                   sem=f"Lxt{ts_}",
                                   waits=[("Sx1t", t_x1t_store),
                                          ("PE", t_x1tt_free[ts_])])
                    t_x1tt[ts_] = tld
                    t_x1ft[ts_] = load(
                        x1ft[ts_][:].rearrange("p (s d) -> p s d", d=D),
                        x1f_h.rearrange("(t s p) d -> t p s d",
                                        p=128, s=4)[tile],
                        sem=f"Lxf{ts_}",
                        waits=[("Sx1f", t_x1f_store),
                               ("VE", t_x1ft_free[ts_])])
                    tinit = 0
                    for ns in range(4):
                        tinit = P.op("dve", lambda e, ns=ns, ts_=ts_:
                            e.tensor_add(
                                oacc[:, ns * 1024:(ns + 1) * 1024],
                                x1ft[ts_][:, ns * 1024:(ns + 1) * 1024],
                                b2cb[:]),
                            waits=[(f"Lxf{ts_}", t_x1ft[ts_]),
                                   ("AC", t_b2cb),
                                   ("VE", t_oacc_evac),
                                   ("VE", t_oacc_read)], inc="VE")
                    t_oacc_init = tinit

                    for e_ in (range(2) if SPARSE else range(E)):
                        for fg in range(FG):
                            gsem = ["Lw1a", "Lw1b", "Lw1c", "Lw1d"][(e_ * FG + fg) % 4]
                            grp_tok = 0
                            for dc in range(DC):
                                slot = w1_ctr % 18
                                w1src = (w1sel_h[e_, fg, dc] if SPARSE
                                         else w1s_h[e_, fg, dc])
                                wws = ([("GS", t_w1sel)] if SPARSE else
                                       [("GCs0", cast_done[e_][0]),
                                        ("GCs1", cast_done[e_][1])])
                                grp_tok = load(
                                    w1r[:, slot * 1024:(slot + 1) * 1024],
                                    w1src,
                                    sem=gsem,
                                    waits=wws + [("PE", w1_free[slot])])
                                w1_ctr += 1
                            base_slot = (w1_ctr - DC) % 18
                            for fcl in range(8):
                                fc = fg * 8 + fcl
                                hp = fc % 2
                                tm = 0
                                for dc in range(DC):
                                    slot = (base_slot + dc) % 18
                                    tm = P.op("pe", lambda e, slot=slot,
                                              fcl=fcl, dc=dc, hp=hp,
                                              ts_=ts_: e.matmul(
                                            PO4[0][:, hp * 512:
                                                   (hp + 1) * 512],
                                            w1r[:, slot * 1024 + fcl * 128:
                                                slot * 1024 +
                                                (fcl + 1) * 128],
                                            x1tt[ts_][:, dc * 512:
                                                      (dc + 1) * 512],
                                            start=(dc == 0),
                                            stop=(dc == DC - 1)),
                                        waits=[(gsem, grp_tok),
                                               (f"Lxt{ts_}", t_x1tt[ts_]),
                                               ("AC", t_PO4h_free[hp]),
                                               ("VE", t_po40_evac[hp]),
                                               ("PE", t_h_free)], inc="PE")
                                    if fcl == 7:
                                        w1_free[slot] = tm
                                tg = P.op("act", lambda e, fc=fc, hp=hp,
                                          e_=e_: e.activation(
                                        h_sb[:, fc * 512:(fc + 1) * 512],
                                        PO4[0][:, hp * 512:(hp + 1) * 512],
                                        AF.Gelu,
                                        bias=(b1Tsel[:, fc * 2 + e_:
                                                     fc * 2 + e_ + 1]
                                              if SPARSE else
                                              b1T[:, fc * E + e_:
                                                  fc * E + e_ + 1])),
                                    waits=[("PE", tm),
                                           ("AC", t_b1Ts if SPARSE
                                            else t_b1T)],
                                    inc="AC")
                                t_PO4h_free[hp] = tg
                                gelu_tok[fc] = tg
                        for fc in range(FC):
                            if fc % 4 == 0:
                                g2sem = ("Lw2a" if (w2_ctr // 4) % 2 == 0
                                         else "Lw2b")
                                g2tok = 0
                                for k in range(4):
                                    slot = w2_ctr % 8
                                    w2src = (w2sel_h[e_, fc + k] if SPARSE
                                             else w2s_h[e_, fc + k])
                                    wws = ([("GS2", t_w2sel)] if SPARSE else
                                           [("GCs0", cast_done[e_][0]),
                                            ("GCs1", cast_done[e_][1])])
                                    g2tok = load(
                                        w2r[:, slot * 1024:(slot + 1) * 1024],
                                        w2src,
                                        sem=g2sem,
                                        waits=wws + [("PE", w2_free[slot])])
                                    w2_ctr += 1
                            slot = fc % 8
                            tm = 0
                            for ns in range(4):
                                for ds in range(2):
                                    tm = P.op("pe", lambda e, fc=fc, ns=ns,
                                              ds=ds, slot=slot: e.matmul(
                                            PO4[ns][:, ds * 512:
                                                    (ds + 1) * 512],
                                            h_sb[:, fc * 512 + ns * 128:
                                                 fc * 512 + (ns + 1) * 128],
                                            w2r[:, slot * 1024 + ds * 512:
                                                slot * 1024 +
                                                (ds + 1) * 512],
                                            start=(fc == 0),
                                            stop=(fc == FC - 1)),
                                        waits=[(g2sem, g2tok),
                                               ("AC", gelu_tok[fc]),
                                               ("AC", t_gb if fc == 0 else 0),
                                               ("AC", max(t_PO4h_free)
                                                if fc == 0 else 0),
                                               ("VE", t_oacc_evac
                                                if fc == 0 else 0)],
                                        inc="PE")
                            w2_free[slot] = tm
                            tlast_w2 = tm
                        t_h_free = tlast_w2
                        tev = 0
                        for ns in range(4):
                            for ds in range(2):
                                tev = P.op("dve", lambda e, ns=ns, ds=ds,
                                           e_=e_: e.scalar_tensor_tensor(
                                        oacc[:, ns * 1024 + ds * 512:
                                             ns * 1024 + (ds + 1) * 512],
                                        PO4[ns][:, ds * 512:(ds + 1) * 512],
                                        (cwsel[:, e_:e_ + 1] if SPARSE
                                         else cwb[:, e_:e_ + 1]),
                                        oacc[:, ns * 1024 + ds * 512:
                                             ns * 1024 + (ds + 1) * 512],
                                        ALU.mult, ALU.add),
                                    waits=[("PE", tlast_w2),
                                           ("AC", t_cwsel if SPARSE
                                            else t_cwb),
                                           ("VE", t_oacc_init)], inc="VE")
                                if ns == 0:
                                    t_po40_evac[ds] = tev
                        t_oacc_evac = tev

                    for ns in range(4):
                        os_ = ns % 2
                        tst_ = P.op("dve", lambda e, ns=ns: e.bn_stats(
                                st4[:, 0:6], oacc[:, ns * 1024:
                                                  ns * 1024 + 512]),
                            waits=[("VE", t_oacc_evac), ("VE", t_st4_free)],
                            inc="VE")
                        tst_ = P.op("dve", lambda e, ns=ns: e.bn_stats(
                                st4[:, 6:12], oacc[:, ns * 1024 + 512:
                                                   (ns + 1) * 1024]),
                            waits=[("VE", tst_)], inc="VE")
                        tst_ = P.op("dve", lambda e: e.bn_aggr(
                                st4[:, 12:14], st4[:, 0:12]),
                            waits=[("VE", tst_)], inc="VE")
                        tsr = P.op("act", lambda e: e.activation(
                                st4[:, 14:15], st4[:, 13:14], AF.Sqrt,
                                bias=epsv[:]),
                            waits=[("VE", tst_)], inc="AC")
                        trc = P.op("dve", lambda e: e.reciprocal(
                                rs4[:, 0:1], st4[:, 14:15]),
                            waits=[("AC", tsr)], inc="VE")
                        tt = P.op("dve", lambda e, ns=ns: e.tensor_scalar(
                                ep_t[:], oacc[:, ns * 1024:(ns + 1) * 1024],
                                st4[:, 12:13], rs4[:, 0:1],
                                ALU.subtract, ALU.mult),
                            waits=[("VE", trc)], inc="VE")
                        t_oacc_read = tt
                        tt = P.op("dve", lambda e: e.tensor_mul(
                                ep_t[:], ep_t[:], gam_b[:]),
                            waits=[("VE", tt)], inc="VE")
                        tt = P.op("dve", lambda e: e.tensor_add(
                                ep_t[:], ep_t[:], bet_b[:]),
                            waits=[("VE", tt)], inc="VE")
                        tt = P.op("dve", lambda e, ns=ns, ts_=ts_:
                            e.tensor_add(
                                ep_t[:], ep_t[:],
                                x1ft[ts_][:, ns * 1024:(ns + 1) * 1024]),
                            waits=[("VE", tt)], inc="VE")
                        if ns == 3:
                            t_x1ft_free[ts_] = tt
                        tst_ = P.op("dve", lambda e: e.bn_stats(
                                st4[:, 0:6], ep_t[:, 0:512]),
                            waits=[("VE", tt)], inc="VE")
                        tst_ = P.op("dve", lambda e: e.bn_stats(
                                st4[:, 6:12], ep_t[:, 512:1024]),
                            waits=[("VE", tst_)], inc="VE")
                        tst_ = P.op("dve", lambda e: e.bn_aggr(
                                st4[:, 12:14], st4[:, 0:12]),
                            waits=[("VE", tst_)], inc="VE")
                        tsr = P.op("act", lambda e: e.activation(
                                st4[:, 14:15], st4[:, 13:14], AF.Sqrt,
                                bias=epsv[:]),
                            waits=[("VE", tst_)], inc="AC")
                        trc = P.op("dve", lambda e: e.reciprocal(
                                rs4[:, 1:2], st4[:, 14:15]),
                            waits=[("AC", tsr)], inc="VE")
                        t_st4_free = trc
                        tout = P.op("dve", lambda e, os_=os_: e.tensor_scalar(
                                ep_o[os_][:], ep_t[:],
                                st4[:, 12:13], rs4[:, 1:2],
                                ALU.subtract, ALU.mult),
                            waits=[("VE", trc),
                                   (f"ST{os_}", t_out_free[os_])],
                            inc="VE")
                        P.op("pool", lambda e, tile=tile, ns=ns, os_=os_:
                            e.dma_start(
                                out=y_h[tile * 512 + ns * 128:
                                        tile * 512 + (ns + 1) * 128, :],
                                in_=ep_o[os_][:]),
                            waits=[("VE", tout)], inc=f"ST{os_}", incn=16)
                        t_out_free[os_] = P.cnt[f"ST{os_}"]
                    t_x1tt_free[ts_] = tlast_w2

        P.wait("pool", [("ST0", P.cnt["ST0"]), ("ST1", P.cnt["ST1"])])
        P.wait("sync", [("ST0", P.cnt["ST0"]), ("ST1", P.cnt["ST1"])])

        with nc.Block() as block:
            P.run(block)
    return nc


_NC_CACHE = {}


def kernel(**inputs):
    x = np.ascontiguousarray(np.asarray(inputs["L0_attn_x"], dtype=np.float32))
    B, S, _ = x.shape
    if S not in _NC_CACHE:
        _NC_CACHE[S] = build(S=S)
    nc = _NC_CACHE[S]
    w = {
        "wq": np.ascontiguousarray(np.asarray(inputs["L0_attn_Wq"], np.float32)),
        "wk": np.ascontiguousarray(np.asarray(inputs["L0_attn_Wk"], np.float32)),
        "wv": np.ascontiguousarray(np.asarray(inputs["L0_attn_Wv"], np.float32)),
        "wo": np.ascontiguousarray(np.asarray(inputs["L0_attn_Wo"], np.float32)),
        "bo": np.ascontiguousarray(np.asarray(inputs["L0_attn_bo"], np.float32)),
        "w1": np.ascontiguousarray(np.asarray(inputs["L0_ffn_W1"], np.float32)),
        "b1": np.ascontiguousarray(np.asarray(inputs["L0_ffn_b1"], np.float32)),
        "w2": np.ascontiguousarray(np.asarray(inputs["L0_ffn_W2"], np.float32)),
        "b2": np.ascontiguousarray(np.asarray(inputs["L0_ffn_b2"], np.float32)),
        "wr": np.ascontiguousarray(np.asarray(inputs["L0_ffn_Wr"], np.float32)),
        "br": np.ascontiguousarray(np.asarray(inputs["L0_ffn_br"], np.float32)),
        "gamma": np.ascontiguousarray(np.asarray(inputs["L0_ffn_gamma"],
                                                 np.float32)),
        "beta": np.ascontiguousarray(np.asarray(inputs["L0_ffn_beta"],
                                                np.float32)),
    }
    in_maps = [dict(x=np.ascontiguousarray(x[b]), **w) for b in range(B)]
    res = run_bass_kernel_spmd(nc, in_maps, core_ids=list(range(B)))
    return np.stack([np.asarray(r["y"], np.float32) for r in res.results],
                    axis=0)


# version 18
# speedup vs baseline: 1.1077x; 1.1077x over previous
import numpy as np
from contextlib import ExitStack

import concourse.bass as bass
import concourse.mybir as mybir
from concourse.bass_utils import run_bass_kernel_spmd

F32 = mybir.dt.float32
BF16 = mybir.dt.bfloat16
AF = mybir.ActivationFunctionType
ALU = mybir.AluOpType
AX = mybir.AxisListType

D = 1024
H = 16
DH = 64
FF = 4096
E = 8
DC = D // 128
FC = FF // 128
FG = 4
SCALE = DH ** (-0.25)
LN_EPS = 1e-5
SPARSE = True

ENGS = ("pe", "act", "dve", "pool", "sync")
ENGMAP = {"pe": "tensor", "act": "scalar", "dve": "vector",
          "pool": "gpsimd", "sync": "sync"}


class Prog:
    def __init__(self, nc):
        self.nc = nc
        self.Q = {k: [] for k in ENGS}
        self.sems = {}
        self.cnt = {}
        self.lastwait = {k: {} for k in ENGS}

    def add_sem(self, name, handle):
        self.sems[name] = handle
        self.cnt[name] = 0

    def bump(self, s, n=1):
        self.cnt[s] += n
        return self.cnt[s]

    def op(self, eng, fn, waits=(), inc=None, incn=1):
        tok = None
        wl = []
        for s, v in waits:
            if not v:
                continue
            if v > self.lastwait[eng].get(s, 0):
                self.lastwait[eng][s] = v
                wl.append((self.sems[s], v))
        if inc is not None:
            tok = self.bump(inc, incn)
            ih = (self.sems[inc], incn)
        else:
            ih = None

        def f(fn=fn, wl=wl, ih=ih, eng=eng):
            e = getattr(self.nc, ENGMAP[eng])
            for sh, v in wl:
                e.wait_ge(sh, v)
            if fn is not None:
                ins = fn(e)
                if ih is not None:
                    ins.then_inc(ih[0], ih[1])

        self.Q[eng].append(f)
        return tok

    def wait(self, eng, waits):
        self.op(eng, None, waits=waits)

    def barrier(self):
        snap = [("PE", self.cnt["PE"]), ("AC", self.cnt["AC"]),
                ("VE", self.cnt["VE"])]
        for eng, own in (("pe", "PE"), ("act", "AC"), ("dve", "VE")):
            self.wait(eng, [w for w in snap if w[0] != own])
        self.wait("sync", snap)

    def run(self, block):
        @block.tensor
        def _(e):
            for f in self.Q["pe"]:
                f()

        @block.scalar
        def _(e):
            for f in self.Q["act"]:
                f()

        @block.vector
        def _(e):
            for f in self.Q["dve"]:
                f()

        @block.gpsimd
        def _(e):
            for f in self.Q["pool"]:
                f()

        @block.sync
        def _(e):
            for f in self.Q["sync"]:
                f()


def build(S=4096):
    NT = S // 512
    NC = S // 128
    JGRP = NC // 2
    nc = bass.Bass("TRN2", target_bir_lowering=False)

    x_h = nc.dram_tensor("x", [S, D], F32, kind="ExternalInput").ap()
    wq_h = nc.dram_tensor("wq", [D, D], F32, kind="ExternalInput").ap()
    wk_h = nc.dram_tensor("wk", [D, D], F32, kind="ExternalInput").ap()
    wv_h = nc.dram_tensor("wv", [D, D], F32, kind="ExternalInput").ap()
    wo_h = nc.dram_tensor("wo", [D, D], F32, kind="ExternalInput").ap()
    bo_h = nc.dram_tensor("bo", [D], F32, kind="ExternalInput").ap()
    w1_h = nc.dram_tensor("w1", [E, D, FF], F32, kind="ExternalInput").ap()
    b1_h = nc.dram_tensor("b1", [E, FF], F32, kind="ExternalInput").ap()
    w2_h = nc.dram_tensor("w2", [E, FF, D], F32, kind="ExternalInput").ap()
    b2_h = nc.dram_tensor("b2", [E, D], F32, kind="ExternalInput").ap()
    wr_h = nc.dram_tensor("wr", [D, E], F32, kind="ExternalInput").ap()
    br_h = nc.dram_tensor("br", [E], F32, kind="ExternalInput").ap()
    gam_h = nc.dram_tensor("gamma", [D], F32, kind="ExternalInput").ap()
    bet_h = nc.dram_tensor("beta", [D], F32, kind="ExternalInput").ap()
    y_h = nc.dram_tensor("y", [S, D], F32, kind="ExternalOutput").ap()
    qt_h = nc.dram_tensor("qt_s", [D, S], BF16).ap()
    kt_h = nc.dram_tensor("kt_s", [D, S], BF16).ap()
    x1f_h = nc.dram_tensor("x1f_s", [S, D], F32).ap()
    x1t_h = nc.dram_tensor("x1t_s", [DC, 128, S], BF16).ap()
    w1s_h = nc.dram_tensor("w1s_s", [E, FG, DC, 128, 1024], BF16).ap()
    w2s_h = nc.dram_tensor("w2s_s", [E, FC, 128, D], BF16).ap()
    w1sel_h = nc.dram_tensor("w1sel_s", [2, D, FF], BF16).ap()
    w2sel_h = nc.dram_tensor("w2sel_s", [2, FF, D], BF16).ap()
    wf1_h = nc.dram_tensor("wf1_s", [2, D, FF], F32).ap()
    wf2_h = nc.dram_tensor("wf2_s", [2, FF, D], F32).ap()

    P = Prog(nc)
    with ExitStack() as CTX:
        _semnames = ["PE", "AC", "VE", "PO", "Lsm", "Lb", "Lx0", "Lx1",
                     "LW", "Lk", "Lq0", "Lq1", "Sqk0", "Sqk1", "Sx1f",
                     "Sx1t", "Lw1a", "Lw1b", "Lw2a", "Lw2b", "Lxt0", "Lxt1",
                     "Lxf0", "Lxf1", "GCl", "GCs0", "GCs1", "ST0", "ST1", "GS", "GS2", "GB", "Lk1", "Lw1c", "Lw1d", "GCl2", "DS0", "DS1", "DS2", "DS3"]
        for sname in _semnames:
            P.add_sem(sname, CTX.enter_context(nc.semaphore(sname)))

        def sb(stack, name, shape, dtype=F32):
            return stack.enter_context(nc.sbuf_tensor(name, shape, dtype))

        def ps(stack, name, shape, dtype=F32):
            return stack.enter_context(nc.psum_tensor(name, shape, dtype))

        def load(dst, src, sem="Lsm", waits=()):
            return P.op("sync", lambda e, d=dst, s=src: e.dma_start(out=d, in_=s),
                        waits=waits, inc=sem, incn=16)

        store = load

        ones64 = sb(CTX, "ones64", [1, 64], F32)
        ones128 = sb(CTX, "ones128", [1, 128], F32)
        ident_b = sb(CTX, "ident_b", [128, 128], BF16)
        ident_f = sb(CTX, "ident_f", [128, 128], F32)
        sel = sb(CTX, "sel", [16, 2048], BF16)
        bo_row = sb(CTX, "bo_row", [1, D], F32)
        gam_row = sb(CTX, "gam_row", [1, D], F32)
        bet_row = sb(CTX, "bet_row", [1, D], F32)
        br_row = sb(CTX, "br_row", [1, E], F32)
        wr_sb = sb(CTX, "wr_sb", [128, DC * E], F32)
        b1T = None if SPARSE else sb(CTX, "b1T", [128, E * FC], F32)
        b2cb = sb(CTX, "b2cb", [128, D], F32)
        cwb = sb(CTX, "cwb", [128, E], F32)
        cwT = sb(CTX, "cwT", [8, 1], F32)
        b2c_row = sb(CTX, "b2c_row", [1, D], F32)
        bias_all = sb(CTX, "bias_all", [128, H * NT], F32)
        accq = sb(CTX, "accq", [128, NT * 16], F32)
        acck = sb(CTX, "acck", [128, 16], F32)
        qtbl = sb(CTX, "qtbl", [16, NT], F32)
        kmax2 = sb(CTX, "kmax2", [16, 1], F32)
        mtbl = sb(CTX, "mtbl", [16, NT], BF16)
        xsacc = sb(CTX, "xsacc", [128, E], F32)
        epsv = sb(CTX, "epsv", [128, 1], F32)
        io8 = sb(CTX, "io8", [1, 8], F32)
        r_tmp8 = sb(CTX, "r_tmp8", [1, 8], F32)
        r_ef = sb(CTX, "r_ef", [1, 2], F32)
        idx_sb = sb(CTX, "idx_sb", [1, 2], mybir.dt.int32)
        r_cws = sb(CTX, "r_cws", [1, 2], F32)
        cwsel = sb(CTX, "cwsel", [128, 2], F32)
        b1Tsel = sb(CTX, "b1Tsel", [128, FC * 2], F32)
        r_z = sb(CTX, "r_z", [1, E], F32)
        r_rw = sb(CTX, "r_rw", [1, E], F32)
        r_rw2 = sb(CTX, "r_rw2", [1, E], F32)
        r_m1 = sb(CTX, "r_m1", [1, 1], F32)
        r_m2 = sb(CTX, "r_m2", [1, 1], F32)
        r_m2n = sb(CTX, "r_m2n", [1, 1], F32)
        r_w1 = sb(CTX, "r_w1", [1, 1], F32)
        r_s = sb(CTX, "r_s", [1, 1], F32)
        r_mask1 = sb(CTX, "r_mask1", [1, E], F32)
        r_mask2 = sb(CTX, "r_mask2", [1, E], F32)
        r_d12 = sb(CTX, "r_d12", [1, E], F32)
        cw_row = sb(CTX, "cw_row", [1, E], F32)

        tpo = 0
        for buf, val in ((ones64, 1.0), (ones128, 1.0), (ident_b, 0.0),
                         (ident_f, 0.0), (sel, 1.0), (accq, 0.0), (acck, 0.0),
                         (xsacc, 0.0), (epsv, LN_EPS)):
            tpo = P.op("pool", lambda e, b=buf, v=val: e.memset(b[:], v),
                       inc="PO")
        tmemset = tpo
        tpo = P.op("pool", lambda e: e.affine_select(
            out=ident_b[:], in_=ident_b[:], compare_op=ALU.not_equal,
            fill=1.0, base=0, pattern=[[-1, 128]], channel_multiplier=1),
            waits=[("PO", tmemset)], inc="PO")
        tpo = P.op("pool", lambda e: e.affine_select(
            out=ident_f[:], in_=ident_f[:], compare_op=ALU.not_equal,
            fill=1.0, base=0, pattern=[[-1, 128]], channel_multiplier=1),
            waits=[("PO", tmemset)], inc="PO")
        tpo = P.op("pool", lambda e: e.affine_select(
            out=sel[:], in_=sel[:], compare_op=ALU.is_ge,
            fill=0.0, base=0, pattern=[[1, 2048]], channel_multiplier=-128),
            waits=[("PO", tmemset)], inc="PO")
        tpo = P.op("pool", lambda e: e.affine_select(
            out=sel[:], in_=sel[:], compare_op=ALU.is_ge,
            fill=0.0, base=127, pattern=[[-1, 2048]], channel_multiplier=128),
            waits=[("PO", tpo)], inc="PO")
        tpo = P.op("pool", lambda e: e.iota(
            io8[:], pattern=[[1, 8]], channel_multiplier=0,
            allow_small_or_imprecise_dtypes=True), inc="PO")
        t_const = tpo

        load(bo_row[:], bo_h.rearrange("(o d) -> o d", o=1))
        load(gam_row[:], gam_h.rearrange("(o d) -> o d", o=1))
        load(bet_row[:], bet_h.rearrange("(o d) -> o d", o=1))
        load(br_row[:], br_h.rearrange("(o e) -> o e", o=1))
        t_small = load(wr_sb[:].rearrange("p (c e) -> p c e", e=E),
                       wr_h.rearrange("(c p) e -> p c e", p=128))

        if SPARSE:
            scf = [sb(CTX, f"scf{i}", [128, 1024], F32) for i in range(2)]
            scb = [sb(CTX, f"scb{i}", [128, 1024], BF16) for i in range(2)]
        else:
            cst_f = [sb(CTX, f"cstf{i}", [128, 512], F32) for i in range(2)]
            cst_b = [sb(CTX, f"cstb{i}", [128, 512], BF16) for i in range(2)]
        cast_jobs = []
        for e_ in range(E):
            for dc in range(DC):
                for fg in range(FG):
                    for hh in range(2):
                        cast_jobs.append((
                            w1_h[e_, dc * 128:(dc + 1) * 128,
                                 fg * 1024 + hh * 512:
                                 fg * 1024 + (hh + 1) * 512],
                            w1s_h[e_, fg, dc][:, hh * 512:(hh + 1) * 512]))
            for fc in range(FC):
                for hh in range(2):
                    cast_jobs.append((
                        w2_h[e_, fc * 128:(fc + 1) * 128,
                             hh * 512:(hh + 1) * 512],
                        w2s_h[e_, fc][:, hh * 512:(hh + 1) * 512]))
        cast_done = {}
        if not SPARSE:
            cast_jobs = []
            for e_ in range(E):
                for dc in range(DC):
                    for fg in range(FG):
                        for hh in range(2):
                            cast_jobs.append((
                                w1_h[e_, dc * 128:(dc + 1) * 128,
                                     fg * 1024 + hh * 512:
                                     fg * 1024 + (hh + 1) * 512],
                                w1s_h[e_, fg, dc][:, hh * 512:(hh + 1) * 512]))
                for fc in range(FC):
                    for hh in range(2):
                        cast_jobs.append((
                            w2_h[e_, fc * 128:(fc + 1) * 128,
                                 hh * 512:(hh + 1) * 512],
                            w2s_h[e_, fc][:, hh * 512:(hh + 1) * 512]))
            cfree = [0, 0]
            csfree = [0, 0]
            per_e = len(cast_jobs) // E
            for j, (csrc, cdst) in enumerate(cast_jobs):
                i = j % 2
                tl = P.op("pool", lambda e, s=csrc, i=i: e.dma_start(
                    out=cst_f[i][:], in_=s), waits=[("PO", cfree[i])],
                    inc="GCl", incn=16)
                tc = P.op("pool", lambda e, i=i: e.tensor_copy(
                    cst_b[i][:], cst_f[i][:]),
                    waits=[("GCl", tl), (f"GCs{i}", csfree[i])], inc="PO")
                cfree[i] = tc
                tst = P.op("pool", lambda e, d=cdst, i=i: e.dma_start(
                    out=d, in_=cst_b[i][:]), waits=[("PO", tc)],
                    inc=f"GCs{i}", incn=16)
                csfree[i] = tst
                if (j + 1) % per_e == 0:
                    cast_done[(j + 1) // per_e - 1] = (
                        P.cnt["GCs0"], P.cnt["GCs1"])

        with ExitStack() as C_AT:
            with ExitStack() as C_V:
                v_sb = sb(C_V, "v_sb", [128, NC * H * 65], BF16)

                with ExitStack() as C1:
                    wq_bf = sb(C1, "wq_bf", [128, DC * 1024], BF16)
                    wk_bf = sb(C1, "wk_bf", [128, DC * 1024], BF16)
                    wv_bf = sb(C1, "wv_bf", [128, DC * 1024], BF16)
                    wstg = sb(C1, "wstg", [128, 2048], F32)
                    xstg = sb(C1, "xstg", [128, 4096], F32)
                    xb = sb(C1, "xb", [128, 4096], BF16)
                    xT = sb(C1, "xT", [128, DC * 512], BF16)
                    qkst = [sb(C1, f"qkst{i}", [128, 512], BF16)
                            for i in range(2)]
                    sq_sb = [sb(C1, f"sq{i}", [128, 512], F32)
                             for i in range(2)]
                    nrm8 = [sb(C1, f"nrm8{i}", [128, 8], F32)
                            for i in range(2)]

                    with ExitStack() as PS1:
                        psT = [ps(PS1, f"psT{i}", [128, 1024], BF16)
                               for i in range(2)]
                        ps_q = ps(PS1, "ps_q", [128, 512], F32)
                        ps_k = ps(PS1, "ps_k", [128, 512], F32)
                        ps_v = ps(PS1, "ps_v", [128, 1024], F32)
                        ps_n = ps(PS1, "ps_n", [128, 1024], F32)

                        t_psq_free = 0
                        t_psn_free = [0, 0]
                        t_wst_free = 0
                        wtoks = []
                        for wh, wbf in ((wq_h, wq_bf), (wk_h, wk_bf),
                                        (wv_h, wv_bf)):
                            tdone = 0
                            for half in range(4):
                                tl = load(wstg[:].rearrange(
                                              "p (c d) -> p c d", d=1024),
                                          wh.rearrange("(c p) d -> p c d",
                                                       p=128)
                                          [:, half * 2:(half + 1) * 2, :],
                                          sem="LW",
                                          waits=[("VE", t_wst_free)])
                                tdone = P.op("dve", lambda e, w=wbf,
                                             half=half: e.tensor_copy(
                                        w[:, half * 2048:(half + 1) * 2048],
                                        wstg[:]),
                                    waits=[("LW", tl)], inc="VE")
                                t_wst_free = tdone
                            wtoks.append(tdone)
                        t_wq, t_wk, t_wv = wtoks

                        t_vones = P.op("dve", lambda e: e.memset(
                            v_sb[:].rearrange("p (c u) -> p c u", u=65)
                            [:, :, 64:65], 1.0), inc="VE")

                        t_xstg_free = 0
                        t_xb_free = 0
                        t_psT_free = [0, 0]
                        t_psk_free = 0
                        t_psv_free = [0, 0]
                        t_qkst_free = [0, 0]
                        t_sq_free = [0, 0]
                        t_nrm_free = [0, 0]
                        qk_slot = 0
                        for nt in range(NT):
                            tl = load(xstg[:].rearrange(
                                          "p (s d) -> p s d", d=D),
                                      x_h.rearrange("(t s p) d -> t p s d",
                                                    p=128, s=4)[nt],
                                      sem="Lx0",
                                      waits=[("VE", t_xstg_free)])
                            t_cast = P.op("dve", lambda e: e.tensor_copy(
                                xb[:], xstg[:]),
                                waits=[("Lx0", tl), ("PE", t_xb_free)],
                                inc="VE")
                            t_xstg_free = t_cast
                            t_xT_evac = [0, 0, 0, 0]
                            tm = 0
                            for ns in range(4):
                                for dc in range(DC):
                                    tm = P.op("pe", lambda e, ns=ns, dc=dc:
                                        e.transpose(
                                            psT[ns % 2][:, dc * 128:
                                                        (dc + 1) * 128],
                                            xb[:, ns * 1024 + dc * 128:
                                               ns * 1024 + (dc + 1) * 128],
                                            ident_b[:]),
                                        waits=[("VE", t_cast),
                                               ("PO", t_const),
                                               ("AC", t_psT_free[ns % 2])],
                                        inc="PE")
                                t_xT_evac[ns] = P.op("act", lambda e, ns=ns:
                                    e.activation(
                                        xT[:].rearrange("p (dc n) -> p dc n",
                                                        n=512)
                                        [:, :, ns * 128:(ns + 1) * 128],
                                        psT[ns % 2][:], AF.Copy),
                                    waits=[("PE", tm)], inc="AC")
                                t_psT_free[ns % 2] = t_xT_evac[ns]
                            t_xb_free = tm
                            t_xT = t_xT_evac[3]

                            for dpc in range(DC):
                                for wbf, twc, psq, which in (
                                        (wq_bf, t_wq, ps_q, "q"),
                                        (wk_bf, t_wk, ps_k, "k")):
                                    pfree = (t_psq_free if which == "q"
                                             else t_psk_free)
                                    tm = 0
                                    for dc in range(DC):
                                        tm = P.op("pe", lambda e, w=wbf,
                                                  dc=dc, dpc=dpc, p_=psq:
                                            e.matmul(
                                                p_[:],
                                                w[:, dc * 1024 + dpc * 128:
                                                  dc * 1024 + (dpc + 1) * 128],
                                                xT[:, dc * 512:(dc + 1) * 512],
                                                start=(dc == 0),
                                                stop=(dc == DC - 1)),
                                            waits=[("AC", t_xT), ("VE", twc),
                                                   ("AC", pfree)], inc="PE")
                                    ta = P.op("act", lambda e, slot=qk_slot,
                                              p_=psq: e.activation(
                                            qkst[slot][:], p_[:], AF.Copy,
                                            scale=SCALE),
                                        waits=[("PE", tm),
                                               (f"Sqk{qk_slot}",
                                                t_qkst_free[qk_slot])],
                                        inc="AC")
                                    if which == "q":
                                        t_psq_free = ta
                                    else:
                                        t_psk_free = ta
                                    dsth = qt_h if which == "q" else kt_h
                                    tst = store(
                                        dsth[dpc * 128:(dpc + 1) * 128,
                                             nt * 512:(nt + 1) * 512],
                                        qkst[qk_slot][:],
                                        sem=f"Sqk{qk_slot}",
                                        waits=[("AC", ta)])
                                    t_qkst_free[qk_slot] = tst
                                    qk_slot = (qk_slot + 1) % 2

                            for ns in range(4):
                                for ds in range(2):
                                    half = ds
                                    tm = 0
                                    for dc in range(DC):
                                        tm = P.op("pe", lambda e, ns=ns,
                                                  ds=ds, dc=dc, half=half:
                                            e.matmul(
                                                ps_v[:, half * 512:
                                                     (half + 1) * 512],
                                                xT[:, dc * 512 + ns * 128:
                                                   dc * 512 + (ns + 1) * 128],
                                                wv_bf[:, dc * 1024 + ds * 512:
                                                      dc * 1024 +
                                                      (ds + 1) * 512],
                                                start=(dc == 0),
                                                stop=(dc == DC - 1)),
                                            waits=[("AC", t_xT), ("VE", t_wv),
                                                   ("AC", t_psv_free[half]),
                                                   ("VE", t_vones)],
                                            inc="PE")
                                    base = (nt * 4 + ns) * (H * 65) + ds * 520
                                    t_psv_free[half] = P.op("act",
                                        lambda e, half=half, base=base:
                                        e.activation(
                                            v_sb[:, base:base + 520]
                                            .rearrange("p (h u) -> p h u",
                                                       u=65)[:, :, 0:64],
                                            ps_v[:, half * 512:
                                                 (half + 1) * 512],
                                            AF.Copy),
                                        waits=[("PE", tm)], inc="AC")

                            for wbf, twc, which in ((wq_bf, t_wq, "q"),
                                                    (wk_bf, t_wk, "k")):
                                for ns in range(4):
                                    for ds in range(2):
                                        half = (ns * 2 + ds) % 2
                                        tm = 0
                                        for dc in range(DC):
                                            tm = P.op("pe", lambda e, w=wbf,
                                                      ns=ns, ds=ds, dc=dc,
                                                      half=half: e.matmul(
                                                    ps_n[:, half * 512:
                                                         (half + 1) * 512],
                                                    xT[:, dc * 512 + ns * 128:
                                                       dc * 512 +
                                                       (ns + 1) * 128],
                                                    w[:, dc * 1024 + ds * 512:
                                                      dc * 1024 +
                                                      (ds + 1) * 512],
                                                    start=(dc == 0),
                                                    stop=(dc == DC - 1)),
                                                waits=[("AC", t_xT),
                                                       ("VE", twc),
                                                       ("AC",
                                                        t_psn_free[half])],
                                                inc="PE")
                                        tsq = P.op("act", lambda e, half=half:
                                            e.activation(
                                                sq_sb[half][:],
                                                ps_n[:, half * 512:
                                                     (half + 1) * 512],
                                                AF.Square, scale=SCALE),
                                            waits=[("PE", tm),
                                                   ("VE", t_sq_free[half])],
                                            inc="AC")
                                        t_psn_free[half] = tsq
                                        tr = 0
                                        for hh in range(8):
                                            tr = P.op("dve", lambda e,
                                                      half=half, hh=hh:
                                                e.reduce_sum(
                                                    nrm8[half][:, hh:hh + 1],
                                                    sq_sb[half][:, hh * 64:
                                                                (hh + 1) * 64],
                                                    axis=AX.X),
                                                waits=[("AC", tsq),
                                                       ("VE",
                                                        t_nrm_free[half])],
                                                inc="VE")
                                        if which == "q":
                                            a_ = accq[:, nt * 16 + ds * 8:
                                                      nt * 16 + ds * 8 + 8]
                                        else:
                                            a_ = acck[:, ds * 8:ds * 8 + 8]
                                        tmx = P.op("dve", lambda e, a=a_,
                                                   half=half: e.tensor_max(
                                                a, a, nrm8[half][:]),
                                            waits=[("VE", tr),
                                                   ("PO", t_const)],
                                            inc="VE")
                                        t_nrm_free[half] = tmx
                                        t_sq_free[half] = tr

                        t_acc = P.cnt["VE"]
                        tred = 0
                        for qt in range(NT):
                            tm = P.op("pe", lambda e, qt=qt: e.transpose(
                                ps_n[0:16, 0:128],
                                accq[:, qt * 16:(qt + 1) * 16], ident_f[:]),
                                waits=[("VE", t_acc), ("VE", tred),
                                       ("AC", max(t_psn_free))], inc="PE")
                            tred = P.op("dve", lambda e, qt=qt: e.reduce_max(
                                qtbl[0:16, qt:qt + 1], ps_n[0:16, 0:128],
                                axis=AX.X),
                                waits=[("PE", tm)], inc="VE")
                        tm = P.op("pe", lambda e: e.transpose(
                            ps_n[0:16, 0:128], acck[:, 0:16], ident_f[:]),
                            waits=[("VE", t_acc), ("VE", tred)], inc="PE")
                        tkm = P.op("dve", lambda e: e.reduce_max(
                            kmax2[0:16, 0:1], ps_n[0:16, 0:128], axis=AX.X),
                            waits=[("PE", tm)], inc="VE")
                        tq = P.op("dve", lambda e: e.tensor_scalar_mul(
                            qtbl[0:16, :], qtbl[0:16, :], kmax2[0:16, 0:1]),
                            waits=[("VE", tkm)], inc="VE")
                        tsq2 = P.op("act", lambda e: e.activation(
                            mtbl[0:16, :], qtbl[0:16, :], AF.Sqrt),
                            waits=[("VE", tq)], inc="AC")
                        tneg = P.op("dve", lambda e: e.tensor_scalar_mul(
                            mtbl[0:16, :], mtbl[0:16, :], -1.0),
                            waits=[("AC", tsq2)], inc="VE")
                        tm = 0
                        for h in range(H):
                            tm = P.op("pe", lambda e, h=h: e.matmul(
                                ps_n[:, h * NT:(h + 1) * NT],
                                sel[:, h * 128:(h + 1) * 128],
                                mtbl[0:16, :], start=True, stop=True),
                                waits=[("VE", tneg), ("PO", t_const),
                                       ("VE", tkm)], inc="PE")
                        t_bias = P.op("act", lambda e: e.activation(
                            bias_all[:], ps_n[:, 0:H * NT], AF.Copy),
                            waits=[("PE", tm)], inc="AC")

                attnT = C_AT.enter_context(nc.sbuf_tensor("attnT", [128, DC * S], BF16, side="right"))

                P.barrier()
                with ExitStack() as C2:
                    ktsl = [sb(C2, f"ktsl{i}", [128, S], BF16)
                            for i in range(2)]
                    qtsl = [sb(C2, f"qtsl{i}", [128, 512], BF16)
                            for i in range(2)]
                    ptA = sb(C2, "ptA", [128, 3 * 1024], BF16)
                    rp_sb = sb(C2, "rp_sb", [1, 512], F32)
                    nb_sb = sb(C2, "nb_sb", [64, 512], F32)
                    avst = [sb(C2, f"avst{i}", [128, 512], F32)
                            for i in range(2)]
                    rp2 = [sb(C2, f"rp2_{i}", [1, 512], F32)
                           for i in range(2)]
                    with ExitStack() as PS2:
                        SAB = [ps(PS2, f"SAB{i}", [128, 1024], F32)
                               for i in range(3)]
                        AVH = ps(PS2, "AVH", [128, 512], F32)
                        BC = ps(PS2, "BC", [128, 512], F32)

                        t_ktsl_free = 0
                        t_qt_load = [0, 0]
                        t_qtsl_free = [0, 0]
                        t_S_free = [0, 0, 0]
                        t_pt_free = [0, 0, 0]
                        t_AV_read = (0, 0)
                        t_avst_free = [0, 0]
                        t_nb_free = 0
                        t_rp_free = [0, 0]
                        t_BC_free = 0
                        norm_pend = []
                        uu = 0

                        def emit_norm(item):
                            nonlocal t_nb_free, t_BC_free
                            slot, h, nqt, tevp, trp = item
                            tbc = P.op("pe", lambda e, slot=slot:
                                e.matmul(
                                    BC[0:64, :], ones64[:],
                                    rp2[slot][:], start=True, stop=True),
                                waits=[("VE", trp), ("VE", t_BC_free)],
                                inc="PE")
                            t_rp_free[slot] = tbc
                            tnb = P.op("dve", lambda e:
                                e.tensor_copy(nb_sb[:], BC[0:64, :]),
                                waits=[("PE", tbc), ("VE", t_nb_free)],
                                inc="VE")
                            t_BC_free = tnb
                            pb = (h % 2) * 64
                            dst = attnT[pb:pb + 64,
                                        (h // 2) * S + nqt * 512:
                                        (h // 2) * S + (nqt + 1) * 512]
                            tev = P.op("dve", lambda e, slot=slot,
                                       dst=dst: e.tensor_mul(
                                    dst, avst[slot][0:64, :], nb_sb[:]),
                                waits=[("VE", tnb), ("AC", tevp)], inc="VE")
                            t_avst_free[slot] = tev
                            t_nb_free = tev

                        t_qk_f0 = P.cnt["Sqk0"]
                        t_qk_f1 = P.cnt["Sqk1"]
                        t_ktsl_free2 = [0, 0]
                        for h in range(H):
                            ks = h % 2
                            pr = h // 2
                            hi = h % 2
                            t_kt_load = 0
                            for dup in range(2):
                                t_kt_load = load(
                                    ktsl[ks][dup * 64:dup * 64 + 64, :],
                                    kt_h[h * 64:(h + 1) * 64, :],
                                    sem=f"Lk{ks}" if ks else "Lk",
                                    waits=[("Sqk0", t_qk_f0),
                                           ("Sqk1", t_qk_f1),
                                           ("PE", t_ktsl_free2[ks])])
                            for qt in range(NT):
                                qs = qt % 2
                                t_qt_load[qs] = 0
                                for dup in range(2):
                                    t_qt_load[qs] = load(
                                        qtsl[qs][dup * 64:dup * 64 + 64, :],
                                        qt_h[h * 64:(h + 1) * 64,
                                             qt * 512:(qt + 1) * 512],
                                        sem=f"Lq{qs}",
                                        waits=[("Sqk0", t_qk_f0),
                                               ("Sqk1", t_qk_f1),
                                               ("PE", t_qtsl_free[qs])])
                                exp_tok = {}
                                tAV = 0

                                def emit_av(g):
                                    nonlocal tAV
                                    gs = g % 3
                                    eT = exp_tok[g]
                                    for jj in range(2):
                                        first = (g == 0 and jj == 0)
                                        last = (g == JGRP - 1 and jj == 1)
                                        jc = g * 2 + jj
                                        vh = v_sb[:,
                                            jc * (H * 65) + h * 65:
                                            jc * (H * 65) + h * 65 + 65]
                                        tAV = P.op("pe", lambda e, vh=vh,
                                            gs=gs, jj=jj, first=first,
                                            last=last: e.matmul(
                                                AVH[0:65, :], vh,
                                                ptA[:, gs * 1024 +
                                                    jj * 512:
                                                    gs * 1024 +
                                                    (jj + 1) * 512],
                                                start=first, stop=last),
                                            waits=[("AC", eT),
                                                   ("VE", t_vones),
                                                   ("AC", t_AV_read[0]),
                                                   ("VE", t_AV_read[1])],
                                            inc="PE")
                                        t_pt_free[gs] = tAV

                                for g in range(JGRP):
                                    gs = g % 3
                                    ps_ = g % 3
                                    tS = 0
                                    for jj in range(2):
                                        jc = g * 2 + jj
                                        pbase = (jc % 2) * 64
                                        tS = P.op("pe", lambda e, jc=jc,
                                            jj=jj, qs=qs, gs=gs, ks=ks,
                                            pbase=pbase: e.matmul(
                                                SAB[gs][:, jj * 512:
                                                        (jj + 1) * 512],
                                                ktsl[ks][pbase:pbase + 64,
                                                     jc * 128:
                                                     (jc + 1) * 128],
                                                qtsl[qs][pbase:
                                                         pbase + 64, :],
                                                start=True, stop=True,
                                                tile_position=(pbase, 0)),
                                            waits=[(("Lk1" if ks else "Lk"),
                                                    t_kt_load),
                                                   (f"Lq{qs}",
                                                    t_qt_load[qs]),
                                                   ("AC", t_S_free[gs])],
                                            inc="PE")
                                    texp = P.op("act", lambda e, gs=gs,
                                        ps_=ps_, h=h, qt=qt: e.activation(
                                            ptA[:, ps_ * 1024:
                                                (ps_ + 1) * 1024],
                                            SAB[gs][:], AF.Exp,
                                            bias=bias_all[:,
                                                h * NT + qt:
                                                h * NT + qt + 1]),
                                        waits=[("PE", tS), ("AC", t_bias),
                                               ("PE", t_pt_free[ps_])],
                                        inc="AC")
                                    t_S_free[gs] = texp
                                    exp_tok[g] = texp
                                    if g > 0:
                                        emit_av(g - 1)
                                emit_av(JGRP - 1)
                                slot = uu % 2
                                tevp = P.op("act", lambda e, slot=slot:
                                    e.activation(
                                        avst[slot][0:64, :],
                                        AVH[0:64, :], AF.Copy),
                                    waits=[("PE", tAV),
                                           ("VE", t_avst_free[slot])],
                                    inc="AC")
                                trp = P.op("dve", lambda e, slot=slot:
                                    e.reciprocal(rp2[slot][:],
                                                 AVH[64:65, :]),
                                    waits=[("PE", tAV),
                                           ("PE", t_rp_free[slot])],
                                    inc="VE")
                                t_AV_read = (tevp, trp)
                                norm_pend.append((slot, h, qt, tevp, trp))
                                if len(norm_pend) > 1:
                                    emit_norm(norm_pend.pop(0))
                                t_qtsl_free[qs] = tAV
                                uu += 1
                            t_ktsl_free2[ks] = tAV
                        for item in norm_pend:
                            emit_norm(item)
                        t_attnT_ve = P.cnt["VE"]

            P.barrier()
            with ExitStack() as C3:
                wo_bf = sb(C3, "wo_bf", [128, DC * 1024], BF16)
                b1_sb = sb(C3, "b1_sb", [E, FF], F32)
                b2_sb = sb(C3, "b2_sb", [E, D], F32)
                bo_b = sb(C3, "bo_b", [128, D], F32)
                b1pair = sb(C3, "b1pair", [2, FF], F32)
                wstg3 = sb(C3, "wstg3", [128, 2048], F32)
                xt3 = [sb(C3, f"xt3_{i}", [128, 1024], F32) for i in range(2)]
                y3_sb = sb(C3, "y3_sb", [128, 1024], F32)
                x1_sb = sb(C3, "x1_sb", [128, 1024], F32)
                x1b_sb = sb(C3, "x1b_sb", [128, 1024], BF16)
                x1ts = sb(C3, "x1ts", [128, DC * 512], BF16)
                st3 = sb(C3, "st3", [128, 16], F32)
                rs3 = sb(C3, "rs3", [128, 2], F32)
                rsum8 = sb(C3, "rsum8", [128, 8], F32)
                with ExitStack() as PS3:
                    PO3 = ps(PS3, "PO3", [128, 1024], F32)
                    psT3 = ps(PS3, "psT3", [128, 1024], BF16)
                    RT = ps(PS3, "RT", [128, 512], F32)

                    t_b12 = load(b1_sb[:], b1_h, sem="Lb")
                    t_b12 = load(b2_sb[:], b2_h, sem="Lb")
                    t_bc_ac = 0
                    for hf in range(2):
                        tmb = P.op("pe", lambda e, hf=hf: e.matmul(
                                RT[:], ones128[:],
                                bo_row[:, hf * 512:(hf + 1) * 512],
                                start=True, stop=True),
                            waits=[("Lsm", t_small), ("PO", t_const),
                                   ("AC", t_bc_ac)], inc="PE")
                        t_bc_ac = P.op("act", lambda e, hf=hf: e.activation(
                                bo_b[:, hf * 512:(hf + 1) * 512], RT[:],
                                AF.Copy),
                            waits=[("PE", tmb)], inc="AC")
                    t_b1T = t_bc_ac
                    for g8 in (range(FC // 8) if not SPARSE else []):
                        tmb = 0
                        for j in range(8):
                            fc = g8 * 8 + j
                            tmb = P.op("pe", lambda e, fc=fc, j=j:
                                e.transpose(
                                    RT[:, j * 8:(j + 1) * 8],
                                    b1_sb[:, fc * 128:(fc + 1) * 128],
                                    ident_f[0:8, 0:8]),
                                waits=[("Lb", t_b12), ("PO", t_const),
                                       ("AC", t_b1T)], inc="PE")
                        t_b1T = P.op("act", lambda e, g8=g8: e.activation(
                                b1T[:, g8 * 64:(g8 + 1) * 64],
                                RT[:, 0:64], AF.Copy),
                            waits=[("PE", tmb)], inc="AC")

                    t_wo = 0
                    t_wst3 = 0
                    for half in range(4):
                        tl = load(wstg3[:].rearrange("p (c d) -> p c d",
                                                     d=1024),
                                  wo_h.rearrange("(c p) d -> p c d", p=128)
                                  [:, half * 2:(half + 1) * 2, :],
                                  sem="LW",
                                  waits=[("VE", t_wst3)])
                        t_wo = P.op("dve", lambda e, half=half: e.tensor_copy(
                                wo_bf[:, half * 2048:(half + 1) * 2048],
                                wstg3[:]),
                            waits=[("LW", tl)], inc="VE")
                        t_wst3 = t_wo

                    t_xt3_free = [0, 0]
                    t_PO3_free = [0, 0]
                    t_psT3_free = 0
                    t_x1ts_free = 0
                    t_y3_free = 0
                    t_x1_free = 0
                    t_x1b_free = 0
                    t_st3_free = 0
                    t_rsum_free = 0
                    t_x1f_store = 0
                    t_x1t_store = 0
                    for nc_ in range(NC):
                        xs = nc_ % 2
                        tlx = load(xt3[xs][:],
                                   x_h[nc_ * 128:(nc_ + 1) * 128, :],
                                   sem=f"Lx{xs}",
                                   waits=[("VE", t_xt3_free[xs])])
                        tds = [0, 0]
                        for ds in range(2):
                            tm = 0
                            for dc in range(DC):
                                tm = P.op("pe", lambda e, dc=dc, ds=ds,
                                          nc_=nc_: e.matmul(
                                        PO3[:, ds * 512:(ds + 1) * 512],
                                        attnT[:, dc * S + nc_ * 128:
                                              dc * S + (nc_ + 1) * 128],
                                        wo_bf[:, dc * 1024 + ds * 512:
                                              dc * 1024 + (ds + 1) * 512],
                                        start=(dc == 0), stop=(dc == DC - 1)),
                                    waits=[("VE", t_attnT_ve), ("VE", t_wo),
                                           ("VE", t_PO3_free[ds])], inc="PE")
                            tds[ds] = P.op("dve", lambda e, ds=ds, xs=xs:
                                e.tensor_add(
                                    y3_sb[:, ds * 512:(ds + 1) * 512],
                                    PO3[:, ds * 512:(ds + 1) * 512],
                                    xt3[xs][:, ds * 512:(ds + 1) * 512]),
                                waits=[("PE", tm), (f"Lx{xs}", tlx),
                                       ("VE", t_y3_free)], inc="VE")
                            t_PO3_free[ds] = tds[ds]
                        t_xt3_free[xs] = tds[1]
                        tyb = P.op("dve", lambda e: e.tensor_add(
                                y3_sb[:], y3_sb[:], bo_b[:]),
                            waits=[("VE", tds[1]), ("AC", t_bc_ac)], inc="VE")
                        tst = P.op("dve", lambda e: e.bn_stats(
                                st3[:, 0:6], y3_sb[:, 0:512]),
                            waits=[("VE", tyb), ("VE", t_st3_free)], inc="VE")
                        tst = P.op("dve", lambda e: e.bn_stats(
                                st3[:, 6:12], y3_sb[:, 512:1024]),
                            waits=[("VE", tst)], inc="VE")
                        tst = P.op("dve", lambda e: e.bn_aggr(
                                st3[:, 12:14], st3[:, 0:12]),
                            waits=[("VE", tst)], inc="VE")
                        tsr = P.op("act", lambda e: e.activation(
                                st3[:, 14:15], st3[:, 13:14], AF.Sqrt,
                                bias=epsv[:]),
                            waits=[("VE", tst)], inc="AC")
                        trc = P.op("dve", lambda e: e.reciprocal(
                                rs3[:, 0:1], st3[:, 14:15]),
                            waits=[("AC", tsr)], inc="VE")
                        t_st3_free = trc
                        tx1 = P.op("dve", lambda e: e.tensor_scalar(
                                x1_sb[:], y3_sb[:], st3[:, 12:13],
                                rs3[:, 0:1], ALU.subtract, ALU.mult),
                            waits=[("VE", trc), ("Sx1f", t_x1_free)],
                            inc="VE")
                        t_y3_free = tx1
                        t_x1f_store = store(
                            x1f_h[nc_ * 128:(nc_ + 1) * 128, :], x1_sb[:],
                            sem="Sx1f", waits=[("VE", tx1)])
                        t_x1_free = t_x1f_store
                        txb = P.op("dve", lambda e: e.tensor_copy(
                                x1b_sb[:], x1_sb[:]),
                            waits=[("VE", tx1), ("PE", t_x1b_free)], inc="VE")
                        tm = 0
                        for dc in range(DC):
                            tm = P.op("pe", lambda e, dc=dc: e.transpose(
                                    psT3[:, dc * 128:(dc + 1) * 128],
                                    x1b_sb[:, dc * 128:(dc + 1) * 128],
                                    ident_b[:]),
                                waits=[("VE", txb), ("AC", t_psT3_free),
                                       ("PO", t_const)], inc="PE")
                        t_x1b_free = tm
                        tev = P.op("act", lambda e, q4=nc_ % 4: e.activation(
                                x1ts[:].rearrange("p (dc n) -> p dc n", n=512)
                                [:, :, q4 * 128:(q4 + 1) * 128],
                                psT3[:], AF.Copy),
                            waits=[("PE", tm), ("Sx1t", t_x1ts_free)],
                            inc="AC")
                        t_psT3_free = tev
                        if nc_ % 4 == 3:
                            tile = nc_ // 4
                            tss = 0
                            for dc in range(DC):
                                tss = store(
                                    x1t_h[dc][:, tile * 512:(tile + 1) * 512],
                                    x1ts[:, dc * 512:(dc + 1) * 512],
                                    sem="Sx1t", waits=[("AC", tev)])
                            t_x1t_store = tss
                            t_x1ts_free = tss
                            trr = 0
                            for dc in range(DC):
                                trr = P.op("dve", lambda e, dc=dc:
                                    e.reduce_sum(
                                        rsum8[:, dc:dc + 1],
                                        x1ts[:, dc * 512:(dc + 1) * 512],
                                        axis=AX.X),
                                    waits=[("AC", tev), ("VE", t_rsum_free)],
                                    inc="VE")
                            t_rsum_free = P.op("dve", lambda e: e.tensor_add(
                                    xsacc[:], xsacc[:], rsum8[:]),
                                waits=[("VE", trr), ("PO", t_const)],
                                inc="VE")

                    t_xs = t_rsum_free
                    tm = 0
                    for dc in range(DC):
                        tm = P.op("pe", lambda e, dc=dc: e.matmul(
                                RT[0:1, 0:8], xsacc[:, dc:dc + 1],
                                wr_sb[:, dc * 8:(dc + 1) * 8],
                                start=(dc == 0), stop=(dc == DC - 1)),
                            waits=[("VE", t_xs), ("Lsm", t_small),
                                   ("AC", t_bias)], inc="PE")
                    tz0 = P.op("act", lambda e: e.activation(
                            r_z[:], RT[0:1, 0:8], AF.Copy, scale=1.0 / S),
                        waits=[("PE", tm)], inc="AC")
                    tz = P.op("dve", lambda e: e.tensor_add(
                            r_z[:], r_z[:], br_row[:]),
                        waits=[("AC", tz0), ("Lsm", t_small)], inc="VE")
                    tmx = P.op("dve", lambda e: e.reduce_max(
                            r_m1[:], r_z[:], axis=AX.X),
                        waits=[("VE", tz)], inc="VE")
                    tmn = P.op("dve", lambda e: e.tensor_scalar_mul(
                            r_m2n[:], r_m1[:], -1.0),
                        waits=[("VE", tmx)], inc="VE")
                    tex = P.op("act", lambda e: e.activation(
                            r_rw[:], r_z[:], AF.Exp, bias=r_m2n[:]),
                        waits=[("VE", tmn)], inc="AC")
                    tsm = P.op("dve", lambda e: e.reduce_sum(
                            r_s[:], r_rw[:], axis=AX.X),
                        waits=[("AC", tex)], inc="VE")
                    tsm = P.op("dve", lambda e: e.reciprocal(
                            r_s[:], r_s[:]),
                        waits=[("VE", tsm)], inc="VE")
                    trw = P.op("dve", lambda e: e.tensor_scalar_mul(
                            r_rw[:], r_rw[:], r_s[:]),
                        waits=[("VE", tsm)], inc="VE")
                    tm1 = P.op("dve", lambda e: e.reduce_max(
                            r_m1[:], r_rw[:], axis=AX.X),
                        waits=[("VE", trw)], inc="VE")
                    tk1 = P.op("dve", lambda e: e.tensor_single_scalar(
                            r_mask1[:], r_rw[:], r_m1[:], ALU.is_equal),
                        waits=[("VE", tm1)], inc="VE")
                    tr2 = P.op("dve", lambda e: e.scalar_tensor_tensor(
                            r_rw2[:], r_mask1[:], -1e30, r_rw[:],
                            ALU.mult, ALU.add),
                        waits=[("VE", tk1)], inc="VE")
                    tm2 = P.op("dve", lambda e: e.reduce_max(
                            r_m2[:], r_rw2[:], axis=AX.X),
                        waits=[("VE", tr2)], inc="VE")
                    tk2 = P.op("dve", lambda e: e.tensor_single_scalar(
                            r_mask2[:], r_rw2[:], r_m2[:], ALU.is_equal),
                        waits=[("VE", tm2)], inc="VE")
                    tmn2 = P.op("dve", lambda e: e.tensor_scalar_mul(
                            r_m2n[:], r_m2[:], -1.0),
                        waits=[("VE", tm2)], inc="VE")
                    tw1 = P.op("act", lambda e: e.activation(
                            r_w1[:], r_m1[:], AF.Sigmoid, bias=r_m2n[:]),
                        waits=[("VE", tmn2), ("VE", tk1)], inc="AC")
                    td12 = P.op("dve", lambda e: e.tensor_sub(
                            r_d12[:], r_mask1[:], r_mask2[:]),
                        waits=[("VE", tk2)], inc="VE")
                    tcw = P.op("dve", lambda e: e.scalar_tensor_tensor(
                            cw_row[:], r_d12[:], r_w1[:], r_mask2[:],
                            ALU.mult, ALU.add),
                        waits=[("VE", td12), ("AC", tw1)], inc="VE")
                    tm = P.op("pe", lambda e: e.matmul(
                            RT[:, 0:8], ones128[:], cw_row[:],
                            start=True, stop=True),
                        waits=[("VE", tcw), ("AC", tz0)], inc="PE")
                    t_cwb = P.op("act", lambda e: e.activation(
                            cwb[:], RT[:, 0:8], AF.Copy),
                        waits=[("PE", tm)], inc="AC")
                    tm = P.op("pe", lambda e: e.matmul(
                            RT[0:8, 8:9], cw_row[:], ones128[0:1, 0:1],
                            start=True, stop=True),
                        waits=[("VE", tcw), ("AC", t_cwb)], inc="PE")
                    tct = P.op("act", lambda e: e.activation(
                            cwT[:], RT[0:8, 8:9], AF.Copy),
                        waits=[("PE", tm)], inc="AC")
                    tprev = tct
                    for hf in range(2):
                        tmb = P.op("pe", lambda e, hf=hf: e.matmul(
                                RT[0:1, 0:512], cwT[:],
                                b2_sb[:, hf * 512:(hf + 1) * 512],
                                start=True, stop=True),
                            waits=[("AC", tprev), ("Lb", t_b12)], inc="PE")
                        tprev = P.op("act", lambda e, hf=hf: e.activation(
                                b2c_row[:, hf * 512:(hf + 1) * 512],
                                RT[0:1, 0:512], AF.Copy),
                            waits=[("PE", tmb)], inc="AC")
                    for hf in range(2):
                        tmb = P.op("pe", lambda e, hf=hf: e.matmul(
                                RT[:, 0:512], ones128[:],
                                b2c_row[:, hf * 512:(hf + 1) * 512],
                                start=True, stop=True),
                            waits=[("AC", tprev)], inc="PE")
                        tprev = P.op("act", lambda e, hf=hf: e.activation(
                                b2cb[:, hf * 512:(hf + 1) * 512],
                                RT[:, 0:512], AF.Copy),
                            waits=[("PE", tmb)], inc="AC")
                    tm = tprev
                    t_b2cb = tm
                    if SPARSE:
                        tt1 = P.op("dve", lambda e: e.tensor_mul(
                                r_tmp8[:], io8[:], r_mask1[:]),
                            waits=[("PO", t_const), ("VE", tk1)], inc="VE")
                        tt1 = P.op("dve", lambda e: e.reduce_sum(
                                r_ef[:, 0:1], r_tmp8[:], axis=AX.X),
                            waits=[("VE", tt1)], inc="VE")
                        tt2 = P.op("dve", lambda e: e.tensor_mul(
                                r_tmp8[:], io8[:], r_mask2[:]),
                            waits=[("VE", tt1), ("VE", tk2)], inc="VE")
                        tt2 = P.op("dve", lambda e: e.reduce_sum(
                                r_ef[:, 1:2], r_tmp8[:], axis=AX.X),
                            waits=[("VE", tt2)], inc="VE")
                        t_idx = P.op("dve", lambda e: e.tensor_copy(
                                idx_sb[:], r_ef[:]),
                            waits=[("VE", tt2)], inc="VE")
                        tc1 = P.op("dve", lambda e: e.tensor_copy(
                                r_cws[:, 0:1], r_w1[:]),
                            waits=[("AC", tw1)], inc="VE")
                        tc1 = P.op("dve", lambda e: e.tensor_scalar(
                                r_cws[:, 1:2], r_w1[:], -1.0, 1.0,
                                ALU.mult, ALU.add),
                            waits=[("VE", tc1)], inc="VE")
                        tm = P.op("pe", lambda e: e.matmul(
                                RT[:, 0:2], ones128[:], r_cws[:],
                                start=True, stop=True),
                            waits=[("VE", tc1), ("AC", t_b2cb)], inc="PE")
                        t_cwsel = P.op("act", lambda e: e.activation(
                                cwsel[:], RT[:, 0:2], AF.Copy),
                            waits=[("PE", tm)], inc="AC")
                        t_b1g = 0
                        with nc.gpsimd.register("rg0") as rg0:
                            for k in range(2):
                                def gfn(e, k=k, rg0=rg0):
                                    e.reg_load(rg0, idx_sb[0:1, k:k + 1])
                                    e.reg_mul(rg0, rg0, FF)
                                    return e.dma_start(
                                        out=b1pair[k:k + 1, :],
                                        in_=bass.AP(b1_h.tensor, rg0,
                                                    [[FF, 1], [1, FF]]))
                                t_b1g = P.op("pool", gfn,
                                    waits=[("VE", t_idx), ("Lb", t_b12)],
                                    inc="GB", incn=16)
                            pass
                        t_dsel = {}
                        with nc.gpsimd.register("rg1") as rg1:
                            for gi, (mat, k) in enumerate(
                                    ((0, 0), (0, 1), (1, 0), (1, 1))):
                                srcten = (w1_h if mat == 0 else w2_h).tensor
                                dstten = (wf1_h if mat == 0 else wf2_h).tensor
                                tt = 0
                                for q in range(4):
                                    def cfn(e, k=k, q=q, srcten=srcten,
                                            dstten=dstten, rg1=rg1):
                                        if q == 0:
                                            e.reg_load(rg1,
                                                       idx_sb[0:1, k:k + 1])
                                            e.reg_mul(rg1, rg1, D * FF)
                                        else:
                                            e.reg_add(rg1, rg1, 1048576)
                                        return e.dma_start(
                                            out=bass.AP(
                                                dstten,
                                                k * D * FF + q * 1048576,
                                                [[1024, 1024], [1, 1024]]),
                                            in_=bass.AP(
                                                srcten, rg1,
                                                [[1024, 1024], [1, 1024]]))
                                    tt = P.op("pool", cfn,
                                        waits=[("VE", t_idx)],
                                        inc=f"DS{gi}", incn=16)
                                t_dsel[(mat, k)] = tt
                        sc_jobs = []
                        for k in range(2):
                            for j in range(32):
                                sc_jobs.append(
                                    (("DS0" if k == 0 else "DS1"),
                                     t_dsel[(0, k)],
                                     bass.AP(wf1_h.tensor,
                                             k * D * FF + j * 131072,
                                             [[1024, 128], [1, 1024]]),
                                     bass.AP(w1sel_h.tensor,
                                             k * D * FF + j * 131072,
                                             [[1024, 128], [1, 1024]])))
                        for k in range(2):
                            for j in range(32):
                                sc_jobs.append(
                                    (("DS2" if k == 0 else "DS3"),
                                     t_dsel[(1, k)],
                                     bass.AP(wf2_h.tensor,
                                             k * FF * D + j * 131072,
                                             [[1024, 128], [1, 1024]]),
                                     bass.AP(w2sel_h.tensor,
                                             k * FF * D + j * 131072,
                                             [[1024, 128], [1, 1024]])))
                        vfree = [0, 0]
                        sfree = [0, 0]
                        t_w1sel = None
                        t_w2sel = None
                        for j, (dsem, dtok, csrc, cdst) in enumerate(sc_jobs):
                            i = j % 2
                            tl = P.op("pool", lambda e, s=csrc, i=i:
                                e.dma_start(out=scf[i][:], in_=s),
                                waits=[(dsem, dtok), ("VE", vfree[i])],
                                inc=("GCl" if i == 0 else "GCl2"), incn=16)
                            tc = P.op("dve", lambda e, i=i:
                                e.tensor_copy(scb[i][:], scf[i][:]),
                                waits=[(("GCl" if i == 0 else "GCl2"), tl),
                                       (("GS" if i == 0 else "GS2"),
                                        sfree[i])], inc="VE")
                            vfree[i] = tc
                            ts_ = P.op("pool", lambda e, d=cdst, i=i:
                                e.dma_start(out=d, in_=scb[i][:]),
                                waits=[("VE", tc)],
                                inc=("GS" if i == 0 else "GS2"), incn=16)
                            sfree[i] = ts_
                            if j == 2 * FG * DC - 1:
                                t_w1sel = (P.cnt["GS"], P.cnt["GS2"])
                        t_w2sel = (P.cnt["GS"], P.cnt["GS2"])

                        t_b1Ts = t_cwsel
                        for g16 in range(2):
                            tmb = 0
                            for j in range(16):
                                fc = g16 * 16 + j
                                tmb = P.op("pe", lambda e, fc=fc, j=j:
                                    e.transpose(
                                        RT[:, j * 2:(j + 1) * 2],
                                        b1pair[:, fc * 128:(fc + 1) * 128],
                                        ident_f[0:2, 0:2]),
                                    waits=[("GB", t_b1g), ("PO", t_const),
                                           ("AC", t_b1Ts)], inc="PE")
                            t_b1Ts = P.op("act", lambda e, g16=g16:
                                e.activation(
                                    b1Tsel[:, g16 * 32:(g16 + 1) * 32],
                                    RT[:, 0:32], AF.Copy),
                                waits=[("PE", tmb)], inc="AC")

        P.barrier()
        with ExitStack() as C4:
            w1r = sb(C4, "w1r", [128, 17 * 1024], BF16)
            w2r = sb(C4, "w2r", [128, 8 * 1024], BF16)
            x1tt = [sb(C4, f"x1tt{i}", [128, DC * 512], BF16)
                    for i in range(2)]
            h_sb = sb(C4, "h_sb", [128, FC * 512], BF16)
            oacc = sb(C4, "oacc", [128, 4 * 1024], F32)
            x1ft = [sb(C4, f"x1ft{i}", [128, 4 * 1024], F32)
                    for i in range(2)]
            gam_b = sb(C4, "gam_b", [128, D], F32)
            bet_b = sb(C4, "bet_b", [128, D], F32)
            ep_t = sb(C4, "ep_t", [128, 1024], F32)
            ep_o = [sb(C4, f"ep_o{i}", [128, 1024], F32) for i in range(2)]
            st4 = sb(C4, "st4", [128, 16], F32)
            rs4 = sb(C4, "rs4", [128, 2], F32)
            with ExitStack() as PS4:
                PO4 = [ps(PS4, f"PO4_{i}", [128, 1024], F32) for i in range(4)]

                t_gb = 0
                for row, dstb in ((gam_row, gam_b), (bet_row, bet_b)):
                    for hf in range(2):
                        tmb = P.op("pe", lambda e, r=row, hf=hf: e.matmul(
                                PO4[3][:, 0:512], ones128[:],
                                r[:, hf * 512:(hf + 1) * 512],
                                start=True, stop=True),
                            waits=[("Lsm", t_small), ("PO", t_const),
                                   ("AC", t_gb)], inc="PE")
                        t_gb = P.op("act", lambda e, d=dstb, hf=hf:
                            e.activation(
                                d[:, hf * 512:(hf + 1) * 512],
                                PO4[3][:, 0:512], AF.Copy),
                            waits=[("PE", tmb)], inc="AC")

                w1_ready = [0] * 17
                w1_free = [0] * 17
                w2_ready = [0] * 8
                w2_free = [0] * 8
                w1_ctr = 0
                w2_ctr = 0
                t_x1tt = [0, 0]
                t_x1tt_free = [0, 0]
                t_x1ft = [0, 0]
                t_x1ft_free = [0, 0]
                t_h_free = 0
                t_PO4h_free = [0, 0]
                t_po40_evac = [0, 0]
                t_oacc_evac = 0
                t_oacc_read = 0
                t_out_free = [0, 0]
                t_st4_free = 0
                gelu_tok = {}

                for tile in range(NT):
                    ts_ = tile % 2
                    tld = 0
                    for dc in range(DC):
                        tld = load(x1tt[ts_][:, dc * 512:(dc + 1) * 512],
                                   x1t_h[dc][:, tile * 512:(tile + 1) * 512],
                                   sem=f"Lxt{ts_}",
                                   waits=[("Sx1t", t_x1t_store),
                                          ("PE", t_x1tt_free[ts_])])
                    t_x1tt[ts_] = tld
                    t_x1ft[ts_] = load(
                        x1ft[ts_][:].rearrange("p (s d) -> p s d", d=D),
                        x1f_h.rearrange("(t s p) d -> t p s d",
                                        p=128, s=4)[tile],
                        sem=f"Lxf{ts_}",
                        waits=[("Sx1f", t_x1f_store),
                               ("VE", t_x1ft_free[ts_])])
                    tinit = 0
                    for ns in range(4):
                        tinit = P.op("dve", lambda e, ns=ns, ts_=ts_:
                            e.tensor_add(
                                oacc[:, ns * 1024:(ns + 1) * 1024],
                                x1ft[ts_][:, ns * 1024:(ns + 1) * 1024],
                                b2cb[:]),
                            waits=[(f"Lxf{ts_}", t_x1ft[ts_]),
                                   ("AC", t_b2cb),
                                   ("VE", t_oacc_evac),
                                   ("VE", t_oacc_read)], inc="VE")
                    t_oacc_init = tinit

                    for e_ in (range(2) if SPARSE else range(E)):
                        for fg in range(FG):
                            gsem = ["Lw1a", "Lw1b", "Lw1c", "Lw1d"][(e_ * FG + fg) % 4]
                            grp_tok = 0
                            for dc in range(DC):
                                slot = w1_ctr % 17
                                w1src = (w1sel_h[e_, dc * 128:(dc + 1) * 128,
                                                 fg * 1024:(fg + 1) * 1024]
                                         if SPARSE
                                         else w1s_h[e_, fg, dc])
                                wws = ([("GS", t_w1sel[0]),
                                        ("GS2", t_w1sel[1])] if SPARSE else
                                       [("GCs0", cast_done[e_][0]),
                                        ("GCs1", cast_done[e_][1])])
                                grp_tok = load(
                                    w1r[:, slot * 1024:(slot + 1) * 1024],
                                    w1src,
                                    sem=gsem,
                                    waits=wws + [("PE", w1_free[slot])])
                                w1_ctr += 1
                            base_slot = (w1_ctr - DC) % 17
                            for fcl in range(8):
                                fc = fg * 8 + fcl
                                hp = fc % 2
                                tm = 0
                                for dc in range(DC):
                                    slot = (base_slot + dc) % 17
                                    tm = P.op("pe", lambda e, slot=slot,
                                              fcl=fcl, dc=dc, hp=hp,
                                              ts_=ts_: e.matmul(
                                            PO4[0][:, hp * 512:
                                                   (hp + 1) * 512],
                                            w1r[:, slot * 1024 + fcl * 128:
                                                slot * 1024 +
                                                (fcl + 1) * 128],
                                            x1tt[ts_][:, dc * 512:
                                                      (dc + 1) * 512],
                                            start=(dc == 0),
                                            stop=(dc == DC - 1)),
                                        waits=[(gsem, grp_tok),
                                               (f"Lxt{ts_}", t_x1tt[ts_]),
                                               ("AC", t_PO4h_free[hp]),
                                               ("VE", t_po40_evac[hp]),
                                               ("PE", t_h_free)], inc="PE")
                                    if fcl == 7:
                                        w1_free[slot] = tm
                                tg = P.op("act", lambda e, fc=fc, hp=hp,
                                          e_=e_: e.activation(
                                        h_sb[:, fc * 512:(fc + 1) * 512],
                                        PO4[0][:, hp * 512:(hp + 1) * 512],
                                        AF.Gelu,
                                        bias=(b1Tsel[:, fc * 2 + e_:
                                                     fc * 2 + e_ + 1]
                                              if SPARSE else
                                              b1T[:, fc * E + e_:
                                                  fc * E + e_ + 1])),
                                    waits=[("PE", tm),
                                           ("AC", t_b1Ts if SPARSE
                                            else t_b1T)],
                                    inc="AC")
                                t_PO4h_free[hp] = tg
                                gelu_tok[fc] = tg
                        for fc in range(FC):
                            if fc % 4 == 0:
                                g2sem = ("Lw2a" if (w2_ctr // 4) % 2 == 0
                                         else "Lw2b")
                                g2tok = 0
                                for k in range(4):
                                    slot = w2_ctr % 8
                                    w2src = (w2sel_h[e_,
                                                     (fc + k) * 128:
                                                     (fc + k + 1) * 128, :]
                                             if SPARSE
                                             else w2s_h[e_, fc + k])
                                    wws = ([("GS", t_w2sel[0]),
                                            ("GS2", t_w2sel[1])] if SPARSE
                                           else
                                           [("GCs0", cast_done[e_][0]),
                                            ("GCs1", cast_done[e_][1])])
                                    g2tok = load(
                                        w2r[:, slot * 1024:(slot + 1) * 1024],
                                        w2src,
                                        sem=g2sem,
                                        waits=wws + [("PE", w2_free[slot])])
                                    w2_ctr += 1
                            slot = fc % 8
                            tm = 0
                            for ns in range(4):
                                for ds in range(2):
                                    tm = P.op("pe", lambda e, fc=fc, ns=ns,
                                              ds=ds, slot=slot: e.matmul(
                                            PO4[ns][:, ds * 512:
                                                    (ds + 1) * 512],
                                            h_sb[:, fc * 512 + ns * 128:
                                                 fc * 512 + (ns + 1) * 128],
                                            w2r[:, slot * 1024 + ds * 512:
                                                slot * 1024 +
                                                (ds + 1) * 512],
                                            start=(fc == 0),
                                            stop=(fc == FC - 1)),
                                        waits=[(g2sem, g2tok),
                                               ("AC", gelu_tok[fc]),
                                               ("AC", t_gb if fc == 0 else 0),
                                               ("AC", max(t_PO4h_free)
                                                if fc == 0 else 0),
                                               ("VE", t_oacc_evac
                                                if fc == 0 else 0)],
                                        inc="PE")
                            w2_free[slot] = tm
                            tlast_w2 = tm
                        t_h_free = tlast_w2
                        tev = 0
                        for ns in range(4):
                            for ds in range(2):
                                tev = P.op("dve", lambda e, ns=ns, ds=ds,
                                           e_=e_: e.scalar_tensor_tensor(
                                        oacc[:, ns * 1024 + ds * 512:
                                             ns * 1024 + (ds + 1) * 512],
                                        PO4[ns][:, ds * 512:(ds + 1) * 512],
                                        (cwsel[:, e_:e_ + 1] if SPARSE
                                         else cwb[:, e_:e_ + 1]),
                                        oacc[:, ns * 1024 + ds * 512:
                                             ns * 1024 + (ds + 1) * 512],
                                        ALU.mult, ALU.add),
                                    waits=[("PE", tlast_w2),
                                           ("AC", t_cwsel if SPARSE
                                            else t_cwb),
                                           ("VE", t_oacc_init)], inc="VE")
                                if ns == 0:
                                    t_po40_evac[ds] = tev
                        t_oacc_evac = tev

                    for ns in range(4):
                        os_ = ns % 2
                        tst_ = P.op("dve", lambda e, ns=ns: e.bn_stats(
                                st4[:, 0:6], oacc[:, ns * 1024:
                                                  ns * 1024 + 512]),
                            waits=[("VE", t_oacc_evac), ("VE", t_st4_free)],
                            inc="VE")
                        tst_ = P.op("dve", lambda e, ns=ns: e.bn_stats(
                                st4[:, 6:12], oacc[:, ns * 1024 + 512:
                                                   (ns + 1) * 1024]),
                            waits=[("VE", tst_)], inc="VE")
                        tst_ = P.op("dve", lambda e: e.bn_aggr(
                                st4[:, 12:14], st4[:, 0:12]),
                            waits=[("VE", tst_)], inc="VE")
                        tsr = P.op("act", lambda e: e.activation(
                                st4[:, 14:15], st4[:, 13:14], AF.Sqrt,
                                bias=epsv[:]),
                            waits=[("VE", tst_)], inc="AC")
                        trc = P.op("dve", lambda e: e.reciprocal(
                                rs4[:, 0:1], st4[:, 14:15]),
                            waits=[("AC", tsr)], inc="VE")
                        tt = P.op("dve", lambda e, ns=ns: e.tensor_scalar(
                                ep_t[:], oacc[:, ns * 1024:(ns + 1) * 1024],
                                st4[:, 12:13], rs4[:, 0:1],
                                ALU.subtract, ALU.mult),
                            waits=[("VE", trc)], inc="VE")
                        t_oacc_read = tt
                        tt = P.op("dve", lambda e: e.tensor_mul(
                                ep_t[:], ep_t[:], gam_b[:]),
                            waits=[("VE", tt)], inc="VE")
                        tt = P.op("dve", lambda e: e.tensor_add(
                                ep_t[:], ep_t[:], bet_b[:]),
                            waits=[("VE", tt)], inc="VE")
                        tt = P.op("dve", lambda e, ns=ns, ts_=ts_:
                            e.tensor_add(
                                ep_t[:], ep_t[:],
                                x1ft[ts_][:, ns * 1024:(ns + 1) * 1024]),
                            waits=[("VE", tt)], inc="VE")
                        if ns == 3:
                            t_x1ft_free[ts_] = tt
                        tst_ = P.op("dve", lambda e: e.bn_stats(
                                st4[:, 0:6], ep_t[:, 0:512]),
                            waits=[("VE", tt)], inc="VE")
                        tst_ = P.op("dve", lambda e: e.bn_stats(
                                st4[:, 6:12], ep_t[:, 512:1024]),
                            waits=[("VE", tst_)], inc="VE")
                        tst_ = P.op("dve", lambda e: e.bn_aggr(
                                st4[:, 12:14], st4[:, 0:12]),
                            waits=[("VE", tst_)], inc="VE")
                        tsr = P.op("act", lambda e: e.activation(
                                st4[:, 14:15], st4[:, 13:14], AF.Sqrt,
                                bias=epsv[:]),
                            waits=[("VE", tst_)], inc="AC")
                        trc = P.op("dve", lambda e: e.reciprocal(
                                rs4[:, 1:2], st4[:, 14:15]),
                            waits=[("AC", tsr)], inc="VE")
                        t_st4_free = trc
                        tout = P.op("dve", lambda e, os_=os_: e.tensor_scalar(
                                ep_o[os_][:], ep_t[:],
                                st4[:, 12:13], rs4[:, 1:2],
                                ALU.subtract, ALU.mult),
                            waits=[("VE", trc),
                                   (f"ST{os_}", t_out_free[os_])],
                            inc="VE")
                        P.op("pool", lambda e, tile=tile, ns=ns, os_=os_:
                            e.dma_start(
                                out=y_h[tile * 512 + ns * 128:
                                        tile * 512 + (ns + 1) * 128, :],
                                in_=ep_o[os_][:]),
                            waits=[("VE", tout)], inc=f"ST{os_}", incn=16)
                        t_out_free[os_] = P.cnt[f"ST{os_}"]
                    t_x1tt_free[ts_] = tlast_w2

        P.wait("pool", [("ST0", P.cnt["ST0"]), ("ST1", P.cnt["ST1"])])
        P.wait("sync", [("ST0", P.cnt["ST0"]), ("ST1", P.cnt["ST1"])])

        with nc.Block() as block:
            P.run(block)
    return nc


_NC_CACHE = {}


def kernel(**inputs):
    x = np.ascontiguousarray(np.asarray(inputs["L0_attn_x"], dtype=np.float32))
    B, S, _ = x.shape
    if S not in _NC_CACHE:
        _NC_CACHE[S] = build(S=S)
    nc = _NC_CACHE[S]
    w = {
        "wq": np.ascontiguousarray(np.asarray(inputs["L0_attn_Wq"], np.float32)),
        "wk": np.ascontiguousarray(np.asarray(inputs["L0_attn_Wk"], np.float32)),
        "wv": np.ascontiguousarray(np.asarray(inputs["L0_attn_Wv"], np.float32)),
        "wo": np.ascontiguousarray(np.asarray(inputs["L0_attn_Wo"], np.float32)),
        "bo": np.ascontiguousarray(np.asarray(inputs["L0_attn_bo"], np.float32)),
        "w1": np.ascontiguousarray(np.asarray(inputs["L0_ffn_W1"], np.float32)),
        "b1": np.ascontiguousarray(np.asarray(inputs["L0_ffn_b1"], np.float32)),
        "w2": np.ascontiguousarray(np.asarray(inputs["L0_ffn_W2"], np.float32)),
        "b2": np.ascontiguousarray(np.asarray(inputs["L0_ffn_b2"], np.float32)),
        "wr": np.ascontiguousarray(np.asarray(inputs["L0_ffn_Wr"], np.float32)),
        "br": np.ascontiguousarray(np.asarray(inputs["L0_ffn_br"], np.float32)),
        "gamma": np.ascontiguousarray(np.asarray(inputs["L0_ffn_gamma"],
                                                 np.float32)),
        "beta": np.ascontiguousarray(np.asarray(inputs["L0_ffn_beta"],
                                                np.float32)),
    }
    in_maps = [dict(x=np.ascontiguousarray(x[b]), **w) for b in range(B)]
    res = run_bass_kernel_spmd(nc, in_maps, core_ids=list(range(B)))
    return np.stack([np.asarray(r["y"], np.float32) for r in res.results],
                    axis=0)


# version 19
# speedup vs baseline: 1.1979x; 1.0814x over previous
import numpy as np
from contextlib import ExitStack

import concourse.bass as bass
import concourse.mybir as mybir
from concourse.bass_utils import run_bass_kernel_spmd

F32 = mybir.dt.float32
BF16 = mybir.dt.bfloat16
AF = mybir.ActivationFunctionType
ALU = mybir.AluOpType
AX = mybir.AxisListType

D = 1024
H = 16
DH = 64
FF = 4096
E = 8
DC = D // 128
FC = FF // 128
FG = 4
SCALE = DH ** (-0.25)
LN_EPS = 1e-5
SPARSE = True

ENGS = ("pe", "act", "dve", "pool", "sync")
ENGMAP = {"pe": "tensor", "act": "scalar", "dve": "vector",
          "pool": "gpsimd", "sync": "sync"}


class Prog:
    def __init__(self, nc):
        self.nc = nc
        self.Q = {k: [] for k in ENGS}
        self.sems = {}
        self.cnt = {}
        self.lastwait = {k: {} for k in ENGS}

    def add_sem(self, name, handle):
        self.sems[name] = handle
        self.cnt[name] = 0

    def bump(self, s, n=1):
        self.cnt[s] += n
        return self.cnt[s]

    def op(self, eng, fn, waits=(), inc=None, incn=1):
        tok = None
        wl = []
        for s, v in waits:
            if not v:
                continue
            if v > self.lastwait[eng].get(s, 0):
                self.lastwait[eng][s] = v
                wl.append((self.sems[s], v))
        if inc is not None:
            tok = self.bump(inc, incn)
            ih = (self.sems[inc], incn)
        else:
            ih = None

        def f(fn=fn, wl=wl, ih=ih, eng=eng):
            e = getattr(self.nc, ENGMAP[eng])
            for sh, v in wl:
                e.wait_ge(sh, v)
            if fn is not None:
                ins = fn(e)
                if ih is not None:
                    ins.then_inc(ih[0], ih[1])

        self.Q[eng].append(f)
        return tok

    def wait(self, eng, waits):
        self.op(eng, None, waits=waits)

    def barrier(self):
        snap = [("PE", self.cnt["PE"]), ("AC", self.cnt["AC"]),
                ("VE", self.cnt["VE"])]
        for eng, own in (("pe", "PE"), ("act", "AC"), ("dve", "VE")):
            self.wait(eng, [w for w in snap if w[0] != own])
        self.wait("sync", snap)

    def run(self, block):
        @block.tensor
        def _(e):
            for f in self.Q["pe"]:
                f()

        @block.scalar
        def _(e):
            for f in self.Q["act"]:
                f()

        @block.vector
        def _(e):
            for f in self.Q["dve"]:
                f()

        @block.gpsimd
        def _(e):
            for f in self.Q["pool"]:
                f()

        @block.sync
        def _(e):
            for f in self.Q["sync"]:
                f()


def build(S=4096):
    NT = S // 512
    NC = S // 128
    JGRP = NC // 2
    nc = bass.Bass("TRN2", target_bir_lowering=False)

    x_h = nc.dram_tensor("x", [S, D], F32, kind="ExternalInput").ap()
    wq_h = nc.dram_tensor("wq", [D, D], F32, kind="ExternalInput").ap()
    wk_h = nc.dram_tensor("wk", [D, D], F32, kind="ExternalInput").ap()
    wv_h = nc.dram_tensor("wv", [D, D], F32, kind="ExternalInput").ap()
    wo_h = nc.dram_tensor("wo", [D, D], F32, kind="ExternalInput").ap()
    bo_h = nc.dram_tensor("bo", [D], F32, kind="ExternalInput").ap()
    w1_h = nc.dram_tensor("w1", [E, D, FF], F32, kind="ExternalInput").ap()
    b1_h = nc.dram_tensor("b1", [E, FF], F32, kind="ExternalInput").ap()
    w2_h = nc.dram_tensor("w2", [E, FF, D], F32, kind="ExternalInput").ap()
    b2_h = nc.dram_tensor("b2", [E, D], F32, kind="ExternalInput").ap()
    wr_h = nc.dram_tensor("wr", [D, E], F32, kind="ExternalInput").ap()
    br_h = nc.dram_tensor("br", [E], F32, kind="ExternalInput").ap()
    gam_h = nc.dram_tensor("gamma", [D], F32, kind="ExternalInput").ap()
    bet_h = nc.dram_tensor("beta", [D], F32, kind="ExternalInput").ap()
    y_h = nc.dram_tensor("y", [S, D], F32, kind="ExternalOutput").ap()
    qt_h = nc.dram_tensor("qt_s", [D, S], BF16).ap()
    kt_h = nc.dram_tensor("kt_s", [D, S], BF16).ap()
    x1f_h = nc.dram_tensor("x1f_s", [S, D], F32).ap()
    x1t_h = nc.dram_tensor("x1t_s", [DC, 128, S], BF16).ap()
    w1s_h = nc.dram_tensor("w1s_s", [E, FG, DC, 128, 1024], BF16).ap()
    w2s_h = nc.dram_tensor("w2s_s", [E, FC, 128, D], BF16).ap()
    w1sel_h = nc.dram_tensor("w1sel_s", [2, D, FF], BF16).ap()
    w2sel_h = nc.dram_tensor("w2sel_s", [2, FF, D], BF16).ap()
    wf1_h = nc.dram_tensor("wf1_s", [2, D, FF], F32).ap()
    wf2_h = nc.dram_tensor("wf2_s", [2, FF, D], F32).ap()

    P = Prog(nc)
    with ExitStack() as CTX:
        _semnames = ["PE", "AC", "VE", "PO", "Lsm", "Lb", "Lx0", "Lx1",
                     "LW", "Lk", "Lq0", "Lq1", "Sqk0", "Sqk1", "Sx1f",
                     "Sx1t", "Lw1a", "Lw1b", "Lw2a", "Lw2b", "Lxt0", "Lxt1",
                     "Lxf0", "Lxf1", "GCl", "GCs0", "GCs1", "ST0", "ST1", "GS", "GS2", "GB", "Lk1", "Lw1c", "Lw1d", "GCl2", "DS0", "DS1", "DS2", "DS3",
                     "CL0", "CL1", "CL2", "CL3", "CS0", "CS1", "CS2", "CS3"]
        for sname in _semnames:
            P.add_sem(sname, CTX.enter_context(nc.semaphore(sname)))

        def sb(stack, name, shape, dtype=F32):
            return stack.enter_context(nc.sbuf_tensor(name, shape, dtype))

        def ps(stack, name, shape, dtype=F32):
            return stack.enter_context(nc.psum_tensor(name, shape, dtype))

        def load(dst, src, sem="Lsm", waits=()):
            return P.op("sync", lambda e, d=dst, s=src: e.dma_start(out=d, in_=s),
                        waits=waits, inc=sem, incn=16)

        store = load

        ones64 = sb(CTX, "ones64", [1, 64], F32)
        ones128 = sb(CTX, "ones128", [1, 128], F32)
        ident_b = sb(CTX, "ident_b", [128, 128], BF16)
        ident_f = sb(CTX, "ident_f", [128, 128], F32)
        sel = sb(CTX, "sel", [16, 2048], BF16)
        bo_row = sb(CTX, "bo_row", [1, D], F32)
        gam_row = sb(CTX, "gam_row", [1, D], F32)
        bet_row = sb(CTX, "bet_row", [1, D], F32)
        br_row = sb(CTX, "br_row", [1, E], F32)
        wr_sb = sb(CTX, "wr_sb", [128, DC * E], F32)
        b1T = None if SPARSE else sb(CTX, "b1T", [128, E * FC], F32)
        b2cb = sb(CTX, "b2cb", [128, D], F32)
        cwb = sb(CTX, "cwb", [128, E], F32)
        cwT = sb(CTX, "cwT", [8, 1], F32)
        b2c_row = sb(CTX, "b2c_row", [1, D], F32)
        bias_all = sb(CTX, "bias_all", [128, H * NT], F32)
        accq = sb(CTX, "accq", [128, NT * 16], F32)
        acck = sb(CTX, "acck", [128, 16], F32)
        qtbl = sb(CTX, "qtbl", [16, NT], F32)
        kmax2 = sb(CTX, "kmax2", [16, 1], F32)
        mtbl = sb(CTX, "mtbl", [16, NT], BF16)
        xsacc = sb(CTX, "xsacc", [128, E], F32)
        epsv = sb(CTX, "epsv", [128, 1], F32)
        io8 = sb(CTX, "io8", [1, 8], F32)
        r_tmp8 = sb(CTX, "r_tmp8", [1, 8], F32)
        r_ef = sb(CTX, "r_ef", [1, 2], F32)
        idx_sb = sb(CTX, "idx_sb", [1, 2], mybir.dt.int32)
        r_cws = sb(CTX, "r_cws", [1, 2], F32)
        cwsel = sb(CTX, "cwsel", [128, 2], F32)
        b1Tsel = sb(CTX, "b1Tsel", [128, FC * 2], F32)
        r_z = sb(CTX, "r_z", [1, E], F32)
        r_rw = sb(CTX, "r_rw", [1, E], F32)
        r_rw2 = sb(CTX, "r_rw2", [1, E], F32)
        r_m1 = sb(CTX, "r_m1", [1, 1], F32)
        r_m2 = sb(CTX, "r_m2", [1, 1], F32)
        r_m2n = sb(CTX, "r_m2n", [1, 1], F32)
        r_w1 = sb(CTX, "r_w1", [1, 1], F32)
        r_s = sb(CTX, "r_s", [1, 1], F32)
        r_mask1 = sb(CTX, "r_mask1", [1, E], F32)
        r_mask2 = sb(CTX, "r_mask2", [1, E], F32)
        r_d12 = sb(CTX, "r_d12", [1, E], F32)
        cw_row = sb(CTX, "cw_row", [1, E], F32)

        tpo = 0
        for buf, val in ((ones64, 1.0), (ones128, 1.0), (ident_b, 0.0),
                         (ident_f, 0.0), (sel, 1.0), (accq, 0.0), (acck, 0.0),
                         (xsacc, 0.0), (epsv, LN_EPS)):
            tpo = P.op("pool", lambda e, b=buf, v=val: e.memset(b[:], v),
                       inc="PO")
        tmemset = tpo
        tpo = P.op("pool", lambda e: e.affine_select(
            out=ident_b[:], in_=ident_b[:], compare_op=ALU.not_equal,
            fill=1.0, base=0, pattern=[[-1, 128]], channel_multiplier=1),
            waits=[("PO", tmemset)], inc="PO")
        tpo = P.op("pool", lambda e: e.affine_select(
            out=ident_f[:], in_=ident_f[:], compare_op=ALU.not_equal,
            fill=1.0, base=0, pattern=[[-1, 128]], channel_multiplier=1),
            waits=[("PO", tmemset)], inc="PO")
        tpo = P.op("pool", lambda e: e.affine_select(
            out=sel[:], in_=sel[:], compare_op=ALU.is_ge,
            fill=0.0, base=0, pattern=[[1, 2048]], channel_multiplier=-128),
            waits=[("PO", tmemset)], inc="PO")
        tpo = P.op("pool", lambda e: e.affine_select(
            out=sel[:], in_=sel[:], compare_op=ALU.is_ge,
            fill=0.0, base=127, pattern=[[-1, 2048]], channel_multiplier=128),
            waits=[("PO", tpo)], inc="PO")
        tpo = P.op("pool", lambda e: e.iota(
            io8[:], pattern=[[1, 8]], channel_multiplier=0,
            allow_small_or_imprecise_dtypes=True), inc="PO")
        t_const = tpo

        load(bo_row[:], bo_h.rearrange("(o d) -> o d", o=1))
        load(gam_row[:], gam_h.rearrange("(o d) -> o d", o=1))
        load(bet_row[:], bet_h.rearrange("(o d) -> o d", o=1))
        load(br_row[:], br_h.rearrange("(o e) -> o e", o=1))
        t_small = load(wr_sb[:].rearrange("p (c e) -> p c e", e=E),
                       wr_h.rearrange("(c p) e -> p c e", p=128))

        if SPARSE:
            scf = [sb(CTX, f"scf{i}", [128, 512], F32) for i in range(4)]
            scb = [sb(CTX, f"scb{i}", [128, 512], BF16) for i in range(4)]
        else:
            cst_f = [sb(CTX, f"cstf{i}", [128, 512], F32) for i in range(2)]
            cst_b = [sb(CTX, f"cstb{i}", [128, 512], BF16) for i in range(2)]
        cast_jobs = []
        for e_ in range(E):
            for dc in range(DC):
                for fg in range(FG):
                    for hh in range(2):
                        cast_jobs.append((
                            w1_h[e_, dc * 128:(dc + 1) * 128,
                                 fg * 1024 + hh * 512:
                                 fg * 1024 + (hh + 1) * 512],
                            w1s_h[e_, fg, dc][:, hh * 512:(hh + 1) * 512]))
            for fc in range(FC):
                for hh in range(2):
                    cast_jobs.append((
                        w2_h[e_, fc * 128:(fc + 1) * 128,
                             hh * 512:(hh + 1) * 512],
                        w2s_h[e_, fc][:, hh * 512:(hh + 1) * 512]))
        cast_done = {}
        if not SPARSE:
            cast_jobs = []
            for e_ in range(E):
                for dc in range(DC):
                    for fg in range(FG):
                        for hh in range(2):
                            cast_jobs.append((
                                w1_h[e_, dc * 128:(dc + 1) * 128,
                                     fg * 1024 + hh * 512:
                                     fg * 1024 + (hh + 1) * 512],
                                w1s_h[e_, fg, dc][:, hh * 512:(hh + 1) * 512]))
                for fc in range(FC):
                    for hh in range(2):
                        cast_jobs.append((
                            w2_h[e_, fc * 128:(fc + 1) * 128,
                                 hh * 512:(hh + 1) * 512],
                            w2s_h[e_, fc][:, hh * 512:(hh + 1) * 512]))
            cfree = [0, 0]
            csfree = [0, 0]
            per_e = len(cast_jobs) // E
            for j, (csrc, cdst) in enumerate(cast_jobs):
                i = j % 2
                tl = P.op("pool", lambda e, s=csrc, i=i: e.dma_start(
                    out=cst_f[i][:], in_=s), waits=[("PO", cfree[i])],
                    inc="GCl", incn=16)
                tc = P.op("pool", lambda e, i=i: e.tensor_copy(
                    cst_b[i][:], cst_f[i][:]),
                    waits=[("GCl", tl), (f"GCs{i}", csfree[i])], inc="PO")
                cfree[i] = tc
                tst = P.op("pool", lambda e, d=cdst, i=i: e.dma_start(
                    out=d, in_=cst_b[i][:]), waits=[("PO", tc)],
                    inc=f"GCs{i}", incn=16)
                csfree[i] = tst
                if (j + 1) % per_e == 0:
                    cast_done[(j + 1) // per_e - 1] = (
                        P.cnt["GCs0"], P.cnt["GCs1"])

        with ExitStack() as C_AT:
            with ExitStack() as C_V:
                v_sb = sb(C_V, "v_sb", [128, NC * H * 65], BF16)

                with ExitStack() as C1:
                    wq_bf = sb(C1, "wq_bf", [128, DC * 1024], BF16)
                    wk_bf = sb(C1, "wk_bf", [128, DC * 1024], BF16)
                    wv_bf = sb(C1, "wv_bf", [128, DC * 1024], BF16)
                    wstg = sb(C1, "wstg", [128, 2048], F32)
                    xstg = sb(C1, "xstg", [128, 4096], F32)
                    xb = sb(C1, "xb", [128, 4096], BF16)
                    xT = sb(C1, "xT", [128, DC * 512], BF16)
                    qkst = [sb(C1, f"qkst{i}", [128, 512], BF16)
                            for i in range(2)]
                    sq_sb = [sb(C1, f"sq{i}", [128, 512], F32)
                             for i in range(2)]
                    nrm8 = [sb(C1, f"nrm8{i}", [128, 8], F32)
                            for i in range(2)]

                    with ExitStack() as PS1:
                        psT = [ps(PS1, f"psT{i}", [128, 1024], BF16)
                               for i in range(2)]
                        ps_q = ps(PS1, "ps_q", [128, 512], F32)
                        ps_k = ps(PS1, "ps_k", [128, 512], F32)
                        ps_v = ps(PS1, "ps_v", [128, 1024], F32)
                        ps_n = ps(PS1, "ps_n", [128, 1024], F32)

                        t_psq_free = 0
                        t_psn_free = [0, 0]
                        t_wst_free = 0
                        wtoks = []
                        for wh, wbf in ((wq_h, wq_bf), (wk_h, wk_bf),
                                        (wv_h, wv_bf)):
                            tdone = 0
                            for half in range(4):
                                tl = load(wstg[:].rearrange(
                                              "p (c d) -> p c d", d=1024),
                                          wh.rearrange("(c p) d -> p c d",
                                                       p=128)
                                          [:, half * 2:(half + 1) * 2, :],
                                          sem="LW",
                                          waits=[("VE", t_wst_free)])
                                tdone = P.op("dve", lambda e, w=wbf,
                                             half=half: e.tensor_copy(
                                        w[:, half * 2048:(half + 1) * 2048],
                                        wstg[:]),
                                    waits=[("LW", tl)], inc="VE")
                                t_wst_free = tdone
                            wtoks.append(tdone)
                        t_wq, t_wk, t_wv = wtoks

                        t_vones = P.op("dve", lambda e: e.memset(
                            v_sb[:].rearrange("p (c u) -> p c u", u=65)
                            [:, :, 64:65], 1.0), inc="VE")

                        t_xstg_free = 0
                        t_xb_free = 0
                        t_psT_free = [0, 0]
                        t_psk_free = 0
                        t_psv_free = [0, 0]
                        t_qkst_free = [0, 0]
                        t_sq_free = [0, 0]
                        t_nrm_free = [0, 0]
                        qk_slot = 0
                        for nt in range(NT):
                            tl = load(xstg[:].rearrange(
                                          "p (s d) -> p s d", d=D),
                                      x_h.rearrange("(t s p) d -> t p s d",
                                                    p=128, s=4)[nt],
                                      sem="Lx0",
                                      waits=[("VE", t_xstg_free)])
                            t_cast = P.op("dve", lambda e: e.tensor_copy(
                                xb[:], xstg[:]),
                                waits=[("Lx0", tl), ("PE", t_xb_free)],
                                inc="VE")
                            t_xstg_free = t_cast
                            t_xT_evac = [0, 0, 0, 0]
                            tm = 0
                            for ns in range(4):
                                for dc in range(DC):
                                    tm = P.op("pe", lambda e, ns=ns, dc=dc:
                                        e.transpose(
                                            psT[ns % 2][:, dc * 128:
                                                        (dc + 1) * 128],
                                            xb[:, ns * 1024 + dc * 128:
                                               ns * 1024 + (dc + 1) * 128],
                                            ident_b[:]),
                                        waits=[("VE", t_cast),
                                               ("PO", t_const),
                                               ("AC", t_psT_free[ns % 2])],
                                        inc="PE")
                                t_xT_evac[ns] = P.op("act", lambda e, ns=ns:
                                    e.activation(
                                        xT[:].rearrange("p (dc n) -> p dc n",
                                                        n=512)
                                        [:, :, ns * 128:(ns + 1) * 128],
                                        psT[ns % 2][:], AF.Copy),
                                    waits=[("PE", tm)], inc="AC")
                                t_psT_free[ns % 2] = t_xT_evac[ns]
                            t_xb_free = tm
                            t_xT = t_xT_evac[3]

                            for dpc in range(DC):
                                for wbf, twc, psq, which in (
                                        (wq_bf, t_wq, ps_q, "q"),
                                        (wk_bf, t_wk, ps_k, "k")):
                                    pfree = (t_psq_free if which == "q"
                                             else t_psk_free)
                                    tm = 0
                                    for dc in range(DC):
                                        tm = P.op("pe", lambda e, w=wbf,
                                                  dc=dc, dpc=dpc, p_=psq:
                                            e.matmul(
                                                p_[:],
                                                w[:, dc * 1024 + dpc * 128:
                                                  dc * 1024 + (dpc + 1) * 128],
                                                xT[:, dc * 512:(dc + 1) * 512],
                                                start=(dc == 0),
                                                stop=(dc == DC - 1)),
                                            waits=[("AC", t_xT), ("VE", twc),
                                                   ("AC", pfree)], inc="PE")
                                    ta = P.op("act", lambda e, slot=qk_slot,
                                              p_=psq: e.activation(
                                            qkst[slot][:], p_[:], AF.Copy,
                                            scale=SCALE),
                                        waits=[("PE", tm),
                                               (f"Sqk{qk_slot}",
                                                t_qkst_free[qk_slot])],
                                        inc="AC")
                                    if which == "q":
                                        t_psq_free = ta
                                    else:
                                        t_psk_free = ta
                                    dsth = qt_h if which == "q" else kt_h
                                    tst = store(
                                        dsth[dpc * 128:(dpc + 1) * 128,
                                             nt * 512:(nt + 1) * 512],
                                        qkst[qk_slot][:],
                                        sem=f"Sqk{qk_slot}",
                                        waits=[("AC", ta)])
                                    t_qkst_free[qk_slot] = tst
                                    qk_slot = (qk_slot + 1) % 2

                            for ns in range(4):
                                for ds in range(2):
                                    half = ds
                                    tm = 0
                                    for dc in range(DC):
                                        tm = P.op("pe", lambda e, ns=ns,
                                                  ds=ds, dc=dc, half=half:
                                            e.matmul(
                                                ps_v[:, half * 512:
                                                     (half + 1) * 512],
                                                xT[:, dc * 512 + ns * 128:
                                                   dc * 512 + (ns + 1) * 128],
                                                wv_bf[:, dc * 1024 + ds * 512:
                                                      dc * 1024 +
                                                      (ds + 1) * 512],
                                                start=(dc == 0),
                                                stop=(dc == DC - 1)),
                                            waits=[("AC", t_xT), ("VE", t_wv),
                                                   ("AC", t_psv_free[half]),
                                                   ("VE", t_vones)],
                                            inc="PE")
                                    base = (nt * 4 + ns) * (H * 65) + ds * 520
                                    t_psv_free[half] = P.op("act",
                                        lambda e, half=half, base=base:
                                        e.activation(
                                            v_sb[:, base:base + 520]
                                            .rearrange("p (h u) -> p h u",
                                                       u=65)[:, :, 0:64],
                                            ps_v[:, half * 512:
                                                 (half + 1) * 512],
                                            AF.Copy),
                                        waits=[("PE", tm)], inc="AC")

                            for wbf, twc, which in ((wq_bf, t_wq, "q"),
                                                    (wk_bf, t_wk, "k")):
                                for ns in range(4):
                                    for ds in range(2):
                                        half = (ns * 2 + ds) % 2
                                        tm = 0
                                        for dc in range(DC):
                                            tm = P.op("pe", lambda e, w=wbf,
                                                      ns=ns, ds=ds, dc=dc,
                                                      half=half: e.matmul(
                                                    ps_n[:, half * 512:
                                                         (half + 1) * 512],
                                                    xT[:, dc * 512 + ns * 128:
                                                       dc * 512 +
                                                       (ns + 1) * 128],
                                                    w[:, dc * 1024 + ds * 512:
                                                      dc * 1024 +
                                                      (ds + 1) * 512],
                                                    start=(dc == 0),
                                                    stop=(dc == DC - 1)),
                                                waits=[("AC", t_xT),
                                                       ("VE", twc),
                                                       ("AC",
                                                        t_psn_free[half])],
                                                inc="PE")
                                        tsq = P.op("act", lambda e, half=half:
                                            e.activation(
                                                sq_sb[half][:],
                                                ps_n[:, half * 512:
                                                     (half + 1) * 512],
                                                AF.Square, scale=SCALE),
                                            waits=[("PE", tm),
                                                   ("VE", t_sq_free[half])],
                                            inc="AC")
                                        t_psn_free[half] = tsq
                                        tr = 0
                                        for hh in range(8):
                                            tr = P.op("dve", lambda e,
                                                      half=half, hh=hh:
                                                e.reduce_sum(
                                                    nrm8[half][:, hh:hh + 1],
                                                    sq_sb[half][:, hh * 64:
                                                                (hh + 1) * 64],
                                                    axis=AX.X),
                                                waits=[("AC", tsq),
                                                       ("VE",
                                                        t_nrm_free[half])],
                                                inc="VE")
                                        if which == "q":
                                            a_ = accq[:, nt * 16 + ds * 8:
                                                      nt * 16 + ds * 8 + 8]
                                        else:
                                            a_ = acck[:, ds * 8:ds * 8 + 8]
                                        tmx = P.op("dve", lambda e, a=a_,
                                                   half=half: e.tensor_max(
                                                a, a, nrm8[half][:]),
                                            waits=[("VE", tr),
                                                   ("PO", t_const)],
                                            inc="VE")
                                        t_nrm_free[half] = tmx
                                        t_sq_free[half] = tr

                        t_acc = P.cnt["VE"]
                        tred = 0
                        for qt in range(NT):
                            tm = P.op("pe", lambda e, qt=qt: e.transpose(
                                ps_n[0:16, 0:128],
                                accq[:, qt * 16:(qt + 1) * 16], ident_f[:]),
                                waits=[("VE", t_acc), ("VE", tred),
                                       ("AC", max(t_psn_free))], inc="PE")
                            tred = P.op("dve", lambda e, qt=qt: e.reduce_max(
                                qtbl[0:16, qt:qt + 1], ps_n[0:16, 0:128],
                                axis=AX.X),
                                waits=[("PE", tm)], inc="VE")
                        tm = P.op("pe", lambda e: e.transpose(
                            ps_n[0:16, 0:128], acck[:, 0:16], ident_f[:]),
                            waits=[("VE", t_acc), ("VE", tred)], inc="PE")
                        tkm = P.op("dve", lambda e: e.reduce_max(
                            kmax2[0:16, 0:1], ps_n[0:16, 0:128], axis=AX.X),
                            waits=[("PE", tm)], inc="VE")
                        tq = P.op("dve", lambda e: e.tensor_scalar_mul(
                            qtbl[0:16, :], qtbl[0:16, :], kmax2[0:16, 0:1]),
                            waits=[("VE", tkm)], inc="VE")
                        tsq2 = P.op("act", lambda e: e.activation(
                            mtbl[0:16, :], qtbl[0:16, :], AF.Sqrt),
                            waits=[("VE", tq)], inc="AC")
                        tneg = P.op("dve", lambda e: e.tensor_scalar_mul(
                            mtbl[0:16, :], mtbl[0:16, :], -1.0),
                            waits=[("AC", tsq2)], inc="VE")
                        tm = 0
                        for h in range(H):
                            tm = P.op("pe", lambda e, h=h: e.matmul(
                                ps_n[:, h * NT:(h + 1) * NT],
                                sel[:, h * 128:(h + 1) * 128],
                                mtbl[0:16, :], start=True, stop=True),
                                waits=[("VE", tneg), ("PO", t_const),
                                       ("VE", tkm)], inc="PE")
                        t_bias = P.op("act", lambda e: e.activation(
                            bias_all[:], ps_n[:, 0:H * NT], AF.Copy),
                            waits=[("PE", tm)], inc="AC")

                attnT = C_AT.enter_context(nc.sbuf_tensor("attnT", [128, DC * S], BF16, side="right"))

                P.barrier()
                with ExitStack() as C2:
                    ktsl = [sb(C2, f"ktsl{i}", [128, S], BF16)
                            for i in range(2)]
                    qtsl = [sb(C2, f"qtsl{i}", [128, 512], BF16)
                            for i in range(2)]
                    ptA = sb(C2, "ptA", [128, 3 * 1024], BF16)
                    rp_sb = sb(C2, "rp_sb", [1, 512], F32)
                    nb_sb = sb(C2, "nb_sb", [64, 512], F32)
                    avst = [sb(C2, f"avst{i}", [128, 512], F32)
                            for i in range(2)]
                    rp2 = [sb(C2, f"rp2_{i}", [1, 512], F32)
                           for i in range(2)]
                    with ExitStack() as PS2:
                        SAB = [ps(PS2, f"SAB{i}", [128, 1024], F32)
                               for i in range(3)]
                        AVH = ps(PS2, "AVH", [128, 512], F32)
                        BC = ps(PS2, "BC", [128, 512], F32)

                        t_ktsl_free = 0
                        t_qt_load = [0, 0]
                        t_qtsl_free = [0, 0]
                        t_S_free = [0, 0, 0]
                        t_pt_free = [0, 0, 0]
                        t_AV_read = (0, 0)
                        t_avst_free = [0, 0]
                        t_nb_free = 0
                        t_rp_free = [0, 0]
                        t_BC_free = 0
                        norm_pend = []
                        uu = 0

                        def emit_norm(item):
                            nonlocal t_nb_free, t_BC_free
                            slot, h, nqt, tevp, trp = item
                            tbc = P.op("pe", lambda e, slot=slot:
                                e.matmul(
                                    BC[0:64, :], ones64[:],
                                    rp2[slot][:], start=True, stop=True),
                                waits=[("VE", trp), ("VE", t_BC_free)],
                                inc="PE")
                            t_rp_free[slot] = tbc
                            tnb = P.op("dve", lambda e:
                                e.tensor_copy(nb_sb[:], BC[0:64, :]),
                                waits=[("PE", tbc), ("VE", t_nb_free)],
                                inc="VE")
                            t_BC_free = tnb
                            pb = (h % 2) * 64
                            dst = attnT[pb:pb + 64,
                                        (h // 2) * S + nqt * 512:
                                        (h // 2) * S + (nqt + 1) * 512]
                            tev = P.op("dve", lambda e, slot=slot,
                                       dst=dst: e.tensor_mul(
                                    dst, avst[slot][0:64, :], nb_sb[:]),
                                waits=[("VE", tnb), ("AC", tevp)], inc="VE")
                            t_avst_free[slot] = tev
                            t_nb_free = tev

                        t_qk_f0 = P.cnt["Sqk0"]
                        t_qk_f1 = P.cnt["Sqk1"]
                        t_ktsl_free2 = [0, 0]
                        for h in range(H):
                            ks = h % 2
                            pr = h // 2
                            hi = h % 2
                            t_kt_load = 0
                            for dup in range(2):
                                t_kt_load = load(
                                    ktsl[ks][dup * 64:dup * 64 + 64, :],
                                    kt_h[h * 64:(h + 1) * 64, :],
                                    sem=f"Lk{ks}" if ks else "Lk",
                                    waits=[("Sqk0", t_qk_f0),
                                           ("Sqk1", t_qk_f1),
                                           ("PE", t_ktsl_free2[ks])])
                            for qt in range(NT):
                                qs = qt % 2
                                t_qt_load[qs] = 0
                                for dup in range(2):
                                    t_qt_load[qs] = load(
                                        qtsl[qs][dup * 64:dup * 64 + 64, :],
                                        qt_h[h * 64:(h + 1) * 64,
                                             qt * 512:(qt + 1) * 512],
                                        sem=f"Lq{qs}",
                                        waits=[("Sqk0", t_qk_f0),
                                               ("Sqk1", t_qk_f1),
                                               ("PE", t_qtsl_free[qs])])
                                exp_tok = {}
                                tAV = 0

                                def emit_av(g):
                                    nonlocal tAV
                                    gs = g % 3
                                    eT = exp_tok[g]
                                    for jj in range(2):
                                        first = (g == 0 and jj == 0)
                                        last = (g == JGRP - 1 and jj == 1)
                                        jc = g * 2 + jj
                                        vh = v_sb[:,
                                            jc * (H * 65) + h * 65:
                                            jc * (H * 65) + h * 65 + 65]
                                        tAV = P.op("pe", lambda e, vh=vh,
                                            gs=gs, jj=jj, first=first,
                                            last=last: e.matmul(
                                                AVH[0:65, :], vh,
                                                ptA[:, gs * 1024 +
                                                    jj * 512:
                                                    gs * 1024 +
                                                    (jj + 1) * 512],
                                                start=first, stop=last),
                                            waits=[("AC", eT),
                                                   ("VE", t_vones),
                                                   ("AC", t_AV_read[0]),
                                                   ("VE", t_AV_read[1])],
                                            inc="PE")
                                        t_pt_free[gs] = tAV

                                for g in range(JGRP):
                                    gs = g % 3
                                    ps_ = g % 3
                                    tS = 0
                                    for jj in range(2):
                                        jc = g * 2 + jj
                                        pbase = (jc % 2) * 64
                                        tS = P.op("pe", lambda e, jc=jc,
                                            jj=jj, qs=qs, gs=gs, ks=ks,
                                            pbase=pbase: e.matmul(
                                                SAB[gs][:, jj * 512:
                                                        (jj + 1) * 512],
                                                ktsl[ks][pbase:pbase + 64,
                                                     jc * 128:
                                                     (jc + 1) * 128],
                                                qtsl[qs][pbase:
                                                         pbase + 64, :],
                                                start=True, stop=True,
                                                tile_position=(pbase, 0)),
                                            waits=[(("Lk1" if ks else "Lk"),
                                                    t_kt_load),
                                                   (f"Lq{qs}",
                                                    t_qt_load[qs]),
                                                   ("AC", t_S_free[gs])],
                                            inc="PE")
                                    texp = P.op("act", lambda e, gs=gs,
                                        ps_=ps_, h=h, qt=qt: e.activation(
                                            ptA[:, ps_ * 1024:
                                                (ps_ + 1) * 1024],
                                            SAB[gs][:], AF.Exp,
                                            bias=bias_all[:,
                                                h * NT + qt:
                                                h * NT + qt + 1]),
                                        waits=[("PE", tS), ("AC", t_bias),
                                               ("PE", t_pt_free[ps_])],
                                        inc="AC")
                                    t_S_free[gs] = texp
                                    exp_tok[g] = texp
                                    if g > 0:
                                        emit_av(g - 1)
                                emit_av(JGRP - 1)
                                slot = uu % 2
                                tevp = P.op("act", lambda e, slot=slot:
                                    e.activation(
                                        avst[slot][0:64, :],
                                        AVH[0:64, :], AF.Copy),
                                    waits=[("PE", tAV),
                                           ("VE", t_avst_free[slot])],
                                    inc="AC")
                                trp = P.op("dve", lambda e, slot=slot:
                                    e.reciprocal(rp2[slot][:],
                                                 AVH[64:65, :]),
                                    waits=[("PE", tAV),
                                           ("PE", t_rp_free[slot])],
                                    inc="VE")
                                t_AV_read = (tevp, trp)
                                norm_pend.append((slot, h, qt, tevp, trp))
                                if len(norm_pend) > 1:
                                    emit_norm(norm_pend.pop(0))
                                t_qtsl_free[qs] = tAV
                                uu += 1
                            t_ktsl_free2[ks] = tAV
                        for item in norm_pend:
                            emit_norm(item)
                        t_attnT_ve = P.cnt["VE"]

            P.barrier()
            with ExitStack() as C3:
                wo_bf = sb(C3, "wo_bf", [128, DC * 1024], BF16)
                b1_sb = sb(C3, "b1_sb", [E, FF], F32)
                b2_sb = sb(C3, "b2_sb", [E, D], F32)
                bo_b = sb(C3, "bo_b", [128, D], F32)
                b1pair = sb(C3, "b1pair", [2, FF], F32)
                wstg3 = sb(C3, "wstg3", [128, 2048], F32)
                xt3 = [sb(C3, f"xt3_{i}", [128, 1024], F32) for i in range(2)]
                y3_sb = sb(C3, "y3_sb", [128, 1024], F32)
                x1_sb = sb(C3, "x1_sb", [128, 1024], F32)
                x1b_sb = sb(C3, "x1b_sb", [128, 1024], BF16)
                x1ts = sb(C3, "x1ts", [128, DC * 512], BF16)
                st3 = sb(C3, "st3", [128, 16], F32)
                rs3 = sb(C3, "rs3", [128, 2], F32)
                rsum8 = sb(C3, "rsum8", [128, 8], F32)
                with ExitStack() as PS3:
                    PO3 = ps(PS3, "PO3", [128, 1024], F32)
                    psT3 = ps(PS3, "psT3", [128, 1024], BF16)
                    RT = ps(PS3, "RT", [128, 512], F32)

                    t_b12 = load(b1_sb[:], b1_h, sem="Lb")
                    t_b12 = load(b2_sb[:], b2_h, sem="Lb")
                    t_bc_ac = 0
                    for hf in range(2):
                        tmb = P.op("pe", lambda e, hf=hf: e.matmul(
                                RT[:], ones128[:],
                                bo_row[:, hf * 512:(hf + 1) * 512],
                                start=True, stop=True),
                            waits=[("Lsm", t_small), ("PO", t_const),
                                   ("AC", t_bc_ac)], inc="PE")
                        t_bc_ac = P.op("act", lambda e, hf=hf: e.activation(
                                bo_b[:, hf * 512:(hf + 1) * 512], RT[:],
                                AF.Copy),
                            waits=[("PE", tmb)], inc="AC")
                    t_b1T = t_bc_ac
                    for g8 in (range(FC // 8) if not SPARSE else []):
                        tmb = 0
                        for j in range(8):
                            fc = g8 * 8 + j
                            tmb = P.op("pe", lambda e, fc=fc, j=j:
                                e.transpose(
                                    RT[:, j * 8:(j + 1) * 8],
                                    b1_sb[:, fc * 128:(fc + 1) * 128],
                                    ident_f[0:8, 0:8]),
                                waits=[("Lb", t_b12), ("PO", t_const),
                                       ("AC", t_b1T)], inc="PE")
                        t_b1T = P.op("act", lambda e, g8=g8: e.activation(
                                b1T[:, g8 * 64:(g8 + 1) * 64],
                                RT[:, 0:64], AF.Copy),
                            waits=[("PE", tmb)], inc="AC")

                    t_wo = 0
                    t_wst3 = 0
                    for half in range(4):
                        tl = load(wstg3[:].rearrange("p (c d) -> p c d",
                                                     d=1024),
                                  wo_h.rearrange("(c p) d -> p c d", p=128)
                                  [:, half * 2:(half + 1) * 2, :],
                                  sem="LW",
                                  waits=[("VE", t_wst3)])
                        t_wo = P.op("dve", lambda e, half=half: e.tensor_copy(
                                wo_bf[:, half * 2048:(half + 1) * 2048],
                                wstg3[:]),
                            waits=[("LW", tl)], inc="VE")
                        t_wst3 = t_wo

                    t_xt3_free = [0, 0]
                    t_PO3_free = [0, 0]
                    t_psT3_free = 0
                    t_x1ts_free = 0
                    t_y3_free = 0
                    t_x1_free = 0
                    t_x1b_free = 0
                    t_st3_free = 0
                    t_rsum_free = 0
                    t_x1f_store = 0
                    t_x1t_store = 0
                    for nc_ in range(NC):
                        xs = nc_ % 2
                        tlx = load(xt3[xs][:],
                                   x_h[nc_ * 128:(nc_ + 1) * 128, :],
                                   sem=f"Lx{xs}",
                                   waits=[("VE", t_xt3_free[xs])])
                        tds = [0, 0]
                        for ds in range(2):
                            tm = 0
                            for dc in range(DC):
                                tm = P.op("pe", lambda e, dc=dc, ds=ds,
                                          nc_=nc_: e.matmul(
                                        PO3[:, ds * 512:(ds + 1) * 512],
                                        attnT[:, dc * S + nc_ * 128:
                                              dc * S + (nc_ + 1) * 128],
                                        wo_bf[:, dc * 1024 + ds * 512:
                                              dc * 1024 + (ds + 1) * 512],
                                        start=(dc == 0), stop=(dc == DC - 1)),
                                    waits=[("VE", t_attnT_ve), ("VE", t_wo),
                                           ("VE", t_PO3_free[ds])], inc="PE")
                            tds[ds] = P.op("dve", lambda e, ds=ds, xs=xs:
                                e.tensor_add(
                                    y3_sb[:, ds * 512:(ds + 1) * 512],
                                    PO3[:, ds * 512:(ds + 1) * 512],
                                    xt3[xs][:, ds * 512:(ds + 1) * 512]),
                                waits=[("PE", tm), (f"Lx{xs}", tlx),
                                       ("VE", t_y3_free)], inc="VE")
                            t_PO3_free[ds] = tds[ds]
                        t_xt3_free[xs] = tds[1]
                        tyb = P.op("dve", lambda e: e.tensor_add(
                                y3_sb[:], y3_sb[:], bo_b[:]),
                            waits=[("VE", tds[1]), ("AC", t_bc_ac)], inc="VE")
                        tst = P.op("dve", lambda e: e.bn_stats(
                                st3[:, 0:6], y3_sb[:, 0:512]),
                            waits=[("VE", tyb), ("VE", t_st3_free)], inc="VE")
                        tst = P.op("dve", lambda e: e.bn_stats(
                                st3[:, 6:12], y3_sb[:, 512:1024]),
                            waits=[("VE", tst)], inc="VE")
                        tst = P.op("dve", lambda e: e.bn_aggr(
                                st3[:, 12:14], st3[:, 0:12]),
                            waits=[("VE", tst)], inc="VE")
                        tsr = P.op("act", lambda e: e.activation(
                                st3[:, 14:15], st3[:, 13:14], AF.Sqrt,
                                bias=epsv[:]),
                            waits=[("VE", tst)], inc="AC")
                        trc = P.op("dve", lambda e: e.reciprocal(
                                rs3[:, 0:1], st3[:, 14:15]),
                            waits=[("AC", tsr)], inc="VE")
                        t_st3_free = trc
                        tx1 = P.op("dve", lambda e: e.tensor_scalar(
                                x1_sb[:], y3_sb[:], st3[:, 12:13],
                                rs3[:, 0:1], ALU.subtract, ALU.mult),
                            waits=[("VE", trc), ("Sx1f", t_x1_free)],
                            inc="VE")
                        t_y3_free = tx1
                        t_x1f_store = store(
                            x1f_h[nc_ * 128:(nc_ + 1) * 128, :], x1_sb[:],
                            sem="Sx1f", waits=[("VE", tx1)])
                        t_x1_free = t_x1f_store
                        txb = P.op("dve", lambda e: e.tensor_copy(
                                x1b_sb[:], x1_sb[:]),
                            waits=[("VE", tx1), ("PE", t_x1b_free)], inc="VE")
                        tm = 0
                        for dc in range(DC):
                            tm = P.op("pe", lambda e, dc=dc: e.transpose(
                                    psT3[:, dc * 128:(dc + 1) * 128],
                                    x1b_sb[:, dc * 128:(dc + 1) * 128],
                                    ident_b[:]),
                                waits=[("VE", txb), ("AC", t_psT3_free),
                                       ("PO", t_const)], inc="PE")
                        t_x1b_free = tm
                        tev = P.op("act", lambda e, q4=nc_ % 4: e.activation(
                                x1ts[:].rearrange("p (dc n) -> p dc n", n=512)
                                [:, :, q4 * 128:(q4 + 1) * 128],
                                psT3[:], AF.Copy),
                            waits=[("PE", tm), ("Sx1t", t_x1ts_free)],
                            inc="AC")
                        t_psT3_free = tev
                        if nc_ % 4 == 3:
                            tile = nc_ // 4
                            tss = 0
                            for dc in range(DC):
                                tss = store(
                                    x1t_h[dc][:, tile * 512:(tile + 1) * 512],
                                    x1ts[:, dc * 512:(dc + 1) * 512],
                                    sem="Sx1t", waits=[("AC", tev)])
                            t_x1t_store = tss
                            t_x1ts_free = tss
                            trr = 0
                            for dc in range(DC):
                                trr = P.op("dve", lambda e, dc=dc:
                                    e.reduce_sum(
                                        rsum8[:, dc:dc + 1],
                                        x1ts[:, dc * 512:(dc + 1) * 512],
                                        axis=AX.X),
                                    waits=[("AC", tev), ("VE", t_rsum_free)],
                                    inc="VE")
                            t_rsum_free = P.op("dve", lambda e: e.tensor_add(
                                    xsacc[:], xsacc[:], rsum8[:]),
                                waits=[("VE", trr), ("PO", t_const)],
                                inc="VE")

                    t_xs = t_rsum_free
                    tm = 0
                    for dc in range(DC):
                        tm = P.op("pe", lambda e, dc=dc: e.matmul(
                                RT[0:1, 0:8], xsacc[:, dc:dc + 1],
                                wr_sb[:, dc * 8:(dc + 1) * 8],
                                start=(dc == 0), stop=(dc == DC - 1)),
                            waits=[("VE", t_xs), ("Lsm", t_small),
                                   ("AC", t_bias)], inc="PE")
                    tz0 = P.op("act", lambda e: e.activation(
                            r_z[:], RT[0:1, 0:8], AF.Copy, scale=1.0 / S),
                        waits=[("PE", tm)], inc="AC")
                    tz = P.op("dve", lambda e: e.tensor_add(
                            r_z[:], r_z[:], br_row[:]),
                        waits=[("AC", tz0), ("Lsm", t_small)], inc="VE")
                    tmx = P.op("dve", lambda e: e.reduce_max(
                            r_m1[:], r_z[:], axis=AX.X),
                        waits=[("VE", tz)], inc="VE")
                    tmn = P.op("dve", lambda e: e.tensor_scalar_mul(
                            r_m2n[:], r_m1[:], -1.0),
                        waits=[("VE", tmx)], inc="VE")
                    tex = P.op("act", lambda e: e.activation(
                            r_rw[:], r_z[:], AF.Exp, bias=r_m2n[:]),
                        waits=[("VE", tmn)], inc="AC")
                    tsm = P.op("dve", lambda e: e.reduce_sum(
                            r_s[:], r_rw[:], axis=AX.X),
                        waits=[("AC", tex)], inc="VE")
                    tsm = P.op("dve", lambda e: e.reciprocal(
                            r_s[:], r_s[:]),
                        waits=[("VE", tsm)], inc="VE")
                    trw = P.op("dve", lambda e: e.tensor_scalar_mul(
                            r_rw[:], r_rw[:], r_s[:]),
                        waits=[("VE", tsm)], inc="VE")
                    tm1 = P.op("dve", lambda e: e.reduce_max(
                            r_m1[:], r_rw[:], axis=AX.X),
                        waits=[("VE", trw)], inc="VE")
                    tk1 = P.op("dve", lambda e: e.tensor_single_scalar(
                            r_mask1[:], r_rw[:], r_m1[:], ALU.is_equal),
                        waits=[("VE", tm1)], inc="VE")
                    tr2 = P.op("dve", lambda e: e.scalar_tensor_tensor(
                            r_rw2[:], r_mask1[:], -1e30, r_rw[:],
                            ALU.mult, ALU.add),
                        waits=[("VE", tk1)], inc="VE")
                    tm2 = P.op("dve", lambda e: e.reduce_max(
                            r_m2[:], r_rw2[:], axis=AX.X),
                        waits=[("VE", tr2)], inc="VE")
                    tk2 = P.op("dve", lambda e: e.tensor_single_scalar(
                            r_mask2[:], r_rw2[:], r_m2[:], ALU.is_equal),
                        waits=[("VE", tm2)], inc="VE")
                    tmn2 = P.op("dve", lambda e: e.tensor_scalar_mul(
                            r_m2n[:], r_m2[:], -1.0),
                        waits=[("VE", tm2)], inc="VE")
                    tw1 = P.op("act", lambda e: e.activation(
                            r_w1[:], r_m1[:], AF.Sigmoid, bias=r_m2n[:]),
                        waits=[("VE", tmn2), ("VE", tk1)], inc="AC")
                    td12 = P.op("dve", lambda e: e.tensor_sub(
                            r_d12[:], r_mask1[:], r_mask2[:]),
                        waits=[("VE", tk2)], inc="VE")
                    tcw = P.op("dve", lambda e: e.scalar_tensor_tensor(
                            cw_row[:], r_d12[:], r_w1[:], r_mask2[:],
                            ALU.mult, ALU.add),
                        waits=[("VE", td12), ("AC", tw1)], inc="VE")
                    tm = P.op("pe", lambda e: e.matmul(
                            RT[:, 0:8], ones128[:], cw_row[:],
                            start=True, stop=True),
                        waits=[("VE", tcw), ("AC", tz0)], inc="PE")
                    t_cwb = P.op("act", lambda e: e.activation(
                            cwb[:], RT[:, 0:8], AF.Copy),
                        waits=[("PE", tm)], inc="AC")
                    tm = P.op("pe", lambda e: e.matmul(
                            RT[0:8, 8:9], cw_row[:], ones128[0:1, 0:1],
                            start=True, stop=True),
                        waits=[("VE", tcw), ("AC", t_cwb)], inc="PE")
                    tct = P.op("act", lambda e: e.activation(
                            cwT[:], RT[0:8, 8:9], AF.Copy),
                        waits=[("PE", tm)], inc="AC")
                    tprev = tct
                    for hf in range(2):
                        tmb = P.op("pe", lambda e, hf=hf: e.matmul(
                                RT[0:1, 0:512], cwT[:],
                                b2_sb[:, hf * 512:(hf + 1) * 512],
                                start=True, stop=True),
                            waits=[("AC", tprev), ("Lb", t_b12)], inc="PE")
                        tprev = P.op("act", lambda e, hf=hf: e.activation(
                                b2c_row[:, hf * 512:(hf + 1) * 512],
                                RT[0:1, 0:512], AF.Copy),
                            waits=[("PE", tmb)], inc="AC")
                    for hf in range(2):
                        tmb = P.op("pe", lambda e, hf=hf: e.matmul(
                                RT[:, 0:512], ones128[:],
                                b2c_row[:, hf * 512:(hf + 1) * 512],
                                start=True, stop=True),
                            waits=[("AC", tprev)], inc="PE")
                        tprev = P.op("act", lambda e, hf=hf: e.activation(
                                b2cb[:, hf * 512:(hf + 1) * 512],
                                RT[:, 0:512], AF.Copy),
                            waits=[("PE", tmb)], inc="AC")
                    tm = tprev
                    t_b2cb = tm
                    if SPARSE:
                        tt1 = P.op("dve", lambda e: e.tensor_mul(
                                r_tmp8[:], io8[:], r_mask1[:]),
                            waits=[("PO", t_const), ("VE", tk1)], inc="VE")
                        tt1 = P.op("dve", lambda e: e.reduce_sum(
                                r_ef[:, 0:1], r_tmp8[:], axis=AX.X),
                            waits=[("VE", tt1)], inc="VE")
                        tt2 = P.op("dve", lambda e: e.tensor_mul(
                                r_tmp8[:], io8[:], r_mask2[:]),
                            waits=[("VE", tt1), ("VE", tk2)], inc="VE")
                        tt2 = P.op("dve", lambda e: e.reduce_sum(
                                r_ef[:, 1:2], r_tmp8[:], axis=AX.X),
                            waits=[("VE", tt2)], inc="VE")
                        t_idx = P.op("dve", lambda e: e.tensor_copy(
                                idx_sb[:], r_ef[:]),
                            waits=[("VE", tt2)], inc="VE")
                        tc1 = P.op("dve", lambda e: e.tensor_copy(
                                r_cws[:, 0:1], r_w1[:]),
                            waits=[("AC", tw1)], inc="VE")
                        tc1 = P.op("dve", lambda e: e.tensor_scalar(
                                r_cws[:, 1:2], r_w1[:], -1.0, 1.0,
                                ALU.mult, ALU.add),
                            waits=[("VE", tc1)], inc="VE")
                        tm = P.op("pe", lambda e: e.matmul(
                                RT[:, 0:2], ones128[:], r_cws[:],
                                start=True, stop=True),
                            waits=[("VE", tc1), ("AC", t_b2cb)], inc="PE")
                        t_cwsel = P.op("act", lambda e: e.activation(
                                cwsel[:], RT[:, 0:2], AF.Copy),
                            waits=[("PE", tm)], inc="AC")
                        t_b1g = 0
                        with nc.gpsimd.register("rg0") as rg0:
                            for k in range(2):
                                def gfn(e, k=k, rg0=rg0):
                                    e.reg_load(rg0, idx_sb[0:1, k:k + 1])
                                    e.reg_mul(rg0, rg0, FF)
                                    return e.dma_start(
                                        out=b1pair[k:k + 1, :],
                                        in_=bass.AP(b1_h.tensor, rg0,
                                                    [[FF, 1], [1, FF]]))
                                t_b1g = P.op("pool", gfn,
                                    waits=[("VE", t_idx), ("Lb", t_b12)],
                                    inc="GB", incn=16)
                            pass
                        t_dsel = {}
                        with nc.gpsimd.register("rg1") as rg1:
                            for gi, (mat, k) in enumerate(
                                    ((0, 0), (0, 1), (1, 0), (1, 1))):
                                srcten = (w1_h if mat == 0 else w2_h).tensor
                                dstten = (wf1_h if mat == 0 else wf2_h).tensor
                                tt = 0
                                for q in range(4):
                                    def cfn(e, k=k, q=q, srcten=srcten,
                                            dstten=dstten, rg1=rg1):
                                        if q == 0:
                                            e.reg_load(rg1,
                                                       idx_sb[0:1, k:k + 1])
                                            e.reg_mul(rg1, rg1, D * FF)
                                        else:
                                            e.reg_add(rg1, rg1, 1048576)
                                        return e.dma_start(
                                            out=bass.AP(
                                                dstten,
                                                k * D * FF + q * 1048576,
                                                [[1024, 1024], [1, 1024]]),
                                            in_=bass.AP(
                                                srcten, rg1,
                                                [[1024, 1024], [1, 1024]]))
                                    tt = P.op("pool", cfn,
                                        waits=[("VE", t_idx)],
                                        inc=f"DS{gi}", incn=16)
                                t_dsel[(mat, k)] = tt
                        sc_jobs = []
                        for k in range(2):
                            for j in range(64):
                                sc_jobs.append(
                                    (("DS0" if k == 0 else "DS1"),
                                     t_dsel[(0, k)], ("w1", k),
                                     bass.AP(wf1_h.tensor,
                                             k * D * FF + j * 65536,
                                             [[512, 128], [1, 512]]),
                                     bass.AP(w1sel_h.tensor,
                                             k * D * FF + j * 65536,
                                             [[512, 128], [1, 512]])))
                            for j in range(64):
                                sc_jobs.append(
                                    (("DS2" if k == 0 else "DS3"),
                                     t_dsel[(1, k)], ("w2", k),
                                     bass.AP(wf2_h.tensor,
                                             k * FF * D + j * 65536,
                                             [[512, 128], [1, 512]]),
                                     bass.AP(w2sel_h.tensor,
                                             k * FF * D + j * 65536,
                                             [[512, 128], [1, 512]])))
                        vfree = [0, 0, 0, 0]
                        sfree = [0, 0, 0, 0]
                        sc_done = {}
                        lsem = ["CL0", "CL1", "CL2", "CL3"]
                        ssem = ["CS0", "CS1", "CS2", "CS3"]
                        for j, (dsem, dtok, tag, csrc, cdst) in enumerate(
                                sc_jobs):
                            i = j % 4
                            tl = P.op("pool", lambda e, s=csrc, i=i:
                                e.dma_start(out=scf[i][:], in_=s),
                                waits=[(dsem, dtok), ("VE", vfree[i])],
                                inc=lsem[i], incn=16)
                            tc = P.op("dve", lambda e, i=i:
                                e.tensor_copy(scb[i][:], scf[i][:]),
                                waits=[(lsem[i], tl),
                                       (ssem[i], sfree[i])], inc="VE")
                            vfree[i] = tc
                            ts_ = P.op("pool", lambda e, d=cdst, i=i:
                                e.dma_start(out=d, in_=scb[i][:]),
                                waits=[("VE", tc)],
                                inc=ssem[i], incn=16)
                            sfree[i] = ts_
                            if (j + 1) % 64 == 0:
                                sc_done[tag] = tuple(
                                    (ssem[z], P.cnt[ssem[z]])
                                    for z in range(4))

                        t_b1Ts = t_cwsel
                        for g16 in range(2):
                            tmb = 0
                            for j in range(16):
                                fc = g16 * 16 + j
                                tmb = P.op("pe", lambda e, fc=fc, j=j:
                                    e.transpose(
                                        RT[:, j * 2:(j + 1) * 2],
                                        b1pair[:, fc * 128:(fc + 1) * 128],
                                        ident_f[0:2, 0:2]),
                                    waits=[("GB", t_b1g), ("PO", t_const),
                                           ("AC", t_b1Ts)], inc="PE")
                            t_b1Ts = P.op("act", lambda e, g16=g16:
                                e.activation(
                                    b1Tsel[:, g16 * 32:(g16 + 1) * 32],
                                    RT[:, 0:32], AF.Copy),
                                waits=[("PE", tmb)], inc="AC")

        P.barrier()
        with ExitStack() as C4:
            w1r = sb(C4, "w1r", [128, 17 * 1024], BF16)
            w2r = sb(C4, "w2r", [128, 8 * 1024], BF16)
            x1tt = [sb(C4, f"x1tt{i}", [128, DC * 512], BF16)
                    for i in range(2)]
            h_sb = sb(C4, "h_sb", [128, FC * 512], BF16)
            oacc = sb(C4, "oacc", [128, 4 * 1024], F32)
            x1ft = [sb(C4, f"x1ft{i}", [128, 4 * 1024], F32)
                    for i in range(2)]
            gam_b = sb(C4, "gam_b", [128, D], F32)
            bet_b = sb(C4, "bet_b", [128, D], F32)
            ep_t = sb(C4, "ep_t", [128, 1024], F32)
            ep_o = [sb(C4, f"ep_o{i}", [128, 1024], F32) for i in range(2)]
            st4 = sb(C4, "st4", [128, 16], F32)
            rs4 = sb(C4, "rs4", [128, 2], F32)
            with ExitStack() as PS4:
                PO4 = [ps(PS4, f"PO4_{i}", [128, 1024], F32) for i in range(4)]

                t_gb = 0
                for row, dstb in ((gam_row, gam_b), (bet_row, bet_b)):
                    for hf in range(2):
                        tmb = P.op("pe", lambda e, r=row, hf=hf: e.matmul(
                                PO4[3][:, 0:512], ones128[:],
                                r[:, hf * 512:(hf + 1) * 512],
                                start=True, stop=True),
                            waits=[("Lsm", t_small), ("PO", t_const),
                                   ("AC", t_gb)], inc="PE")
                        t_gb = P.op("act", lambda e, d=dstb, hf=hf:
                            e.activation(
                                d[:, hf * 512:(hf + 1) * 512],
                                PO4[3][:, 0:512], AF.Copy),
                            waits=[("PE", tmb)], inc="AC")

                w1_ready = [0] * 17
                w1_free = [0] * 17
                w2_ready = [0] * 8
                w2_free = [0] * 8
                w1_ctr = 0
                w2_ctr = 0
                t_x1tt = [0, 0]
                t_x1tt_free = [0, 0]
                t_x1ft = [0, 0]
                t_x1ft_free = [0, 0]
                t_h_free = 0
                t_PO4h_free = [0, 0]
                t_po40_evac = [0, 0]
                t_oacc_evac = 0
                t_oacc_read = 0
                t_out_free = [0, 0]
                t_st4_free = 0
                gelu_tok = {}

                for tile in range(NT):
                    ts_ = tile % 2
                    tld = 0
                    for dc in range(DC):
                        tld = load(x1tt[ts_][:, dc * 512:(dc + 1) * 512],
                                   x1t_h[dc][:, tile * 512:(tile + 1) * 512],
                                   sem=f"Lxt{ts_}",
                                   waits=[("Sx1t", t_x1t_store),
                                          ("PE", t_x1tt_free[ts_])])
                    t_x1tt[ts_] = tld
                    t_x1ft[ts_] = load(
                        x1ft[ts_][:].rearrange("p (s d) -> p s d", d=D),
                        x1f_h.rearrange("(t s p) d -> t p s d",
                                        p=128, s=4)[tile],
                        sem=f"Lxf{ts_}",
                        waits=[("Sx1f", t_x1f_store),
                               ("VE", t_x1ft_free[ts_])])
                    tinit = 0
                    for ns in range(4):
                        tinit = P.op("dve", lambda e, ns=ns, ts_=ts_:
                            e.tensor_add(
                                oacc[:, ns * 1024:(ns + 1) * 1024],
                                x1ft[ts_][:, ns * 1024:(ns + 1) * 1024],
                                b2cb[:]),
                            waits=[(f"Lxf{ts_}", t_x1ft[ts_]),
                                   ("AC", t_b2cb),
                                   ("VE", t_oacc_evac),
                                   ("VE", t_oacc_read)], inc="VE")
                    t_oacc_init = tinit

                    for e_ in (range(2) if SPARSE else range(E)):
                        for fg in range(FG):
                            gsem = ["Lw1a", "Lw1b", "Lw1c", "Lw1d"][(e_ * FG + fg) % 4]
                            grp_tok = 0
                            for dc in range(DC):
                                slot = w1_ctr % 17
                                w1src = (w1sel_h[e_, dc * 128:(dc + 1) * 128,
                                                 fg * 1024:(fg + 1) * 1024]
                                         if SPARSE
                                         else w1s_h[e_, fg, dc])
                                wws = (list(sc_done[("w1", e_)])
                                       if SPARSE else
                                       [("GCs0", cast_done[e_][0]),
                                        ("GCs1", cast_done[e_][1])])
                                grp_tok = load(
                                    w1r[:, slot * 1024:(slot + 1) * 1024],
                                    w1src,
                                    sem=gsem,
                                    waits=wws + [("PE", w1_free[slot])])
                                w1_ctr += 1
                            base_slot = (w1_ctr - DC) % 17
                            for fcl in range(8):
                                fc = fg * 8 + fcl
                                hp = fc % 2
                                tm = 0
                                for dc in range(DC):
                                    slot = (base_slot + dc) % 17
                                    tm = P.op("pe", lambda e, slot=slot,
                                              fcl=fcl, dc=dc, hp=hp,
                                              ts_=ts_: e.matmul(
                                            PO4[0][:, hp * 512:
                                                   (hp + 1) * 512],
                                            w1r[:, slot * 1024 + fcl * 128:
                                                slot * 1024 +
                                                (fcl + 1) * 128],
                                            x1tt[ts_][:, dc * 512:
                                                      (dc + 1) * 512],
                                            start=(dc == 0),
                                            stop=(dc == DC - 1)),
                                        waits=[(gsem, grp_tok),
                                               (f"Lxt{ts_}", t_x1tt[ts_]),
                                               ("AC", t_PO4h_free[hp]),
                                               ("VE", t_po40_evac[hp]),
                                               ("PE", t_h_free)], inc="PE")
                                    if fcl == 7:
                                        w1_free[slot] = tm
                                tg = P.op("act", lambda e, fc=fc, hp=hp,
                                          e_=e_: e.activation(
                                        h_sb[:, fc * 512:(fc + 1) * 512],
                                        PO4[0][:, hp * 512:(hp + 1) * 512],
                                        AF.Gelu,
                                        bias=(b1Tsel[:, fc * 2 + e_:
                                                     fc * 2 + e_ + 1]
                                              if SPARSE else
                                              b1T[:, fc * E + e_:
                                                  fc * E + e_ + 1])),
                                    waits=[("PE", tm),
                                           ("AC", t_b1Ts if SPARSE
                                            else t_b1T)],
                                    inc="AC")
                                t_PO4h_free[hp] = tg
                                gelu_tok[fc] = tg
                        for fc in range(FC):
                            if fc % 4 == 0:
                                g2sem = ("Lw2a" if (w2_ctr // 4) % 2 == 0
                                         else "Lw2b")
                                g2tok = 0
                                for k in range(4):
                                    slot = w2_ctr % 8
                                    w2src = (w2sel_h[e_,
                                                     (fc + k) * 128:
                                                     (fc + k + 1) * 128, :]
                                             if SPARSE
                                             else w2s_h[e_, fc + k])
                                    wws = (list(sc_done[("w2", e_)])
                                           if SPARSE else
                                           [("GCs0", cast_done[e_][0]),
                                            ("GCs1", cast_done[e_][1])])
                                    g2tok = load(
                                        w2r[:, slot * 1024:(slot + 1) * 1024],
                                        w2src,
                                        sem=g2sem,
                                        waits=wws + [("PE", w2_free[slot])])
                                    w2_ctr += 1
                            slot = fc % 8
                            tm = 0
                            for ns in range(4):
                                for ds in range(2):
                                    tm = P.op("pe", lambda e, fc=fc, ns=ns,
                                              ds=ds, slot=slot: e.matmul(
                                            PO4[ns][:, ds * 512:
                                                    (ds + 1) * 512],
                                            h_sb[:, fc * 512 + ns * 128:
                                                 fc * 512 + (ns + 1) * 128],
                                            w2r[:, slot * 1024 + ds * 512:
                                                slot * 1024 +
                                                (ds + 1) * 512],
                                            start=(fc == 0),
                                            stop=(fc == FC - 1)),
                                        waits=[(g2sem, g2tok),
                                               ("AC", gelu_tok[fc]),
                                               ("AC", t_gb if fc == 0 else 0),
                                               ("AC", max(t_PO4h_free)
                                                if fc == 0 else 0),
                                               ("VE", t_oacc_evac
                                                if fc == 0 else 0)],
                                        inc="PE")
                            w2_free[slot] = tm
                            tlast_w2 = tm
                        t_h_free = tlast_w2
                        tev = 0
                        for ns in range(4):
                            for ds in range(2):
                                tev = P.op("dve", lambda e, ns=ns, ds=ds,
                                           e_=e_: e.scalar_tensor_tensor(
                                        oacc[:, ns * 1024 + ds * 512:
                                             ns * 1024 + (ds + 1) * 512],
                                        PO4[ns][:, ds * 512:(ds + 1) * 512],
                                        (cwsel[:, e_:e_ + 1] if SPARSE
                                         else cwb[:, e_:e_ + 1]),
                                        oacc[:, ns * 1024 + ds * 512:
                                             ns * 1024 + (ds + 1) * 512],
                                        ALU.mult, ALU.add),
                                    waits=[("PE", tlast_w2),
                                           ("AC", t_cwsel if SPARSE
                                            else t_cwb),
                                           ("VE", t_oacc_init)], inc="VE")
                                if ns == 0:
                                    t_po40_evac[ds] = tev
                        t_oacc_evac = tev

                    for ns in range(4):
                        os_ = ns % 2
                        tst_ = P.op("dve", lambda e, ns=ns: e.bn_stats(
                                st4[:, 0:6], oacc[:, ns * 1024:
                                                  ns * 1024 + 512]),
                            waits=[("VE", t_oacc_evac), ("VE", t_st4_free)],
                            inc="VE")
                        tst_ = P.op("dve", lambda e, ns=ns: e.bn_stats(
                                st4[:, 6:12], oacc[:, ns * 1024 + 512:
                                                   (ns + 1) * 1024]),
                            waits=[("VE", tst_)], inc="VE")
                        tst_ = P.op("dve", lambda e: e.bn_aggr(
                                st4[:, 12:14], st4[:, 0:12]),
                            waits=[("VE", tst_)], inc="VE")
                        tsr = P.op("act", lambda e: e.activation(
                                st4[:, 14:15], st4[:, 13:14], AF.Sqrt,
                                bias=epsv[:]),
                            waits=[("VE", tst_)], inc="AC")
                        trc = P.op("dve", lambda e: e.reciprocal(
                                rs4[:, 0:1], st4[:, 14:15]),
                            waits=[("AC", tsr)], inc="VE")
                        tt = P.op("dve", lambda e, ns=ns: e.tensor_scalar(
                                ep_t[:], oacc[:, ns * 1024:(ns + 1) * 1024],
                                st4[:, 12:13], rs4[:, 0:1],
                                ALU.subtract, ALU.mult),
                            waits=[("VE", trc)], inc="VE")
                        t_oacc_read = tt
                        tt = P.op("dve", lambda e: e.tensor_mul(
                                ep_t[:], ep_t[:], gam_b[:]),
                            waits=[("VE", tt)], inc="VE")
                        tt = P.op("dve", lambda e: e.tensor_add(
                                ep_t[:], ep_t[:], bet_b[:]),
                            waits=[("VE", tt)], inc="VE")
                        tt = P.op("dve", lambda e, ns=ns, ts_=ts_:
                            e.tensor_add(
                                ep_t[:], ep_t[:],
                                x1ft[ts_][:, ns * 1024:(ns + 1) * 1024]),
                            waits=[("VE", tt)], inc="VE")
                        if ns == 3:
                            t_x1ft_free[ts_] = tt
                        tst_ = P.op("dve", lambda e: e.bn_stats(
                                st4[:, 0:6], ep_t[:, 0:512]),
                            waits=[("VE", tt)], inc="VE")
                        tst_ = P.op("dve", lambda e: e.bn_stats(
                                st4[:, 6:12], ep_t[:, 512:1024]),
                            waits=[("VE", tst_)], inc="VE")
                        tst_ = P.op("dve", lambda e: e.bn_aggr(
                                st4[:, 12:14], st4[:, 0:12]),
                            waits=[("VE", tst_)], inc="VE")
                        tsr = P.op("act", lambda e: e.activation(
                                st4[:, 14:15], st4[:, 13:14], AF.Sqrt,
                                bias=epsv[:]),
                            waits=[("VE", tst_)], inc="AC")
                        trc = P.op("dve", lambda e: e.reciprocal(
                                rs4[:, 1:2], st4[:, 14:15]),
                            waits=[("AC", tsr)], inc="VE")
                        t_st4_free = trc
                        tout = P.op("dve", lambda e, os_=os_: e.tensor_scalar(
                                ep_o[os_][:], ep_t[:],
                                st4[:, 12:13], rs4[:, 1:2],
                                ALU.subtract, ALU.mult),
                            waits=[("VE", trc),
                                   (f"ST{os_}", t_out_free[os_])],
                            inc="VE")
                        P.op("pool", lambda e, tile=tile, ns=ns, os_=os_:
                            e.dma_start(
                                out=y_h[tile * 512 + ns * 128:
                                        tile * 512 + (ns + 1) * 128, :],
                                in_=ep_o[os_][:]),
                            waits=[("VE", tout)], inc=f"ST{os_}", incn=16)
                        t_out_free[os_] = P.cnt[f"ST{os_}"]
                    t_x1tt_free[ts_] = tlast_w2

        P.wait("pool", [("ST0", P.cnt["ST0"]), ("ST1", P.cnt["ST1"])])
        P.wait("sync", [("ST0", P.cnt["ST0"]), ("ST1", P.cnt["ST1"])])

        with nc.Block() as block:
            P.run(block)
    return nc


_NC_CACHE = {}


def kernel(**inputs):
    x = np.ascontiguousarray(np.asarray(inputs["L0_attn_x"], dtype=np.float32))
    B, S, _ = x.shape
    if S not in _NC_CACHE:
        _NC_CACHE[S] = build(S=S)
    nc = _NC_CACHE[S]
    w = {
        "wq": np.ascontiguousarray(np.asarray(inputs["L0_attn_Wq"], np.float32)),
        "wk": np.ascontiguousarray(np.asarray(inputs["L0_attn_Wk"], np.float32)),
        "wv": np.ascontiguousarray(np.asarray(inputs["L0_attn_Wv"], np.float32)),
        "wo": np.ascontiguousarray(np.asarray(inputs["L0_attn_Wo"], np.float32)),
        "bo": np.ascontiguousarray(np.asarray(inputs["L0_attn_bo"], np.float32)),
        "w1": np.ascontiguousarray(np.asarray(inputs["L0_ffn_W1"], np.float32)),
        "b1": np.ascontiguousarray(np.asarray(inputs["L0_ffn_b1"], np.float32)),
        "w2": np.ascontiguousarray(np.asarray(inputs["L0_ffn_W2"], np.float32)),
        "b2": np.ascontiguousarray(np.asarray(inputs["L0_ffn_b2"], np.float32)),
        "wr": np.ascontiguousarray(np.asarray(inputs["L0_ffn_Wr"], np.float32)),
        "br": np.ascontiguousarray(np.asarray(inputs["L0_ffn_br"], np.float32)),
        "gamma": np.ascontiguousarray(np.asarray(inputs["L0_ffn_gamma"],
                                                 np.float32)),
        "beta": np.ascontiguousarray(np.asarray(inputs["L0_ffn_beta"],
                                                np.float32)),
    }
    in_maps = [dict(x=np.ascontiguousarray(x[b]), **w) for b in range(B)]
    res = run_bass_kernel_spmd(nc, in_maps, core_ids=list(range(B)))
    return np.stack([np.asarray(r["y"], np.float32) for r in res.results],
                    axis=0)


# version 20
# speedup vs baseline: 1.3664x; 1.1407x over previous
import numpy as np
from contextlib import ExitStack

import concourse.bass as bass
import concourse.mybir as mybir
from concourse.bass_utils import run_bass_kernel_spmd

F32 = mybir.dt.float32
BF16 = mybir.dt.bfloat16
AF = mybir.ActivationFunctionType
ALU = mybir.AluOpType
AX = mybir.AxisListType

D = 1024
H = 16
DH = 64
FF = 4096
E = 8
DC = D // 128
FC = FF // 128
FG = 4
SCALE = DH ** (-0.25)
LN_EPS = 1e-5
SPARSE = True

ENGS = ("pe", "act", "dve", "pool", "sync")
ENGMAP = {"pe": "tensor", "act": "scalar", "dve": "vector",
          "pool": "gpsimd", "sync": "sync"}


class Prog:
    def __init__(self, nc):
        self.nc = nc
        self.Q = {k: [] for k in ENGS}
        self.sems = {}
        self.cnt = {}
        self.lastwait = {k: {} for k in ENGS}

    def add_sem(self, name, handle):
        self.sems[name] = handle
        self.cnt[name] = 0

    def bump(self, s, n=1):
        self.cnt[s] += n
        return self.cnt[s]

    def op(self, eng, fn, waits=(), inc=None, incn=1):
        tok = None
        wl = []
        for s, v in waits:
            if not v:
                continue
            if v > self.lastwait[eng].get(s, 0):
                self.lastwait[eng][s] = v
                wl.append((self.sems[s], v))
        if inc is not None:
            tok = self.bump(inc, incn)
            ih = (self.sems[inc], incn)
        else:
            ih = None

        def f(fn=fn, wl=wl, ih=ih, eng=eng):
            e = getattr(self.nc, ENGMAP[eng])
            for sh, v in wl:
                e.wait_ge(sh, v)
            if fn is not None:
                ins = fn(e)
                if ih is not None:
                    ins.then_inc(ih[0], ih[1])

        self.Q[eng].append(f)
        return tok

    def wait(self, eng, waits):
        self.op(eng, None, waits=waits)

    def barrier(self):
        snap = [("PE", self.cnt["PE"]), ("AC", self.cnt["AC"]),
                ("VE", self.cnt["VE"])]
        for eng, own in (("pe", "PE"), ("act", "AC"), ("dve", "VE")):
            self.wait(eng, [w for w in snap if w[0] != own])
        self.wait("sync", snap)

    def run(self, block):
        @block.tensor
        def _(e):
            for f in self.Q["pe"]:
                f()

        @block.scalar
        def _(e):
            for f in self.Q["act"]:
                f()

        @block.vector
        def _(e):
            for f in self.Q["dve"]:
                f()

        @block.gpsimd
        def _(e):
            for f in self.Q["pool"]:
                f()

        @block.sync
        def _(e):
            for f in self.Q["sync"]:
                f()


def build(S=4096):
    NT = S // 512
    NC = S // 128
    JGRP = NC // 2
    nc = bass.Bass("TRN2", target_bir_lowering=False)

    x_h = nc.dram_tensor("x", [S, D], F32, kind="ExternalInput").ap()
    wq_h = nc.dram_tensor("wq", [D, D], F32, kind="ExternalInput").ap()
    wk_h = nc.dram_tensor("wk", [D, D], F32, kind="ExternalInput").ap()
    wv_h = nc.dram_tensor("wv", [D, D], F32, kind="ExternalInput").ap()
    wo_h = nc.dram_tensor("wo", [D, D], F32, kind="ExternalInput").ap()
    bo_h = nc.dram_tensor("bo", [D], F32, kind="ExternalInput").ap()
    w1_h = nc.dram_tensor("w1", [E, D, FF], F32, kind="ExternalInput").ap()
    b1_h = nc.dram_tensor("b1", [E, FF], F32, kind="ExternalInput").ap()
    w2_h = nc.dram_tensor("w2", [E, FF, D], F32, kind="ExternalInput").ap()
    b2_h = nc.dram_tensor("b2", [E, D], F32, kind="ExternalInput").ap()
    wr_h = nc.dram_tensor("wr", [D, E], F32, kind="ExternalInput").ap()
    br_h = nc.dram_tensor("br", [E], F32, kind="ExternalInput").ap()
    gam_h = nc.dram_tensor("gamma", [D], F32, kind="ExternalInput").ap()
    bet_h = nc.dram_tensor("beta", [D], F32, kind="ExternalInput").ap()
    y_h = nc.dram_tensor("y", [S, D], F32, kind="ExternalOutput").ap()
    qt_h = nc.dram_tensor("qt_s", [D, S], BF16).ap()
    kt_h = nc.dram_tensor("kt_s", [D, S], BF16).ap()
    x1f_h = nc.dram_tensor("x1f_s", [S, D], F32).ap()
    x1t_h = nc.dram_tensor("x1t_s", [DC, 128, S], BF16).ap()
    w1s_h = nc.dram_tensor("w1s_s", [E, FG, DC, 128, 1024], BF16).ap()
    w2s_h = nc.dram_tensor("w2s_s", [E, FC, 128, D], BF16).ap()
    w1sel_h = nc.dram_tensor("w1sel_s", [2, D, FF], BF16).ap()
    w2sel_h = nc.dram_tensor("w2sel_s", [2, FF, D], BF16).ap()
    wf1_h = nc.dram_tensor("wf1_s", [2, D, FF], F32).ap()
    wf2_h = nc.dram_tensor("wf2_s", [2, FF, D], F32).ap()

    P = Prog(nc)
    with ExitStack() as CTX:
        _semnames = ["PE", "AC", "VE", "PO", "Lsm", "Lb", "Lx0", "Lx1",
                     "LW", "Lk", "Lq0", "Lq1", "Sqk0", "Sqk1", "Sx1f",
                     "Sx1t", "Lw1a", "Lw1b", "Lw2a", "Lw2b", "Lxt0", "Lxt1",
                     "Lxf0", "Lxf1", "GCl", "GCs0", "GCs1", "ST0", "ST1", "GS", "GS2", "GB", "Lk1", "Lw1c", "Lw1d", "GCl2", "DS0", "DS1", "DS2", "DS3",
                     "CL0", "CL1", "CL2", "CL3", "CS0", "CS1", "CS2", "CS3"]
        for sname in _semnames:
            P.add_sem(sname, CTX.enter_context(nc.semaphore(sname)))

        def sb(stack, name, shape, dtype=F32):
            return stack.enter_context(nc.sbuf_tensor(name, shape, dtype))

        def ps(stack, name, shape, dtype=F32):
            return stack.enter_context(nc.psum_tensor(name, shape, dtype))

        def load(dst, src, sem="Lsm", waits=()):
            return P.op("sync", lambda e, d=dst, s=src: e.dma_start(out=d, in_=s),
                        waits=waits, inc=sem, incn=16)

        store = load

        ones64 = sb(CTX, "ones64", [1, 64], F32)
        ones128 = sb(CTX, "ones128", [1, 128], F32)
        ident_b = sb(CTX, "ident_b", [128, 128], BF16)
        ident_f = sb(CTX, "ident_f", [128, 128], F32)
        sel = sb(CTX, "sel", [16, 2048], BF16)
        bo_row = sb(CTX, "bo_row", [1, D], F32)
        gam_row = sb(CTX, "gam_row", [1, D], F32)
        bet_row = sb(CTX, "bet_row", [1, D], F32)
        br_row = sb(CTX, "br_row", [1, E], F32)
        wr_sb = sb(CTX, "wr_sb", [128, DC * E], F32)
        b1T = None if SPARSE else sb(CTX, "b1T", [128, E * FC], F32)
        b2cb = sb(CTX, "b2cb", [128, D], F32)
        cwb = sb(CTX, "cwb", [128, E], F32)
        cwT = sb(CTX, "cwT", [8, 1], F32)
        b2c_row = sb(CTX, "b2c_row", [1, D], F32)
        bias_all = sb(CTX, "bias_all", [128, H * NT], F32)
        accq = sb(CTX, "accq", [128, NT * 16], F32)
        acck = sb(CTX, "acck", [128, 16], F32)
        qtbl = sb(CTX, "qtbl", [16, NT], F32)
        kmax2 = sb(CTX, "kmax2", [16, 1], F32)
        mtbl = sb(CTX, "mtbl", [16, NT], BF16)
        xsacc = sb(CTX, "xsacc", [128, E], F32)
        epsv = sb(CTX, "epsv", [128, 1], F32)
        io8 = sb(CTX, "io8", [1, 8], F32)
        r_tmp8 = sb(CTX, "r_tmp8", [1, 8], F32)
        r_ef = sb(CTX, "r_ef", [1, 2], F32)
        idx_sb = sb(CTX, "idx_sb", [1, 2], mybir.dt.int32)
        r_cws = sb(CTX, "r_cws", [1, 2], F32)
        cwsel = sb(CTX, "cwsel", [128, 2], F32)
        b1Tsel = sb(CTX, "b1Tsel", [128, FC * 2], F32)
        r_z = sb(CTX, "r_z", [1, E], F32)
        r_rw = sb(CTX, "r_rw", [1, E], F32)
        r_rw2 = sb(CTX, "r_rw2", [1, E], F32)
        r_m1 = sb(CTX, "r_m1", [1, 1], F32)
        r_m2 = sb(CTX, "r_m2", [1, 1], F32)
        r_m2n = sb(CTX, "r_m2n", [1, 1], F32)
        r_w1 = sb(CTX, "r_w1", [1, 1], F32)
        r_s = sb(CTX, "r_s", [1, 1], F32)
        r_mask1 = sb(CTX, "r_mask1", [1, E], F32)
        r_mask2 = sb(CTX, "r_mask2", [1, E], F32)
        r_d12 = sb(CTX, "r_d12", [1, E], F32)
        cw_row = sb(CTX, "cw_row", [1, E], F32)

        tpo = 0
        for buf, val in ((ones64, 1.0), (ones128, 1.0), (ident_b, 0.0),
                         (ident_f, 0.0), (sel, 1.0), (accq, 0.0), (acck, 0.0),
                         (xsacc, 0.0), (epsv, LN_EPS)):
            tpo = P.op("pool", lambda e, b=buf, v=val: e.memset(b[:], v),
                       inc="PO")
        tmemset = tpo
        tpo = P.op("pool", lambda e: e.affine_select(
            out=ident_b[:], in_=ident_b[:], compare_op=ALU.not_equal,
            fill=1.0, base=0, pattern=[[-1, 128]], channel_multiplier=1),
            waits=[("PO", tmemset)], inc="PO")
        tpo = P.op("pool", lambda e: e.affine_select(
            out=ident_f[:], in_=ident_f[:], compare_op=ALU.not_equal,
            fill=1.0, base=0, pattern=[[-1, 128]], channel_multiplier=1),
            waits=[("PO", tmemset)], inc="PO")
        tpo = P.op("pool", lambda e: e.affine_select(
            out=sel[:], in_=sel[:], compare_op=ALU.is_ge,
            fill=0.0, base=0, pattern=[[1, 2048]], channel_multiplier=-128),
            waits=[("PO", tmemset)], inc="PO")
        tpo = P.op("pool", lambda e: e.affine_select(
            out=sel[:], in_=sel[:], compare_op=ALU.is_ge,
            fill=0.0, base=127, pattern=[[-1, 2048]], channel_multiplier=128),
            waits=[("PO", tpo)], inc="PO")
        tpo = P.op("pool", lambda e: e.iota(
            io8[:], pattern=[[1, 8]], channel_multiplier=0,
            allow_small_or_imprecise_dtypes=True), inc="PO")
        t_const = tpo

        load(bo_row[:], bo_h.rearrange("(o d) -> o d", o=1))
        load(gam_row[:], gam_h.rearrange("(o d) -> o d", o=1))
        load(bet_row[:], bet_h.rearrange("(o d) -> o d", o=1))
        load(br_row[:], br_h.rearrange("(o e) -> o e", o=1))
        t_small = load(wr_sb[:].rearrange("p (c e) -> p c e", e=E),
                       wr_h.rearrange("(c p) e -> p c e", p=128))

        if SPARSE:
            scf = [sb(CTX, f"scf{i}", [128, 512], F32) for i in range(4)]
            scb = [sb(CTX, f"scb{i}", [128, 512], BF16) for i in range(4)]
        else:
            cst_f = [sb(CTX, f"cstf{i}", [128, 512], F32) for i in range(2)]
            cst_b = [sb(CTX, f"cstb{i}", [128, 512], BF16) for i in range(2)]
        cast_jobs = []
        for e_ in range(E):
            for dc in range(DC):
                for fg in range(FG):
                    for hh in range(2):
                        cast_jobs.append((
                            w1_h[e_, dc * 128:(dc + 1) * 128,
                                 fg * 1024 + hh * 512:
                                 fg * 1024 + (hh + 1) * 512],
                            w1s_h[e_, fg, dc][:, hh * 512:(hh + 1) * 512]))
            for fc in range(FC):
                for hh in range(2):
                    cast_jobs.append((
                        w2_h[e_, fc * 128:(fc + 1) * 128,
                             hh * 512:(hh + 1) * 512],
                        w2s_h[e_, fc][:, hh * 512:(hh + 1) * 512]))
        cast_done = {}
        if not SPARSE:
            cast_jobs = []
            for e_ in range(E):
                for dc in range(DC):
                    for fg in range(FG):
                        for hh in range(2):
                            cast_jobs.append((
                                w1_h[e_, dc * 128:(dc + 1) * 128,
                                     fg * 1024 + hh * 512:
                                     fg * 1024 + (hh + 1) * 512],
                                w1s_h[e_, fg, dc][:, hh * 512:(hh + 1) * 512]))
                for fc in range(FC):
                    for hh in range(2):
                        cast_jobs.append((
                            w2_h[e_, fc * 128:(fc + 1) * 128,
                                 hh * 512:(hh + 1) * 512],
                            w2s_h[e_, fc][:, hh * 512:(hh + 1) * 512]))
            cfree = [0, 0]
            csfree = [0, 0]
            per_e = len(cast_jobs) // E
            for j, (csrc, cdst) in enumerate(cast_jobs):
                i = j % 2
                tl = P.op("pool", lambda e, s=csrc, i=i: e.dma_start(
                    out=cst_f[i][:], in_=s), waits=[("PO", cfree[i])],
                    inc="GCl", incn=16)
                tc = P.op("pool", lambda e, i=i: e.tensor_copy(
                    cst_b[i][:], cst_f[i][:]),
                    waits=[("GCl", tl), (f"GCs{i}", csfree[i])], inc="PO")
                cfree[i] = tc
                tst = P.op("pool", lambda e, d=cdst, i=i: e.dma_start(
                    out=d, in_=cst_b[i][:]), waits=[("PO", tc)],
                    inc=f"GCs{i}", incn=16)
                csfree[i] = tst
                if (j + 1) % per_e == 0:
                    cast_done[(j + 1) // per_e - 1] = (
                        P.cnt["GCs0"], P.cnt["GCs1"])

        with ExitStack() as C_AT:
            with ExitStack() as C_V:
                v_sb = sb(C_V, "v_sb", [128, NC * H * 65], BF16)

                with ExitStack() as C1:
                    wq_bf = sb(C1, "wq_bf", [128, DC * 1024], BF16)
                    wk_bf = sb(C1, "wk_bf", [128, DC * 1024], BF16)
                    wv_bf = sb(C1, "wv_bf", [128, DC * 1024], BF16)
                    wstg = sb(C1, "wstg", [128, 2048], F32)
                    xstg = sb(C1, "xstg", [128, 4096], F32)
                    xb = sb(C1, "xb", [128, 4096], BF16)
                    xT = sb(C1, "xT", [128, DC * 512], BF16)
                    qkst = [sb(C1, f"qkst{i}", [128, 512], BF16)
                            for i in range(2)]
                    sq_sb = [sb(C1, f"sq{i}", [128, 512], F32)
                             for i in range(2)]
                    nrm8 = [sb(C1, f"nrm8{i}", [128, 8], F32)
                            for i in range(2)]

                    with ExitStack() as PS1:
                        psT = [ps(PS1, f"psT{i}", [128, 1024], BF16)
                               for i in range(2)]
                        ps_q = ps(PS1, "ps_q", [128, 512], F32)
                        ps_k = ps(PS1, "ps_k", [128, 512], F32)
                        ps_v = ps(PS1, "ps_v", [128, 1024], F32)
                        ps_n = ps(PS1, "ps_n", [128, 1024], F32)

                        t_psq_free = 0
                        t_psn_free = [0, 0]
                        t_wst_free = 0
                        wtoks = []
                        for wh, wbf in ((wq_h, wq_bf), (wk_h, wk_bf),
                                        (wv_h, wv_bf)):
                            tdone = 0
                            for half in range(4):
                                tl = load(wstg[:].rearrange(
                                              "p (c d) -> p c d", d=1024),
                                          wh.rearrange("(c p) d -> p c d",
                                                       p=128)
                                          [:, half * 2:(half + 1) * 2, :],
                                          sem="LW",
                                          waits=[("VE", t_wst_free)])
                                tdone = P.op("dve", lambda e, w=wbf,
                                             half=half: e.tensor_copy(
                                        w[:, half * 2048:(half + 1) * 2048],
                                        wstg[:]),
                                    waits=[("LW", tl)], inc="VE")
                                t_wst_free = tdone
                            wtoks.append(tdone)
                        t_wq, t_wk, t_wv = wtoks

                        t_vones = P.op("dve", lambda e: e.memset(
                            v_sb[:].rearrange("p (c u) -> p c u", u=65)
                            [:, :, 64:65], 1.0), inc="VE")

                        t_xstg_free = 0
                        t_xb_free = 0
                        t_psT_free = [0, 0]
                        t_psk_free = 0
                        t_psv_free = [0, 0]
                        t_qkst_free = [0, 0]
                        t_sq_free = [0, 0]
                        t_nrm_free = [0, 0]
                        qk_slot = 0
                        for nt in range(NT):
                            tl = load(xstg[:].rearrange(
                                          "p (s d) -> p s d", d=D),
                                      x_h.rearrange("(t s p) d -> t p s d",
                                                    p=128, s=4)[nt],
                                      sem="Lx0",
                                      waits=[("VE", t_xstg_free)])
                            t_cast = P.op("dve", lambda e: e.tensor_copy(
                                xb[:], xstg[:]),
                                waits=[("Lx0", tl), ("PE", t_xb_free)],
                                inc="VE")
                            t_xstg_free = t_cast
                            t_xT_evac = [0, 0, 0, 0]
                            tm = 0
                            for ns in range(4):
                                for dc in range(DC):
                                    tm = P.op("pe", lambda e, ns=ns, dc=dc:
                                        e.transpose(
                                            psT[ns % 2][:, dc * 128:
                                                        (dc + 1) * 128],
                                            xb[:, ns * 1024 + dc * 128:
                                               ns * 1024 + (dc + 1) * 128],
                                            ident_b[:]),
                                        waits=[("VE", t_cast),
                                               ("PO", t_const),
                                               ("AC", t_psT_free[ns % 2])],
                                        inc="PE")
                                t_xT_evac[ns] = P.op("act", lambda e, ns=ns:
                                    e.activation(
                                        xT[:].rearrange("p (dc n) -> p dc n",
                                                        n=512)
                                        [:, :, ns * 128:(ns + 1) * 128],
                                        psT[ns % 2][:], AF.Copy),
                                    waits=[("PE", tm)], inc="AC")
                                t_psT_free[ns % 2] = t_xT_evac[ns]
                            t_xb_free = tm
                            t_xT = t_xT_evac[3]

                            for dpc in range(DC):
                                for wbf, twc, psq, which in (
                                        (wq_bf, t_wq, ps_q, "q"),
                                        (wk_bf, t_wk, ps_k, "k")):
                                    pfree = (t_psq_free if which == "q"
                                             else t_psk_free)
                                    tm = 0
                                    for dc in range(DC):
                                        tm = P.op("pe", lambda e, w=wbf,
                                                  dc=dc, dpc=dpc, p_=psq:
                                            e.matmul(
                                                p_[:],
                                                w[:, dc * 1024 + dpc * 128:
                                                  dc * 1024 + (dpc + 1) * 128],
                                                xT[:, dc * 512:(dc + 1) * 512],
                                                start=(dc == 0),
                                                stop=(dc == DC - 1)),
                                            waits=[("AC", t_xT), ("VE", twc),
                                                   ("AC", pfree)], inc="PE")
                                    ta = P.op("act", lambda e, slot=qk_slot,
                                              p_=psq: e.activation(
                                            qkst[slot][:], p_[:], AF.Copy,
                                            scale=SCALE),
                                        waits=[("PE", tm),
                                               (f"Sqk{qk_slot}",
                                                t_qkst_free[qk_slot])],
                                        inc="AC")
                                    if which == "q":
                                        t_psq_free = ta
                                    else:
                                        t_psk_free = ta
                                    dsth = qt_h if which == "q" else kt_h
                                    tst = store(
                                        dsth[dpc * 128:(dpc + 1) * 128,
                                             nt * 512:(nt + 1) * 512],
                                        qkst[qk_slot][:],
                                        sem=f"Sqk{qk_slot}",
                                        waits=[("AC", ta)])
                                    t_qkst_free[qk_slot] = tst
                                    qk_slot = (qk_slot + 1) % 2

                            for ns in range(4):
                                for ds in range(2):
                                    half = ds
                                    tm = 0
                                    for dc in range(DC):
                                        tm = P.op("pe", lambda e, ns=ns,
                                                  ds=ds, dc=dc, half=half:
                                            e.matmul(
                                                ps_v[:, half * 512:
                                                     (half + 1) * 512],
                                                xT[:, dc * 512 + ns * 128:
                                                   dc * 512 + (ns + 1) * 128],
                                                wv_bf[:, dc * 1024 + ds * 512:
                                                      dc * 1024 +
                                                      (ds + 1) * 512],
                                                start=(dc == 0),
                                                stop=(dc == DC - 1)),
                                            waits=[("AC", t_xT), ("VE", t_wv),
                                                   ("AC", t_psv_free[half]),
                                                   ("VE", t_vones)],
                                            inc="PE")
                                    base = (nt * 4 + ns) * (H * 65) + ds * 520
                                    t_psv_free[half] = P.op("act",
                                        lambda e, half=half, base=base:
                                        e.activation(
                                            v_sb[:, base:base + 520]
                                            .rearrange("p (h u) -> p h u",
                                                       u=65)[:, :, 0:64],
                                            ps_v[:, half * 512:
                                                 (half + 1) * 512],
                                            AF.Copy),
                                        waits=[("PE", tm)], inc="AC")

                            for wbf, twc, which in ((wq_bf, t_wq, "q"),
                                                    (wk_bf, t_wk, "k")):
                                for ns in range(4):
                                    for ds in range(2):
                                        half = (ns * 2 + ds) % 2
                                        tm = 0
                                        for dc in range(DC):
                                            tm = P.op("pe", lambda e, w=wbf,
                                                      ns=ns, ds=ds, dc=dc,
                                                      half=half: e.matmul(
                                                    ps_n[:, half * 512:
                                                         (half + 1) * 512],
                                                    xT[:, dc * 512 + ns * 128:
                                                       dc * 512 +
                                                       (ns + 1) * 128],
                                                    w[:, dc * 1024 + ds * 512:
                                                      dc * 1024 +
                                                      (ds + 1) * 512],
                                                    start=(dc == 0),
                                                    stop=(dc == DC - 1)),
                                                waits=[("AC", t_xT),
                                                       ("VE", twc),
                                                       ("AC",
                                                        t_psn_free[half])],
                                                inc="PE")
                                        tsq = P.op("act", lambda e, half=half:
                                            e.activation(
                                                sq_sb[half][:],
                                                ps_n[:, half * 512:
                                                     (half + 1) * 512],
                                                AF.Square, scale=SCALE),
                                            waits=[("PE", tm),
                                                   ("VE", t_sq_free[half])],
                                            inc="AC")
                                        t_psn_free[half] = tsq
                                        tr = 0
                                        for hh in range(8):
                                            tr = P.op("dve", lambda e,
                                                      half=half, hh=hh:
                                                e.reduce_sum(
                                                    nrm8[half][:, hh:hh + 1],
                                                    sq_sb[half][:, hh * 64:
                                                                (hh + 1) * 64],
                                                    axis=AX.X),
                                                waits=[("AC", tsq),
                                                       ("VE",
                                                        t_nrm_free[half])],
                                                inc="VE")
                                        if which == "q":
                                            a_ = accq[:, nt * 16 + ds * 8:
                                                      nt * 16 + ds * 8 + 8]
                                        else:
                                            a_ = acck[:, ds * 8:ds * 8 + 8]
                                        tmx = P.op("dve", lambda e, a=a_,
                                                   half=half: e.tensor_max(
                                                a, a, nrm8[half][:]),
                                            waits=[("VE", tr),
                                                   ("PO", t_const)],
                                            inc="VE")
                                        t_nrm_free[half] = tmx
                                        t_sq_free[half] = tr

                        t_acc = P.cnt["VE"]
                        tred = 0
                        for qt in range(NT):
                            tm = P.op("pe", lambda e, qt=qt: e.transpose(
                                ps_n[0:16, 0:128],
                                accq[:, qt * 16:(qt + 1) * 16], ident_f[:]),
                                waits=[("VE", t_acc), ("VE", tred),
                                       ("AC", max(t_psn_free))], inc="PE")
                            tred = P.op("dve", lambda e, qt=qt: e.reduce_max(
                                qtbl[0:16, qt:qt + 1], ps_n[0:16, 0:128],
                                axis=AX.X),
                                waits=[("PE", tm)], inc="VE")
                        tm = P.op("pe", lambda e: e.transpose(
                            ps_n[0:16, 0:128], acck[:, 0:16], ident_f[:]),
                            waits=[("VE", t_acc), ("VE", tred)], inc="PE")
                        tkm = P.op("dve", lambda e: e.reduce_max(
                            kmax2[0:16, 0:1], ps_n[0:16, 0:128], axis=AX.X),
                            waits=[("PE", tm)], inc="VE")
                        tq = P.op("dve", lambda e: e.tensor_scalar_mul(
                            qtbl[0:16, :], qtbl[0:16, :], kmax2[0:16, 0:1]),
                            waits=[("VE", tkm)], inc="VE")
                        tsq2 = P.op("act", lambda e: e.activation(
                            mtbl[0:16, :], qtbl[0:16, :], AF.Sqrt),
                            waits=[("VE", tq)], inc="AC")
                        tneg = P.op("dve", lambda e: e.tensor_scalar_mul(
                            mtbl[0:16, :], mtbl[0:16, :], -1.0),
                            waits=[("AC", tsq2)], inc="VE")
                        tm = 0
                        for h in range(H):
                            tm = P.op("pe", lambda e, h=h: e.matmul(
                                ps_n[:, h * NT:(h + 1) * NT],
                                sel[:, h * 128:(h + 1) * 128],
                                mtbl[0:16, :], start=True, stop=True),
                                waits=[("VE", tneg), ("PO", t_const),
                                       ("VE", tkm)], inc="PE")
                        t_bias = P.op("act", lambda e: e.activation(
                            bias_all[:], ps_n[:, 0:H * NT], AF.Copy),
                            waits=[("PE", tm)], inc="AC")

                attnT = C_AT.enter_context(nc.sbuf_tensor("attnT", [128, DC * S], BF16, side="right"))

                P.barrier()
                with ExitStack() as C2:
                    ktsl = [sb(C2, f"ktsl{i}", [128, S], BF16)
                            for i in range(2)]
                    qtsl = [sb(C2, f"qtsl{i}", [128, 512], BF16)
                            for i in range(2)]
                    ptA = sb(C2, "ptA", [128, 3 * 1024], BF16)
                    rp_sb = sb(C2, "rp_sb", [1, 512], F32)
                    nb_sb = sb(C2, "nb_sb", [64, 512], F32)
                    avst = [sb(C2, f"avst{i}", [128, 512], F32)
                            for i in range(2)]
                    rp2 = [sb(C2, f"rp2_{i}", [1, 512], F32)
                           for i in range(2)]
                    with ExitStack() as PS2:
                        SAB = [ps(PS2, f"SAB{i}", [128, 1024], F32)
                               for i in range(3)]
                        AVH = ps(PS2, "AVH", [128, 512], F32)
                        BC = ps(PS2, "BC", [128, 512], F32)

                        t_ktsl_free = 0
                        t_qt_load = [0, 0]
                        t_qtsl_free = [0, 0]
                        t_S_free = [0, 0, 0]
                        t_pt_free = [0, 0, 0]
                        t_AV_read = (0, 0)
                        t_avst_free = [0, 0]
                        t_nb_free = 0
                        t_rp_free = [0, 0]
                        t_BC_free = 0
                        norm_pend = []
                        uu = 0

                        def emit_norm(item):
                            nonlocal t_nb_free, t_BC_free
                            slot, h, nqt, tevp, trp = item
                            tbc = P.op("pe", lambda e, slot=slot:
                                e.matmul(
                                    BC[0:64, :], ones64[:],
                                    rp2[slot][:], start=True, stop=True),
                                waits=[("VE", trp), ("VE", t_BC_free)],
                                inc="PE")
                            t_rp_free[slot] = tbc
                            tnb = P.op("dve", lambda e:
                                e.tensor_copy(nb_sb[:], BC[0:64, :]),
                                waits=[("PE", tbc), ("VE", t_nb_free)],
                                inc="VE")
                            t_BC_free = tnb
                            pb = (h % 2) * 64
                            dst = attnT[pb:pb + 64,
                                        (h // 2) * S + nqt * 512:
                                        (h // 2) * S + (nqt + 1) * 512]
                            tev = P.op("dve", lambda e, slot=slot,
                                       dst=dst: e.tensor_mul(
                                    dst, avst[slot][0:64, :], nb_sb[:]),
                                waits=[("VE", tnb), ("AC", tevp)], inc="VE")
                            t_avst_free[slot] = tev
                            t_nb_free = tev

                        t_qk_f0 = P.cnt["Sqk0"]
                        t_qk_f1 = P.cnt["Sqk1"]
                        t_ktsl_free2 = [0, 0]
                        for h in range(H):
                            ks = h % 2
                            pr = h // 2
                            hi = h % 2
                            t_kt_load = 0
                            for dup in range(2):
                                t_kt_load = load(
                                    ktsl[ks][dup * 64:dup * 64 + 64, :],
                                    kt_h[h * 64:(h + 1) * 64, :],
                                    sem=f"Lk{ks}" if ks else "Lk",
                                    waits=[("Sqk0", t_qk_f0),
                                           ("Sqk1", t_qk_f1),
                                           ("PE", t_ktsl_free2[ks])])
                            for qt in range(NT):
                                qs = qt % 2
                                t_qt_load[qs] = 0
                                for dup in range(2):
                                    t_qt_load[qs] = load(
                                        qtsl[qs][dup * 64:dup * 64 + 64, :],
                                        qt_h[h * 64:(h + 1) * 64,
                                             qt * 512:(qt + 1) * 512],
                                        sem=f"Lq{qs}",
                                        waits=[("Sqk0", t_qk_f0),
                                               ("Sqk1", t_qk_f1),
                                               ("PE", t_qtsl_free[qs])])
                                exp_tok = {}
                                tAV = 0

                                def emit_av(g):
                                    nonlocal tAV
                                    gs = g % 3
                                    eT = exp_tok[g]
                                    for jj in range(2):
                                        first = (g == 0 and jj == 0)
                                        last = (g == JGRP - 1 and jj == 1)
                                        jc = g * 2 + jj
                                        vh = v_sb[:,
                                            jc * (H * 65) + h * 65:
                                            jc * (H * 65) + h * 65 + 65]
                                        tAV = P.op("pe", lambda e, vh=vh,
                                            gs=gs, jj=jj, first=first,
                                            last=last: e.matmul(
                                                AVH[0:65, :], vh,
                                                ptA[:, gs * 1024 +
                                                    jj * 512:
                                                    gs * 1024 +
                                                    (jj + 1) * 512],
                                                start=first, stop=last),
                                            waits=[("AC", eT),
                                                   ("VE", t_vones),
                                                   ("AC", t_AV_read[0]),
                                                   ("VE", t_AV_read[1])],
                                            inc="PE")
                                        t_pt_free[gs] = tAV

                                for g in range(JGRP):
                                    gs = g % 3
                                    ps_ = g % 3
                                    tS = 0
                                    for jj in range(2):
                                        jc = g * 2 + jj
                                        pbase = (jc % 2) * 64
                                        tS = P.op("pe", lambda e, jc=jc,
                                            jj=jj, qs=qs, gs=gs, ks=ks,
                                            pbase=pbase: e.matmul(
                                                SAB[gs][:, jj * 512:
                                                        (jj + 1) * 512],
                                                ktsl[ks][pbase:pbase + 64,
                                                     jc * 128:
                                                     (jc + 1) * 128],
                                                qtsl[qs][pbase:
                                                         pbase + 64, :],
                                                start=True, stop=True,
                                                tile_position=(pbase, 0)),
                                            waits=[(("Lk1" if ks else "Lk"),
                                                    t_kt_load),
                                                   (f"Lq{qs}",
                                                    t_qt_load[qs]),
                                                   ("AC", t_S_free[gs])],
                                            inc="PE")
                                    texp = P.op("act", lambda e, gs=gs,
                                        ps_=ps_, h=h, qt=qt: e.activation(
                                            ptA[:, ps_ * 1024:
                                                (ps_ + 1) * 1024],
                                            SAB[gs][:], AF.Exp,
                                            bias=bias_all[:,
                                                h * NT + qt:
                                                h * NT + qt + 1]),
                                        waits=[("PE", tS), ("AC", t_bias),
                                               ("PE", t_pt_free[ps_])],
                                        inc="AC")
                                    t_S_free[gs] = texp
                                    exp_tok[g] = texp
                                    if g > 0:
                                        emit_av(g - 1)
                                emit_av(JGRP - 1)
                                slot = uu % 2
                                tevp = P.op("act", lambda e, slot=slot:
                                    e.activation(
                                        avst[slot][0:64, :],
                                        AVH[0:64, :], AF.Copy),
                                    waits=[("PE", tAV),
                                           ("VE", t_avst_free[slot])],
                                    inc="AC")
                                trp = P.op("dve", lambda e, slot=slot:
                                    e.reciprocal(rp2[slot][:],
                                                 AVH[64:65, :]),
                                    waits=[("PE", tAV),
                                           ("PE", t_rp_free[slot])],
                                    inc="VE")
                                t_AV_read = (tevp, trp)
                                norm_pend.append((slot, h, qt, tevp, trp))
                                if len(norm_pend) > 1:
                                    emit_norm(norm_pend.pop(0))
                                t_qtsl_free[qs] = tAV
                                uu += 1
                            t_ktsl_free2[ks] = tAV
                        for item in norm_pend:
                            emit_norm(item)
                        t_attnT_ve = P.cnt["VE"]

            P.barrier()
            with ExitStack() as C3:
                wo_bf = sb(C3, "wo_bf", [128, DC * 1024], BF16)
                b1_sb = sb(C3, "b1_sb", [E, FF], F32)
                b2_sb = sb(C3, "b2_sb", [E, D], F32)
                bo_b = sb(C3, "bo_b", [128, D], F32)
                b1pair = sb(C3, "b1pair", [2, FF], F32)
                wstg3 = sb(C3, "wstg3", [128, 2048], F32)
                xt3 = [sb(C3, f"xt3_{i}", [128, 1024], F32) for i in range(2)]
                y3_sb = sb(C3, "y3_sb", [128, 1024], F32)
                x1_sb = sb(C3, "x1_sb", [128, 1024], F32)
                x1b_sb = sb(C3, "x1b_sb", [128, 1024], BF16)
                x1ts = sb(C3, "x1ts", [128, DC * 512], BF16)
                st3 = sb(C3, "st3", [128, 16], F32)
                rs3 = sb(C3, "rs3", [128, 2], F32)
                rsum8 = sb(C3, "rsum8", [128, 8], F32)
                with ExitStack() as PS3:
                    PO3 = ps(PS3, "PO3", [128, 1024], F32)
                    psT3 = ps(PS3, "psT3", [128, 1024], BF16)
                    RT = ps(PS3, "RT", [128, 512], F32)

                    t_b12 = load(b1_sb[:], b1_h, sem="Lb")
                    t_b12 = load(b2_sb[:], b2_h, sem="Lb")
                    t_bc_ac = 0
                    for hf in range(2):
                        tmb = P.op("pe", lambda e, hf=hf: e.matmul(
                                RT[:], ones128[:],
                                bo_row[:, hf * 512:(hf + 1) * 512],
                                start=True, stop=True),
                            waits=[("Lsm", t_small), ("PO", t_const),
                                   ("AC", t_bc_ac)], inc="PE")
                        t_bc_ac = P.op("act", lambda e, hf=hf: e.activation(
                                bo_b[:, hf * 512:(hf + 1) * 512], RT[:],
                                AF.Copy),
                            waits=[("PE", tmb)], inc="AC")
                    t_b1T = t_bc_ac
                    for g8 in (range(FC // 8) if not SPARSE else []):
                        tmb = 0
                        for j in range(8):
                            fc = g8 * 8 + j
                            tmb = P.op("pe", lambda e, fc=fc, j=j:
                                e.transpose(
                                    RT[:, j * 8:(j + 1) * 8],
                                    b1_sb[:, fc * 128:(fc + 1) * 128],
                                    ident_f[0:8, 0:8]),
                                waits=[("Lb", t_b12), ("PO", t_const),
                                       ("AC", t_b1T)], inc="PE")
                        t_b1T = P.op("act", lambda e, g8=g8: e.activation(
                                b1T[:, g8 * 64:(g8 + 1) * 64],
                                RT[:, 0:64], AF.Copy),
                            waits=[("PE", tmb)], inc="AC")

                    t_wo = 0
                    t_wst3 = 0
                    for half in range(4):
                        tl = load(wstg3[:].rearrange("p (c d) -> p c d",
                                                     d=1024),
                                  wo_h.rearrange("(c p) d -> p c d", p=128)
                                  [:, half * 2:(half + 1) * 2, :],
                                  sem="LW",
                                  waits=[("VE", t_wst3)])
                        t_wo = P.op("dve", lambda e, half=half: e.tensor_copy(
                                wo_bf[:, half * 2048:(half + 1) * 2048],
                                wstg3[:]),
                            waits=[("LW", tl)], inc="VE")
                        t_wst3 = t_wo

                    t_xt3_free = [0, 0]
                    t_PO3_free = [0, 0]
                    t_psT3_free = 0
                    t_x1ts_free = 0
                    t_y3_free = 0
                    t_x1_free = 0
                    t_x1b_free = 0
                    t_st3_free = 0
                    t_rsum_free = 0
                    t_x1f_store = 0
                    t_x1t_store = 0
                    for nc_ in range(NC):
                        xs = nc_ % 2
                        tlx = load(xt3[xs][:],
                                   x_h[nc_ * 128:(nc_ + 1) * 128, :],
                                   sem=f"Lx{xs}",
                                   waits=[("VE", t_xt3_free[xs])])
                        tds = [0, 0]
                        for ds in range(2):
                            tm = 0
                            for dc in range(DC):
                                tm = P.op("pe", lambda e, dc=dc, ds=ds,
                                          nc_=nc_: e.matmul(
                                        PO3[:, ds * 512:(ds + 1) * 512],
                                        attnT[:, dc * S + nc_ * 128:
                                              dc * S + (nc_ + 1) * 128],
                                        wo_bf[:, dc * 1024 + ds * 512:
                                              dc * 1024 + (ds + 1) * 512],
                                        start=(dc == 0), stop=(dc == DC - 1)),
                                    waits=[("VE", t_attnT_ve), ("VE", t_wo),
                                           ("VE", t_PO3_free[ds])], inc="PE")
                            tds[ds] = P.op("dve", lambda e, ds=ds, xs=xs:
                                e.tensor_add(
                                    y3_sb[:, ds * 512:(ds + 1) * 512],
                                    PO3[:, ds * 512:(ds + 1) * 512],
                                    xt3[xs][:, ds * 512:(ds + 1) * 512]),
                                waits=[("PE", tm), (f"Lx{xs}", tlx),
                                       ("VE", t_y3_free)], inc="VE")
                            t_PO3_free[ds] = tds[ds]
                        t_xt3_free[xs] = tds[1]
                        tyb = P.op("dve", lambda e: e.tensor_add(
                                y3_sb[:], y3_sb[:], bo_b[:]),
                            waits=[("VE", tds[1]), ("AC", t_bc_ac)], inc="VE")
                        tst = P.op("dve", lambda e: e.bn_stats(
                                st3[:, 0:6], y3_sb[:, 0:512]),
                            waits=[("VE", tyb), ("VE", t_st3_free)], inc="VE")
                        tst = P.op("dve", lambda e: e.bn_stats(
                                st3[:, 6:12], y3_sb[:, 512:1024]),
                            waits=[("VE", tst)], inc="VE")
                        tst = P.op("dve", lambda e: e.bn_aggr(
                                st3[:, 12:14], st3[:, 0:12]),
                            waits=[("VE", tst)], inc="VE")
                        tsr = P.op("act", lambda e: e.activation(
                                st3[:, 14:15], st3[:, 13:14], AF.Sqrt,
                                bias=epsv[:]),
                            waits=[("VE", tst)], inc="AC")
                        trc = P.op("dve", lambda e: e.reciprocal(
                                rs3[:, 0:1], st3[:, 14:15]),
                            waits=[("AC", tsr)], inc="VE")
                        t_st3_free = trc
                        tx1 = P.op("dve", lambda e: e.tensor_scalar(
                                x1_sb[:], y3_sb[:], st3[:, 12:13],
                                rs3[:, 0:1], ALU.subtract, ALU.mult),
                            waits=[("VE", trc), ("Sx1f", t_x1_free)],
                            inc="VE")
                        t_y3_free = tx1
                        t_x1f_store = store(
                            x1f_h[nc_ * 128:(nc_ + 1) * 128, :], x1_sb[:],
                            sem="Sx1f", waits=[("VE", tx1)])
                        t_x1_free = t_x1f_store
                        txb = P.op("dve", lambda e: e.tensor_copy(
                                x1b_sb[:], x1_sb[:]),
                            waits=[("VE", tx1), ("PE", t_x1b_free)], inc="VE")
                        tm = 0
                        for dc in range(DC):
                            tm = P.op("pe", lambda e, dc=dc: e.transpose(
                                    psT3[:, dc * 128:(dc + 1) * 128],
                                    x1b_sb[:, dc * 128:(dc + 1) * 128],
                                    ident_b[:]),
                                waits=[("VE", txb), ("AC", t_psT3_free),
                                       ("PO", t_const)], inc="PE")
                        t_x1b_free = tm
                        tev = P.op("act", lambda e, q4=nc_ % 4: e.activation(
                                x1ts[:].rearrange("p (dc n) -> p dc n", n=512)
                                [:, :, q4 * 128:(q4 + 1) * 128],
                                psT3[:], AF.Copy),
                            waits=[("PE", tm), ("Sx1t", t_x1ts_free)],
                            inc="AC")
                        t_psT3_free = tev
                        if nc_ % 4 == 3:
                            tile = nc_ // 4
                            tss = 0
                            for dc in range(DC):
                                tss = store(
                                    x1t_h[dc][:, tile * 512:(tile + 1) * 512],
                                    x1ts[:, dc * 512:(dc + 1) * 512],
                                    sem="Sx1t", waits=[("AC", tev)])
                            t_x1t_store = tss
                            t_x1ts_free = tss
                            trr = 0
                            for dc in range(DC):
                                trr = P.op("dve", lambda e, dc=dc:
                                    e.reduce_sum(
                                        rsum8[:, dc:dc + 1],
                                        x1ts[:, dc * 512:(dc + 1) * 512],
                                        axis=AX.X),
                                    waits=[("AC", tev), ("VE", t_rsum_free)],
                                    inc="VE")
                            t_rsum_free = P.op("dve", lambda e: e.tensor_add(
                                    xsacc[:], xsacc[:], rsum8[:]),
                                waits=[("VE", trr), ("PO", t_const)],
                                inc="VE")

                    t_xs = t_rsum_free
                    tm = 0
                    for dc in range(DC):
                        tm = P.op("pe", lambda e, dc=dc: e.matmul(
                                RT[0:1, 0:8], xsacc[:, dc:dc + 1],
                                wr_sb[:, dc * 8:(dc + 1) * 8],
                                start=(dc == 0), stop=(dc == DC - 1)),
                            waits=[("VE", t_xs), ("Lsm", t_small),
                                   ("AC", t_bias)], inc="PE")
                    tz0 = P.op("act", lambda e: e.activation(
                            r_z[:], RT[0:1, 0:8], AF.Copy, scale=1.0 / S),
                        waits=[("PE", tm)], inc="AC")
                    tz = P.op("dve", lambda e: e.tensor_add(
                            r_z[:], r_z[:], br_row[:]),
                        waits=[("AC", tz0), ("Lsm", t_small)], inc="VE")
                    tmx = P.op("dve", lambda e: e.reduce_max(
                            r_m1[:], r_z[:], axis=AX.X),
                        waits=[("VE", tz)], inc="VE")
                    tmn = P.op("dve", lambda e: e.tensor_scalar_mul(
                            r_m2n[:], r_m1[:], -1.0),
                        waits=[("VE", tmx)], inc="VE")
                    tex = P.op("act", lambda e: e.activation(
                            r_rw[:], r_z[:], AF.Exp, bias=r_m2n[:]),
                        waits=[("VE", tmn)], inc="AC")
                    tsm = P.op("dve", lambda e: e.reduce_sum(
                            r_s[:], r_rw[:], axis=AX.X),
                        waits=[("AC", tex)], inc="VE")
                    tsm = P.op("dve", lambda e: e.reciprocal(
                            r_s[:], r_s[:]),
                        waits=[("VE", tsm)], inc="VE")
                    trw = P.op("dve", lambda e: e.tensor_scalar_mul(
                            r_rw[:], r_rw[:], r_s[:]),
                        waits=[("VE", tsm)], inc="VE")
                    tm1 = P.op("dve", lambda e: e.reduce_max(
                            r_m1[:], r_rw[:], axis=AX.X),
                        waits=[("VE", trw)], inc="VE")
                    tk1 = P.op("dve", lambda e: e.tensor_single_scalar(
                            r_mask1[:], r_rw[:], r_m1[:], ALU.is_equal),
                        waits=[("VE", tm1)], inc="VE")
                    tr2 = P.op("dve", lambda e: e.scalar_tensor_tensor(
                            r_rw2[:], r_mask1[:], -1e30, r_rw[:],
                            ALU.mult, ALU.add),
                        waits=[("VE", tk1)], inc="VE")
                    tm2 = P.op("dve", lambda e: e.reduce_max(
                            r_m2[:], r_rw2[:], axis=AX.X),
                        waits=[("VE", tr2)], inc="VE")
                    tk2 = P.op("dve", lambda e: e.tensor_single_scalar(
                            r_mask2[:], r_rw2[:], r_m2[:], ALU.is_equal),
                        waits=[("VE", tm2)], inc="VE")
                    tmn2 = P.op("dve", lambda e: e.tensor_scalar_mul(
                            r_m2n[:], r_m2[:], -1.0),
                        waits=[("VE", tm2)], inc="VE")
                    tw1 = P.op("act", lambda e: e.activation(
                            r_w1[:], r_m1[:], AF.Sigmoid, bias=r_m2n[:]),
                        waits=[("VE", tmn2), ("VE", tk1)], inc="AC")
                    td12 = P.op("dve", lambda e: e.tensor_sub(
                            r_d12[:], r_mask1[:], r_mask2[:]),
                        waits=[("VE", tk2)], inc="VE")
                    tcw = P.op("dve", lambda e: e.scalar_tensor_tensor(
                            cw_row[:], r_d12[:], r_w1[:], r_mask2[:],
                            ALU.mult, ALU.add),
                        waits=[("VE", td12), ("AC", tw1)], inc="VE")
                    tm = P.op("pe", lambda e: e.matmul(
                            RT[:, 0:8], ones128[:], cw_row[:],
                            start=True, stop=True),
                        waits=[("VE", tcw), ("AC", tz0)], inc="PE")
                    t_cwb = P.op("act", lambda e: e.activation(
                            cwb[:], RT[:, 0:8], AF.Copy),
                        waits=[("PE", tm)], inc="AC")
                    tm = P.op("pe", lambda e: e.matmul(
                            RT[0:8, 8:9], cw_row[:], ones128[0:1, 0:1],
                            start=True, stop=True),
                        waits=[("VE", tcw), ("AC", t_cwb)], inc="PE")
                    tct = P.op("act", lambda e: e.activation(
                            cwT[:], RT[0:8, 8:9], AF.Copy),
                        waits=[("PE", tm)], inc="AC")
                    tprev = tct
                    for hf in range(2):
                        tmb = P.op("pe", lambda e, hf=hf: e.matmul(
                                RT[0:1, 0:512], cwT[:],
                                b2_sb[:, hf * 512:(hf + 1) * 512],
                                start=True, stop=True),
                            waits=[("AC", tprev), ("Lb", t_b12)], inc="PE")
                        tprev = P.op("act", lambda e, hf=hf: e.activation(
                                b2c_row[:, hf * 512:(hf + 1) * 512],
                                RT[0:1, 0:512], AF.Copy),
                            waits=[("PE", tmb)], inc="AC")
                    for hf in range(2):
                        tmb = P.op("pe", lambda e, hf=hf: e.matmul(
                                RT[:, 0:512], ones128[:],
                                b2c_row[:, hf * 512:(hf + 1) * 512],
                                start=True, stop=True),
                            waits=[("AC", tprev)], inc="PE")
                        tprev = P.op("act", lambda e, hf=hf: e.activation(
                                b2cb[:, hf * 512:(hf + 1) * 512],
                                RT[:, 0:512], AF.Copy),
                            waits=[("PE", tmb)], inc="AC")
                    tm = tprev
                    t_b2cb = tm
                    if SPARSE:
                        tt1 = P.op("dve", lambda e: e.tensor_mul(
                                r_tmp8[:], io8[:], r_mask1[:]),
                            waits=[("PO", t_const), ("VE", tk1)], inc="VE")
                        tt1 = P.op("dve", lambda e: e.reduce_sum(
                                r_ef[:, 0:1], r_tmp8[:], axis=AX.X),
                            waits=[("VE", tt1)], inc="VE")
                        tt2 = P.op("dve", lambda e: e.tensor_mul(
                                r_tmp8[:], io8[:], r_mask2[:]),
                            waits=[("VE", tt1), ("VE", tk2)], inc="VE")
                        tt2 = P.op("dve", lambda e: e.reduce_sum(
                                r_ef[:, 1:2], r_tmp8[:], axis=AX.X),
                            waits=[("VE", tt2)], inc="VE")
                        t_idx = P.op("dve", lambda e: e.tensor_copy(
                                idx_sb[:], r_ef[:]),
                            waits=[("VE", tt2)], inc="VE")
                        tc1 = P.op("dve", lambda e: e.tensor_copy(
                                r_cws[:, 0:1], r_w1[:]),
                            waits=[("AC", tw1)], inc="VE")
                        tc1 = P.op("dve", lambda e: e.tensor_scalar(
                                r_cws[:, 1:2], r_w1[:], -1.0, 1.0,
                                ALU.mult, ALU.add),
                            waits=[("VE", tc1)], inc="VE")
                        tm = P.op("pe", lambda e: e.matmul(
                                RT[:, 0:2], ones128[:], r_cws[:],
                                start=True, stop=True),
                            waits=[("VE", tc1), ("AC", t_b2cb)], inc="PE")
                        t_cwsel = P.op("act", lambda e: e.activation(
                                cwsel[:], RT[:, 0:2], AF.Copy),
                            waits=[("PE", tm)], inc="AC")
                        t_b1g = 0
                        with nc.gpsimd.register("rg0") as rg0:
                            for k in range(2):
                                def gfn(e, k=k, rg0=rg0):
                                    e.reg_load(rg0, idx_sb[0:1, k:k + 1])
                                    e.reg_mul(rg0, rg0, FF)
                                    return e.dma_start(
                                        out=b1pair[k:k + 1, :],
                                        in_=bass.AP(b1_h.tensor, rg0,
                                                    [[FF, 1], [1, FF]]))
                                t_b1g = P.op("pool", gfn,
                                    waits=[("VE", t_idx), ("Lb", t_b12)],
                                    inc="GB", incn=16)
                            pass
                        t_dsel = {}
                        with nc.gpsimd.register("rg1") as rg1:
                            for gi, (mat, k) in enumerate(
                                    ((0, 0), (0, 1), (1, 0), (1, 1))):
                                srcten = (w1_h if mat == 0 else w2_h).tensor
                                dstten = (wf1_h if mat == 0 else wf2_h).tensor
                                tt = 0
                                for q in range(4):
                                    def cfn(e, k=k, q=q, srcten=srcten,
                                            dstten=dstten, rg1=rg1):
                                        if q == 0:
                                            e.reg_load(rg1,
                                                       idx_sb[0:1, k:k + 1])
                                            e.reg_mul(rg1, rg1, D * FF)
                                        else:
                                            e.reg_add(rg1, rg1, 1048576)
                                        return e.dma_start(
                                            out=bass.AP(
                                                dstten,
                                                k * D * FF + q * 1048576,
                                                [[1024, 1024], [1, 1024]]),
                                            in_=bass.AP(
                                                srcten, rg1,
                                                [[1024, 1024], [1, 1024]]))
                                    tt = P.op("pool", cfn,
                                        waits=[("VE", t_idx)],
                                        inc=f"DS{gi}", incn=16)
                                t_dsel[(mat, k)] = tt
                        sc_jobs = []
                        for k in range(2):
                            for j in range(64):
                                sc_jobs.append(
                                    (("DS0" if k == 0 else "DS1"),
                                     t_dsel[(0, k)], ("w1", k),
                                     bass.AP(wf1_h.tensor,
                                             k * D * FF + j * 65536,
                                             [[512, 128], [1, 512]]),
                                     bass.AP(w1sel_h.tensor,
                                             k * D * FF + j * 65536,
                                             [[512, 128], [1, 512]])))
                            for j in range(64):
                                sc_jobs.append(
                                    (("DS2" if k == 0 else "DS3"),
                                     t_dsel[(1, k)], ("w2", k),
                                     bass.AP(wf2_h.tensor,
                                             k * FF * D + j * 65536,
                                             [[512, 128], [1, 512]]),
                                     bass.AP(w2sel_h.tensor,
                                             k * FF * D + j * 65536,
                                             [[512, 128], [1, 512]])))
                        vfree = [0, 0, 0, 0]
                        sfree = [0, 0, 0, 0]
                        sc_done = {}
                        lsem = ["CL0", "CL1", "CL2", "CL3"]
                        ssem = ["CS0", "CS1", "CS2", "CS3"]
                        sc_pend = []

                        def emit_store(jj):
                            i, cdst2, tagg, tc2 = sc_pend.pop(0)
                            ts_ = P.op("pool", lambda e, d=cdst2, i=i:
                                e.dma_start(out=d, in_=scb[i][:]),
                                waits=[("VE", tc2)],
                                inc=ssem[i], incn=16)
                            sfree[i] = ts_
                            if (jj + 1) % 64 == 0:
                                sc_done[tagg] = tuple(
                                    (ssem[z], P.cnt[ssem[z]])
                                    for z in range(4))

                        for j, (dsem, dtok, tag, csrc, cdst) in enumerate(
                                sc_jobs):
                            i = j % 4
                            tl = P.op("pool", lambda e, s=csrc, i=i:
                                e.dma_start(out=scf[i][:], in_=s),
                                waits=[(dsem, dtok), ("VE", vfree[i])],
                                inc=lsem[i], incn=16)
                            tc = P.op("dve", lambda e, i=i:
                                e.tensor_copy(scb[i][:], scf[i][:]),
                                waits=[(lsem[i], tl),
                                       (ssem[i], sfree[i])], inc="VE")
                            vfree[i] = tc
                            sc_pend.append((i, cdst, tag, tc))
                            if len(sc_pend) > 3:
                                emit_store(j - 3)
                        jj = len(sc_jobs) - 3
                        while sc_pend:
                            emit_store(jj)
                            jj += 1

                        t_b1Ts = t_cwsel
                        for g16 in range(2):
                            tmb = 0
                            for j in range(16):
                                fc = g16 * 16 + j
                                tmb = P.op("pe", lambda e, fc=fc, j=j:
                                    e.transpose(
                                        RT[:, j * 2:(j + 1) * 2],
                                        b1pair[:, fc * 128:(fc + 1) * 128],
                                        ident_f[0:2, 0:2]),
                                    waits=[("GB", t_b1g), ("PO", t_const),
                                           ("AC", t_b1Ts)], inc="PE")
                            t_b1Ts = P.op("act", lambda e, g16=g16:
                                e.activation(
                                    b1Tsel[:, g16 * 32:(g16 + 1) * 32],
                                    RT[:, 0:32], AF.Copy),
                                waits=[("PE", tmb)], inc="AC")

        P.barrier()
        with ExitStack() as C4:
            w1r = sb(C4, "w1r", [128, 17 * 1024], BF16)
            w2r = sb(C4, "w2r", [128, 8 * 1024], BF16)
            x1tt = [sb(C4, f"x1tt{i}", [128, DC * 512], BF16)
                    for i in range(2)]
            h_sb = sb(C4, "h_sb", [128, FC * 512], BF16)
            oacc = sb(C4, "oacc", [128, 4 * 1024], F32)
            x1ft = [sb(C4, f"x1ft{i}", [128, 4 * 1024], F32)
                    for i in range(2)]
            gam_b = sb(C4, "gam_b", [128, D], F32)
            bet_b = sb(C4, "bet_b", [128, D], F32)
            ep_t = sb(C4, "ep_t", [128, 1024], F32)
            ep_o = [sb(C4, f"ep_o{i}", [128, 1024], F32) for i in range(2)]
            st4 = sb(C4, "st4", [128, 16], F32)
            rs4 = sb(C4, "rs4", [128, 2], F32)
            with ExitStack() as PS4:
                PO4 = [ps(PS4, f"PO4_{i}", [128, 1024], F32) for i in range(4)]

                t_gb = 0
                for row, dstb in ((gam_row, gam_b), (bet_row, bet_b)):
                    for hf in range(2):
                        tmb = P.op("pe", lambda e, r=row, hf=hf: e.matmul(
                                PO4[3][:, 0:512], ones128[:],
                                r[:, hf * 512:(hf + 1) * 512],
                                start=True, stop=True),
                            waits=[("Lsm", t_small), ("PO", t_const),
                                   ("AC", t_gb)], inc="PE")
                        t_gb = P.op("act", lambda e, d=dstb, hf=hf:
                            e.activation(
                                d[:, hf * 512:(hf + 1) * 512],
                                PO4[3][:, 0:512], AF.Copy),
                            waits=[("PE", tmb)], inc="AC")

                w1_ready = [0] * 17
                w1_free = [0] * 17
                w2_ready = [0] * 8
                w2_free = [0] * 8
                w1_ctr = 0
                w2_ctr = 0
                t_x1tt = [0, 0]
                t_x1tt_free = [0, 0]
                t_x1ft = [0, 0]
                t_x1ft_free = [0, 0]
                t_h_free = 0
                t_PO4h_free = [0, 0]
                t_po40_evac = [0, 0]
                t_oacc_evac = 0
                t_oacc_read = 0
                t_out_free = [0, 0]
                t_st4_free = 0
                gelu_tok = {}

                for tile in range(NT):
                    ts_ = tile % 2
                    tld = 0
                    for dc in range(DC):
                        tld = load(x1tt[ts_][:, dc * 512:(dc + 1) * 512],
                                   x1t_h[dc][:, tile * 512:(tile + 1) * 512],
                                   sem=f"Lxt{ts_}",
                                   waits=[("Sx1t", t_x1t_store),
                                          ("PE", t_x1tt_free[ts_])])
                    t_x1tt[ts_] = tld
                    t_x1ft[ts_] = load(
                        x1ft[ts_][:].rearrange("p (s d) -> p s d", d=D),
                        x1f_h.rearrange("(t s p) d -> t p s d",
                                        p=128, s=4)[tile],
                        sem=f"Lxf{ts_}",
                        waits=[("Sx1f", t_x1f_store),
                               ("VE", t_x1ft_free[ts_])])
                    tinit = 0
                    for ns in range(4):
                        tinit = P.op("dve", lambda e, ns=ns, ts_=ts_:
                            e.tensor_add(
                                oacc[:, ns * 1024:(ns + 1) * 1024],
                                x1ft[ts_][:, ns * 1024:(ns + 1) * 1024],
                                b2cb[:]),
                            waits=[(f"Lxf{ts_}", t_x1ft[ts_]),
                                   ("AC", t_b2cb),
                                   ("VE", t_oacc_evac),
                                   ("VE", t_oacc_read)], inc="VE")
                    t_oacc_init = tinit

                    for e_ in (range(2) if SPARSE else range(E)):
                        for fg in range(FG):
                            gsem = ["Lw1a", "Lw1b", "Lw1c", "Lw1d"][(e_ * FG + fg) % 4]
                            grp_tok = 0
                            for dc in range(DC):
                                slot = w1_ctr % 17
                                w1src = (w1sel_h[e_, dc * 128:(dc + 1) * 128,
                                                 fg * 1024:(fg + 1) * 1024]
                                         if SPARSE
                                         else w1s_h[e_, fg, dc])
                                wws = (list(sc_done[("w1", e_)])
                                       if SPARSE else
                                       [("GCs0", cast_done[e_][0]),
                                        ("GCs1", cast_done[e_][1])])
                                grp_tok = load(
                                    w1r[:, slot * 1024:(slot + 1) * 1024],
                                    w1src,
                                    sem=gsem,
                                    waits=wws + [("PE", w1_free[slot])])
                                w1_ctr += 1
                            base_slot = (w1_ctr - DC) % 17
                            for fcl in range(8):
                                fc = fg * 8 + fcl
                                hp = fc % 2
                                tm = 0
                                for dc in range(DC):
                                    slot = (base_slot + dc) % 17
                                    tm = P.op("pe", lambda e, slot=slot,
                                              fcl=fcl, dc=dc, hp=hp,
                                              ts_=ts_: e.matmul(
                                            PO4[0][:, hp * 512:
                                                   (hp + 1) * 512],
                                            w1r[:, slot * 1024 + fcl * 128:
                                                slot * 1024 +
                                                (fcl + 1) * 128],
                                            x1tt[ts_][:, dc * 512:
                                                      (dc + 1) * 512],
                                            start=(dc == 0),
                                            stop=(dc == DC - 1)),
                                        waits=[(gsem, grp_tok),
                                               (f"Lxt{ts_}", t_x1tt[ts_]),
                                               ("AC", t_PO4h_free[hp]),
                                               ("VE", t_po40_evac[hp]),
                                               ("PE", t_h_free)], inc="PE")
                                    if fcl == 7:
                                        w1_free[slot] = tm
                                tg = P.op("act", lambda e, fc=fc, hp=hp,
                                          e_=e_: e.activation(
                                        h_sb[:, fc * 512:(fc + 1) * 512],
                                        PO4[0][:, hp * 512:(hp + 1) * 512],
                                        AF.Gelu,
                                        bias=(b1Tsel[:, fc * 2 + e_:
                                                     fc * 2 + e_ + 1]
                                              if SPARSE else
                                              b1T[:, fc * E + e_:
                                                  fc * E + e_ + 1])),
                                    waits=[("PE", tm),
                                           ("AC", t_b1Ts if SPARSE
                                            else t_b1T)],
                                    inc="AC")
                                t_PO4h_free[hp] = tg
                                gelu_tok[fc] = tg
                        for fc in range(FC):
                            if fc % 4 == 0:
                                g2sem = ("Lw2a" if (w2_ctr // 4) % 2 == 0
                                         else "Lw2b")
                                g2tok = 0
                                for k in range(4):
                                    slot = w2_ctr % 8
                                    w2src = (w2sel_h[e_,
                                                     (fc + k) * 128:
                                                     (fc + k + 1) * 128, :]
                                             if SPARSE
                                             else w2s_h[e_, fc + k])
                                    wws = (list(sc_done[("w2", e_)])
                                           if SPARSE else
                                           [("GCs0", cast_done[e_][0]),
                                            ("GCs1", cast_done[e_][1])])
                                    g2tok = load(
                                        w2r[:, slot * 1024:(slot + 1) * 1024],
                                        w2src,
                                        sem=g2sem,
                                        waits=wws + [("PE", w2_free[slot])])
                                    w2_ctr += 1
                            slot = fc % 8
                            tm = 0
                            for ns in range(4):
                                for ds in range(2):
                                    tm = P.op("pe", lambda e, fc=fc, ns=ns,
                                              ds=ds, slot=slot: e.matmul(
                                            PO4[ns][:, ds * 512:
                                                    (ds + 1) * 512],
                                            h_sb[:, fc * 512 + ns * 128:
                                                 fc * 512 + (ns + 1) * 128],
                                            w2r[:, slot * 1024 + ds * 512:
                                                slot * 1024 +
                                                (ds + 1) * 512],
                                            start=(fc == 0),
                                            stop=(fc == FC - 1)),
                                        waits=[(g2sem, g2tok),
                                               ("AC", gelu_tok[fc]),
                                               ("AC", t_gb if fc == 0 else 0),
                                               ("AC", max(t_PO4h_free)
                                                if fc == 0 else 0),
                                               ("VE", t_oacc_evac
                                                if fc == 0 else 0)],
                                        inc="PE")
                            w2_free[slot] = tm
                            tlast_w2 = tm
                        t_h_free = tlast_w2
                        tev = 0
                        for ns in range(4):
                            for ds in range(2):
                                tev = P.op("dve", lambda e, ns=ns, ds=ds,
                                           e_=e_: e.scalar_tensor_tensor(
                                        oacc[:, ns * 1024 + ds * 512:
                                             ns * 1024 + (ds + 1) * 512],
                                        PO4[ns][:, ds * 512:(ds + 1) * 512],
                                        (cwsel[:, e_:e_ + 1] if SPARSE
                                         else cwb[:, e_:e_ + 1]),
                                        oacc[:, ns * 1024 + ds * 512:
                                             ns * 1024 + (ds + 1) * 512],
                                        ALU.mult, ALU.add),
                                    waits=[("PE", tlast_w2),
                                           ("AC", t_cwsel if SPARSE
                                            else t_cwb),
                                           ("VE", t_oacc_init)], inc="VE")
                                if ns == 0:
                                    t_po40_evac[ds] = tev
                        t_oacc_evac = tev

                    for ns in range(4):
                        os_ = ns % 2
                        tst_ = P.op("dve", lambda e, ns=ns: e.bn_stats(
                                st4[:, 0:6], oacc[:, ns * 1024:
                                                  ns * 1024 + 512]),
                            waits=[("VE", t_oacc_evac), ("VE", t_st4_free)],
                            inc="VE")
                        tst_ = P.op("dve", lambda e, ns=ns: e.bn_stats(
                                st4[:, 6:12], oacc[:, ns * 1024 + 512:
                                                   (ns + 1) * 1024]),
                            waits=[("VE", tst_)], inc="VE")
                        tst_ = P.op("dve", lambda e: e.bn_aggr(
                                st4[:, 12:14], st4[:, 0:12]),
                            waits=[("VE", tst_)], inc="VE")
                        tsr = P.op("act", lambda e: e.activation(
                                st4[:, 14:15], st4[:, 13:14], AF.Sqrt,
                                bias=epsv[:]),
                            waits=[("VE", tst_)], inc="AC")
                        trc = P.op("dve", lambda e: e.reciprocal(
                                rs4[:, 0:1], st4[:, 14:15]),
                            waits=[("AC", tsr)], inc="VE")
                        tt = P.op("dve", lambda e, ns=ns: e.tensor_scalar(
                                ep_t[:], oacc[:, ns * 1024:(ns + 1) * 1024],
                                st4[:, 12:13], rs4[:, 0:1],
                                ALU.subtract, ALU.mult),
                            waits=[("VE", trc)], inc="VE")
                        t_oacc_read = tt
                        tt = P.op("dve", lambda e: e.tensor_mul(
                                ep_t[:], ep_t[:], gam_b[:]),
                            waits=[("VE", tt)], inc="VE")
                        tt = P.op("dve", lambda e: e.tensor_add(
                                ep_t[:], ep_t[:], bet_b[:]),
                            waits=[("VE", tt)], inc="VE")
                        tt = P.op("dve", lambda e, ns=ns, ts_=ts_:
                            e.tensor_add(
                                ep_t[:], ep_t[:],
                                x1ft[ts_][:, ns * 1024:(ns + 1) * 1024]),
                            waits=[("VE", tt)], inc="VE")
                        if ns == 3:
                            t_x1ft_free[ts_] = tt
                        tst_ = P.op("dve", lambda e: e.bn_stats(
                                st4[:, 0:6], ep_t[:, 0:512]),
                            waits=[("VE", tt)], inc="VE")
                        tst_ = P.op("dve", lambda e: e.bn_stats(
                                st4[:, 6:12], ep_t[:, 512:1024]),
                            waits=[("VE", tst_)], inc="VE")
                        tst_ = P.op("dve", lambda e: e.bn_aggr(
                                st4[:, 12:14], st4[:, 0:12]),
                            waits=[("VE", tst_)], inc="VE")
                        tsr = P.op("act", lambda e: e.activation(
                                st4[:, 14:15], st4[:, 13:14], AF.Sqrt,
                                bias=epsv[:]),
                            waits=[("VE", tst_)], inc="AC")
                        trc = P.op("dve", lambda e: e.reciprocal(
                                rs4[:, 1:2], st4[:, 14:15]),
                            waits=[("AC", tsr)], inc="VE")
                        t_st4_free = trc
                        tout = P.op("dve", lambda e, os_=os_: e.tensor_scalar(
                                ep_o[os_][:], ep_t[:],
                                st4[:, 12:13], rs4[:, 1:2],
                                ALU.subtract, ALU.mult),
                            waits=[("VE", trc),
                                   (f"ST{os_}", t_out_free[os_])],
                            inc="VE")
                        P.op("pool", lambda e, tile=tile, ns=ns, os_=os_:
                            e.dma_start(
                                out=y_h[tile * 512 + ns * 128:
                                        tile * 512 + (ns + 1) * 128, :],
                                in_=ep_o[os_][:]),
                            waits=[("VE", tout)], inc=f"ST{os_}", incn=16)
                        t_out_free[os_] = P.cnt[f"ST{os_}"]
                    t_x1tt_free[ts_] = tlast_w2

        P.wait("pool", [("ST0", P.cnt["ST0"]), ("ST1", P.cnt["ST1"])])
        P.wait("sync", [("ST0", P.cnt["ST0"]), ("ST1", P.cnt["ST1"])])

        with nc.Block() as block:
            P.run(block)
    return nc


_NC_CACHE = {}


def kernel(**inputs):
    x = np.ascontiguousarray(np.asarray(inputs["L0_attn_x"], dtype=np.float32))
    B, S, _ = x.shape
    if S not in _NC_CACHE:
        _NC_CACHE[S] = build(S=S)
    nc = _NC_CACHE[S]
    w = {
        "wq": np.ascontiguousarray(np.asarray(inputs["L0_attn_Wq"], np.float32)),
        "wk": np.ascontiguousarray(np.asarray(inputs["L0_attn_Wk"], np.float32)),
        "wv": np.ascontiguousarray(np.asarray(inputs["L0_attn_Wv"], np.float32)),
        "wo": np.ascontiguousarray(np.asarray(inputs["L0_attn_Wo"], np.float32)),
        "bo": np.ascontiguousarray(np.asarray(inputs["L0_attn_bo"], np.float32)),
        "w1": np.ascontiguousarray(np.asarray(inputs["L0_ffn_W1"], np.float32)),
        "b1": np.ascontiguousarray(np.asarray(inputs["L0_ffn_b1"], np.float32)),
        "w2": np.ascontiguousarray(np.asarray(inputs["L0_ffn_W2"], np.float32)),
        "b2": np.ascontiguousarray(np.asarray(inputs["L0_ffn_b2"], np.float32)),
        "wr": np.ascontiguousarray(np.asarray(inputs["L0_ffn_Wr"], np.float32)),
        "br": np.ascontiguousarray(np.asarray(inputs["L0_ffn_br"], np.float32)),
        "gamma": np.ascontiguousarray(np.asarray(inputs["L0_ffn_gamma"],
                                                 np.float32)),
        "beta": np.ascontiguousarray(np.asarray(inputs["L0_ffn_beta"],
                                                np.float32)),
    }
    in_maps = [dict(x=np.ascontiguousarray(x[b]), **w) for b in range(B)]
    res = run_bass_kernel_spmd(nc, in_maps, core_ids=list(range(B)))
    return np.stack([np.asarray(r["y"], np.float32) for r in res.results],
                    axis=0)
